# version 06
# baseline (speedup 1.0000x reference)
import contextlib

import numpy as np
import ml_dtypes

import concourse.bass as bass
import concourse.mybir as mybir
import concourse.bass_utils as _bass_utils
from concourse.bass_utils import run_bass_kernel_spmd

if not getattr(_bass_utils, "_ldw_opt_patched", False):
    _orig_run_command = _bass_utils.run_command

    def _run_command_ldw(cmd, **kw):
        if isinstance(cmd, list):
            cmd = ["--enable-ldw-opt=true" if c == "--enable-ldw-opt=false"
                   else c for c in cmd]
        return _orig_run_command(cmd, **kw)

    _bass_utils.run_command = _run_command_ldw
    _bass_utils._ldw_opt_patched = True

F32 = mybir.dt.float32
BF16 = mybir.dt.bfloat16
AF = mybir.ActivationFunctionType
ALU = mybir.AluOpType
AX = mybir.AxisListType

T_TOK = 2048
D = 1024
NT = 16
ND = 8
NM = 513

LAUNCH_PLAN = [[0, 1, 2, 3]]
DEBUG_TAPS = ()
_LAST_RES = None

ENGINES = ("tensor", "vector", "scalar", "gpsimd", "sync")


class Rec:
    __slots__ = ("eng", "fn", "deps", "needed", "idx", "inc_no", "dma")

    def __init__(self, eng, fn, deps, idx, dma=None):
        self.eng, self.fn, self.deps, self.idx, self.dma = eng, fn, deps, idx, dma
        self.needed = False
        self.inc_no = None


class Buf:
    __slots__ = ("w", "rs", "name")

    def __init__(self, name=""):
        self.w = None
        self.rs = []
        self.name = name


def handoff(frm, *tos):
    for t in tos:
        t.w = frm.w
        t.rs = list(frm.rs)


def fork_arena(ctx, key, *newbufs):
    group = ctx.groups.setdefault(key, [])
    haz = []
    for b in group:
        if b.w is not None:
            haz.append(b.w)
        haz.extend(b.rs)
    for nb in newbufs:
        nb.w = None
        nb.rs = list(haz)
    ctx.groups[key] = list(newbufs)


class Prog:
    def __init__(self, nc):
        self.nc = nc
        self.per_eng = {e: [] for e in ENGINES}
        self.dma_sems = {}
        self.final_deps = []
        self.fence = []

    def set_fence(self):
        self.fence = [q[-1] for q in self.per_eng.values() if q]

    def do(self, eng, fn, reads=(), writes=(), extra=()):
        deps = list(self.fence)
        for b in reads:
            if b.w is not None:
                deps.append(b.w)
        for b in writes:
            if b.w is not None:
                deps.append(b.w)
            deps.extend(b.rs)
        deps.extend(extra)
        rec = Rec(eng, fn, deps, len(self.per_eng[eng]))
        self.per_eng[eng].append(rec)
        for b in reads:
            b.rs.append(rec)
        for b in writes:
            b.w = rec
            b.rs = []
        return rec

    def dma(self, queue, sem_key, fn, reads=(), writes=(), extra=()):
        cnt = self.dma_sems.get(sem_key, 0) + 16
        self.dma_sems[sem_key] = cnt
        rec = self.do(queue, fn, reads=reads, writes=writes, extra=extra)
        rec.dma = (sem_key, cnt)
        return rec

    def finish(self, recs):
        self.final_deps.extend(recs)

    def finalize(self, st):
        nc = self.nc
        for eng in ENGINES:
            for rec in self.per_eng[eng]:
                for dep in rec.deps:
                    if dep.dma is None and dep.eng == "tensor" and rec.eng == "tensor":
                        continue
                    if dep.dma is None:
                        dep.needed = True
        for dep in self.final_deps:
            if dep.dma is None:
                dep.needed = True
        eng_total = {}
        for eng in ENGINES:
            n = 0
            for rec in self.per_eng[eng]:
                if rec.dma is None and rec.needed:
                    n += 1
                    rec.inc_no = n
            eng_total[eng] = n
        eng_sem = {e: st.enter_context(nc.semaphore(f"se_{e}"))
                   for e in ENGINES if eng_total[e]}
        dma_sem = {k: st.enter_context(nc.semaphore(f"sd_{k}"))
                   for k in self.dma_sems}

        def tok(dep):
            if dep.dma is not None:
                return ("d", dep.dma[0], dep.dma[1])
            assert dep.inc_no is not None, "dep without inc"
            return ("e", dep.eng, dep.inc_no)

        def emit_for(eng, ns):
            water = {}
            for rec in self.per_eng[eng]:
                for dep in rec.deps:
                    if dep.dma is None and dep.eng == "tensor" and eng == "tensor":
                        continue
                    kind, key, val = tok(dep)
                    wkey = (kind, key)
                    if water.get(wkey, -1) >= val:
                        continue
                    sem = dma_sem[key] if kind == "d" else eng_sem[key]
                    ns.wait_ge(sem, val)
                    water[wkey] = val
                inst = rec.fn(ns)
                if rec.dma is not None:
                    inst.then_inc(dma_sem[rec.dma[0]], 16)
                elif rec.needed:
                    if inst is None:
                        inst = ns.nop()
                    inst.then_inc(eng_sem[eng], 1)
            if eng == "sync":
                water = {}
                for dep in self.final_deps:
                    kind, key, val = tok(dep)
                    wkey = (kind, key)
                    if water.get(wkey, -1) >= val:
                        continue
                    sem = dma_sem[key] if kind == "d" else eng_sem[key]
                    ns.wait_ge(sem, val)
                    water[wkey] = val

        block = st.enter_context(nc.Block())

        @block.tensor
        def _(ns):
            emit_for("tensor", ns)

        @block.vector
        def _(ns):
            emit_for("vector", ns)

        @block.scalar
        def _(ns):
            emit_for("scalar", ns)

        @block.gpsimd
        def _(ns):
            emit_for("gpsimd", ns)

        @block.sync
        def _(ns):
            emit_for("sync", ns)


_CONST_CACHE = {}


def host_constants():
    if _CONST_CACHE:
        return _CONST_CACHE
    c = _CONST_CACHE
    N = 1024
    m = np.arange(NM, dtype=np.float64)
    d = np.arange(N, dtype=np.float64)
    ang = 2.0 * np.pi * np.outer(d, m) / N
    c["c_dftcos"] = np.cos(ang).astype(np.float32)
    c["c_dftsin"] = (-np.sin(ang)).astype(np.float32)
    w = np.where((m == 0) | (m == N // 2), 1.0, 2.0)
    angT = ang.T
    c["c_idftA"] = ((w[:, None] * np.cos(angT)) / N).astype(np.float32)
    c["c_idftB"] = ((-w[:, None] * np.sin(angT)) / N).astype(np.float32)
    c["c_eye_f32"] = np.eye(128, dtype=np.float32)
    c["c_eye_bf"] = np.eye(128, dtype=np.float32)
    DH, WIN, n = 64, 512, T_TOK
    half = np.arange(0, DH, 2, dtype=np.float64)
    inv_freq = 1.0 / 10000.0 ** (half / DH)
    t = np.arange(n, dtype=np.float64)
    freqs = t[:, None] * inv_freq[None, :]
    freqs = np.concatenate([freqs, freqs], axis=-1)
    base = (half + 0.4 * DH) / (1.4 * DH)
    power = (t - n // 2) / float(WIN // 2)
    scale = base[None, :] ** power[:, None]
    scale = np.concatenate([scale, scale], axis=-1)
    cos, sin = np.cos(freqs), np.sin(freqs)
    sgn = np.where(np.arange(DH) < DH // 2, -1.0, 1.0)[None, :]
    c["c_rotA"] = (cos * scale).astype(np.float32)
    c["c_rotB"] = (sin * scale * sgn).astype(np.float32)
    c["c_rotC"] = (cos / scale).astype(np.float32)
    c["c_rotD"] = (sin / scale * sgn).astype(np.float32)
    i = np.arange(128)[:, None]
    j = np.arange(128)[None, :]
    c["c_tri"] = np.where(j <= i, 0.0, -1.0e30).astype(np.float32)
    c["c_triu"] = np.where(j >= i, 0.0, -1.0e30).astype(np.float32)
    c["c_shift"] = np.zeros((128, 128), np.float32)
    for jj in range(127):
        c["c_shift"][jj, jj + 1] = 1.0
    c["c_bnd"] = np.zeros((128, 128), np.float32)
    c["c_bnd"][127, 0] = 1.0
    hs = np.zeros((8, 128, 16), dtype=np.float32)
    for dt in range(8):
        for p in range(128):
            hs[dt, p, (dt * 128 + p) // 64] = 1.0
    c["c_hsum16"] = hs
    c["c_eye16"] = np.eye(16, dtype=np.float32)
    return c


class Ctx:
    pass


def mm_chunks(ns, out_fn, lhsT, rhs_fn, nfree, start, stop, chunk=512):
    inst = None
    off = 0
    while off < nfree:
        w = min(chunk, nfree - off)
        inst = ns.matmul(out_fn(off, w), lhsT, rhs_fn(off, w),
                         start=start, stop=stop and off + w >= nfree)
        off += w
    return inst


def dbg_now(ctx, nm, ap, bufs, shape):
    if nm not in DEBUG_TAPS:
        return
    od = ctx.nc.dram_tensor(f"dbg_{nm}", shape, F32, kind="ExternalOutput").ap()
    rec = ctx.P.dma("gpsimd", f"dbg_{nm}", lambda ns: ns.dma_start(out=od, in_=ap),
                    reads=list(bufs))
    ctx.P.finish([rec])


def wload(ctx, name, src_ap, shape, dtype=BF16, queue="gpsimd", last_dim=8192):
    nc, P = ctx.nc, ctx.P
    t = ctx.st.enter_context(nc.sbuf_tensor(name, shape, dtype))
    b = Buf(name)
    ctx.P.dma(queue, name, lambda ns: ns.dma_start(
        out=t[:], in_=src_ap, max_dma_last_dim=last_dim), writes=[b])
    return t, b


def emit_outer_ln(ctx):
    nc, P, st = ctx.nc, ctx.P, ctx.st
    x, xi, xnT = ctx.x, ctx.xi, ctx.xnT
    bx, bxi, bxnT = ctx.bx, ctx.bxi, ctx.bxnT
    fork_arena(ctx, "xi", bxi)
    stats = st.enter_context(nc.sbuf_tensor(f"lnst{ctx.lid}", [128, 4, 16], F32))
    bstats = Buf("lnst")
    psAl, bAl = ctx.psA, ctx.psA_b
    for t in range(NT):
        P.do("vector", lambda ns, t=t: ns.tensor_reduce(
            out=stats[:, 0, t:t + 1], in_=x[:, t, :], axis=AX.X, op=ALU.add),
            reads=[bx], writes=[bstats])
        bank = (t % 2) * 2
        P.do("scalar", lambda ns, t=t, bank=bank: ns.activation(
            out=psAl[:, bank * 512:bank * 512 + 1024], in_=x[:, t, :],
            func=AF.Square, accum_out=stats[:, 1, t:t + 1]),
            reads=[bx], writes=[bstats, bAl[bank], bAl[bank + 1]])
    P.do("vector", lambda ns: ns.tensor_scalar(
        out=stats[:, 0, :], in0=stats[:, 0, :], scalar1=1.0 / D, scalar2=None,
        op0=ALU.mult), reads=[bstats], writes=[bstats])
    P.do("vector", lambda ns: ns.scalar_tensor_tensor(
        out=stats[:, 2, :], in0=stats[:, 0, :], scalar=-1.0, in1=stats[:, 0, :],
        op0=ALU.mult, op1=ALU.mult), reads=[bstats], writes=[bstats])
    P.do("vector", lambda ns: ns.scalar_tensor_tensor(
        out=stats[:, 2, :], in0=stats[:, 1, :], scalar=1.0 / D, in1=stats[:, 2, :],
        op0=ALU.mult, op1=ALU.add), reads=[bstats], writes=[bstats])
    P.do("vector", lambda ns: ns.tensor_scalar(
        out=stats[:, 2, :], in0=stats[:, 2, :], scalar1=1.0e-5, scalar2=None,
        op0=ALU.add), reads=[bstats], writes=[bstats])
    P.do("scalar", lambda ns: ns.activation(
        out=stats[:, 3, :], in_=stats[:, 2, :], func=AF.Sqrt),
        reads=[bstats], writes=[bstats])
    P.do("vector", lambda ns: ns.reciprocal(
        out=stats[:, 3, :], in_=stats[:, 3, :]), reads=[bstats], writes=[bstats])
    for t in range(NT):
        P.do("vector", lambda ns, t=t: ns.tensor_scalar(
            out=xi[:, t, :], in0=x[:, t, :], scalar1=stats[:, 0, t:t + 1],
            scalar2=stats[:, 3, t:t + 1], op0=ALU.subtract, op1=ALU.mult),
            reads=[bx, bstats], writes=[bxi])
    eye = ctx.eye_f32
    psA, bA = ctx.psA, ctx.psA_b
    for t in range(NT):
        for dp in range(4):
            bank = (t * 4 + dp) % 4
            P.do("tensor", lambda ns, t=t, dp=dp, bank=bank: ns.transpose(
                psA[:, bank * 512:bank * 512 + 128],
                xi[:, t, 2 * dp * 128:(2 * dp + 1) * 128], eye[:]),
                reads=[bxi, ctx.beye_f32], writes=[bA[bank]])
            P.do("tensor", lambda ns, t=t, dp=dp, bank=bank: ns.transpose(
                psA[:, bank * 512 + 128:bank * 512 + 256],
                xi[:, t, (2 * dp + 1) * 128:(2 * dp + 2) * 128], eye[:]),
                reads=[bxi, ctx.beye_f32], writes=[bA[bank]])
            P.do("vector", lambda ns, t=t, dp=dp, bank=bank: ns.tensor_copy(
                out=xnT[:, 2 * dp:2 * dp + 2, t * 128:(t + 1) * 128],
                in_=psA[:, bank * 512:bank * 512 + 256].rearrange(
                    "p (a c) -> p a c", a=2)),
                reads=[bA[bank]], writes=[bxnT])


def emit_l0_attn(ctx):
    nc, P, st = ctx.nc, ctx.P, ctx.st
    xnT, bxnT = ctx.xnT, ctx.bxnT
    x, bx = ctx.x, ctx.bx
    inp = ctx.inp
    psA, psB = ctx.psA, ctx.psB
    bA, bB = ctx.psA_b, ctx.psB_b

    xib = ctx.xi[:].rearrange("p a b -> p (a b)").bitcast(BF16)
    bxi = ctx.bxi
    Kre = xib[:, 0:8192].rearrange("p (mt t) -> p mt t", mt=4)
    Kim = xib[:, 8192:16384].rearrange("p (mt t) -> p mt t", mt=4)
    Qre, Qim = Kre, Kim
    Vre_mt = xib[:, 16384:18432]
    Vim_mt = xib[:, 18432:20480]
    wstage = xib[:, 20480:22528].rearrange("p (dt e) -> p dt e", dt=8)
    Kre4 = xib[0:1, 24576:26624]
    Qre4 = Kre4
    Vre4 = xib[0:1, 26624:28672]
    junkb = xib[:, 28672:30720]
    idfA4 = xib[0:1, 30720:31744]
    bV, bw, b4, bj, bi4 = (Buf("l0V"), Buf("l0ws"), Buf("l0b4"), Buf("l0junk"),
                           Buf("l0i4"))
    fork_arena(ctx, "xi", bxi, bV, bw, b4, bj, bi4)

    KTt = st.enter_context(nc.sbuf_tensor("l0KT", [128, 8, T_TOK], BF16))
    bKT = Buf("l0KT")
    tmpv = KTt[:].rearrange("p a b -> p (a b)").bitcast(F32)
    arena = st.enter_context(nc.sbuf_tensor("l0ar", [128, 4096], BF16))
    barC = Buf("l0ar")
    dftst = [arena[:, i * 1024:(i + 1) * 1024].rearrange("p (dt m) -> p dt m", dt=8)
             for i in range(2)]
    dft4 = arena[:, 2048:2056].rearrange("p (dt m) -> p dt m", dt=8)
    idfst = [arena[:, i * 1024:(i + 1) * 1024] for i in range(4)]

    bias_aps = [xib[0:1, 22528:23552], xib[0:1, 23552:24576],
                xib[0:1, 31744:32768]]
    bbias = Buf("l0bias")
    ctx.groups["xi"].append(bbias)
    bbias.rs = list(bxi.rs)
    bbias.w = bxi.w
    for i, nm in enumerate(["L0_attn_bq", "L0_attn_bk", "L0_attn_bv"]):
        P.dma("gpsimd", "l0bias", lambda ns, i=i, nm=nm: ns.dma_start(
            out=bias_aps[i], in_=inp[nm].unsqueeze(0)),
            writes=[bbias])
    P.dma("gpsimd", "l0dft4", lambda ns: ns.dma_start(
        out=dft4, in_=inp["c_dftcos"][:, 512:513].rearrange(
            "(dt p) m -> p dt m", p=128),
        allow_slow_non_contiguous=True), writes=[barC])

    sacc = st.enter_context(nc.sbuf_tensor("l0sacc", [128, 8, 5], F32))
    bsacc = Buf("sacc")
    pp = [0]

    def proj(wname, bi):
        for quarter in range(4):
            P.dma("gpsimd", "l0ws", lambda ns, wname=wname, quarter=quarter: ns.dma_start(
                out=wstage,
                in_=inp[wname][:, quarter * 256:(quarter + 1) * 256].rearrange(
                    "(dt p) e -> p dt e", p=128), max_dma_last_dim=1024),
                writes=[bw])
            for el in range(2):
                et = quarter * 2 + el
                ps, bps = (psA, bA) if pp[0] % 2 == 0 else (psB, bB)
                pp[0] += 1
                def mk(ns, el=el, et=et, bi=bi, ps=ps):
                    inst = None
                    for dt in range(8):
                        inst = mm_chunks(ns, lambda o, w: ps[:, o:o + w],
                                         wstage[:, dt, el * 128:(el + 1) * 128],
                                         lambda o, w: xnT[:, dt, o:o + w], T_TOK,
                                         start=(dt == 0), stop=False)
                    inst = mm_chunks(ns, lambda o, w: ps[:, o:o + w],
                                     bias_aps[bi][:, et * 128:(et + 1) * 128],
                                     lambda o, w: ctx.ones1[:, 0:w], T_TOK,
                                     start=False, stop=True)
                    return inst
                P.do("tensor", mk, reads=[bw, bxnT, bbias, ctx.bones1], writes=bps)
                P.do("scalar", lambda ns, et=et, ps=ps: ns.activation(
                    out=KTt[:, et, :], in_=ps[:], func=AF.Identity),
                    reads=bps, writes=[bKT])

    def rfft_tile(table, msl, out_ap, bout, rows=128):
        sb = pp[0] % 2
        if rows == 128:
            P.dma("gpsimd", f"l0dft{sb}", lambda ns, table=table, msl=msl, sb=sb: ns.dma_start(
                out=dftst[sb], in_=inp[table][:, msl].rearrange(
                    "(dt p) m -> p dt m", p=128), max_dma_last_dim=1024),
                writes=[barC])
            lhs_src, blhs = dftst[sb], barC
        else:
            lhs_src, blhs = dft4, barC
        ps, bps = (psA, bA) if pp[0] % 2 == 0 else (psB, bB)
        pp[0] += 1
        def mk(ns, lhs_src=lhs_src, ps=ps, rows=rows):
            inst = None
            for dt in range(8):
                inst = mm_chunks(ns, lambda o, w: ps[0:rows, o:o + w],
                                 lhs_src[:, dt, :],
                                 lambda o, w: KTt[:, dt, o:o + w], T_TOK,
                                 start=(dt == 0), stop=(dt == 7))
            return inst
        P.do("tensor", mk, reads=[blhs, bKT], writes=bps)
        P.do("scalar", lambda ns, out_ap=out_ap, ps=ps, rows=rows: ns.activation(
            out=out_ap, in_=ps[0:rows, 0:T_TOK], func=AF.Identity),
            reads=bps, writes=[bout])

    proj("L0_attn_Wk", 1)
    for mt in range(4):
        rfft_tile("c_dftcos", slice(mt * 128, (mt + 1) * 128), Kre[:, mt, :], bxi)
        rfft_tile("c_dftsin", slice(mt * 128, (mt + 1) * 128), Kim[:, mt, :], bxi)
    rfft_tile("c_dftcos", None, Kre4, b4, rows=1)

    proj("L0_attn_Wv", 2)
    for mt in range(4):
        rfft_tile("c_dftcos", slice(mt * 128, (mt + 1) * 128), Vre_mt, bV)
        rfft_tile("c_dftsin", slice(mt * 128, (mt + 1) * 128), Vim_mt, bV)
        for slot, A, bAb, Vb in [(0, Kre[:, mt, :], bxi, Vre_mt),
                                 (1, Kim[:, mt, :], bxi, Vim_mt),
                                 (2, Kre[:, mt, :], bxi, Vim_mt),
                                 (3, Kim[:, mt, :], bxi, Vre_mt)]:
            P.do("vector", lambda ns, A=A, Vb=Vb, mt=mt, slot=slot: ns.scalar_tensor_tensor(
                out=junkb, in0=A, scalar=1.0, in1=Vb,
                op0=ALU.mult, op1=ALU.mult, accum_out=sacc[:, slot, mt:mt + 1]),
                reads=[bAb, bV], writes=[bj, bsacc])
    rfft_tile("c_dftcos", None, Vre4, b4, rows=1)
    P.do("vector", lambda ns: ns.scalar_tensor_tensor(
        out=junkb[0:1, :], in0=Kre4, scalar=1.0, in1=Vre4,
        op0=ALU.mult, op1=ALU.mult, accum_out=sacc[0:1, 0, 4:5]),
        reads=[b4], writes=[bj, bsacc])
    P.do("vector", lambda ns: ns.tensor_tensor(
        out=sacc[:, 4, 0:4], in0=sacc[:, 0, 0:4], in1=sacc[:, 1, 0:4],
        op=ALU.subtract), reads=[bsacc], writes=[bsacc])
    P.do("vector", lambda ns: ns.tensor_tensor(
        out=sacc[:, 5, 0:4], in0=sacc[:, 2, 0:4], in1=sacc[:, 3, 0:4],
        op=ALU.add), reads=[bsacc], writes=[bsacc])
    P.do("vector", lambda ns: ns.tensor_scalar(
        out=sacc[:, 6, 0:4], in0=sacc[:, 4, 0:4], scalar1=-1.0, scalar2=None,
        op0=ALU.mult), reads=[bsacc], writes=[bsacc])

    proj("L0_attn_Wq", 0)
    for mt in range(4):
        rfft_tile("c_dftcos", slice(mt * 128, (mt + 1) * 128), Qre[:, mt, :], bxi)
        rfft_tile("c_dftsin", slice(mt * 128, (mt + 1) * 128), Qim[:, mt, :], bxi)
    rfft_tile("c_dftcos", None, Qre4, b4, rows=1)

    t1 = tmpv[:, 0:2048]
    t2 = tmpv[:, 2048:4096]
    for mt in range(4):
        qre, qim = Qre[:, mt, :], Qim[:, mt, :]
        sre = sacc[:, 4, mt:mt + 1]
        sim = sacc[:, 5, mt:mt + 1]
        nre = sacc[:, 6, mt:mt + 1]
        P.do("vector", lambda ns, qre=qre, sre=sre: ns.tensor_scalar(
            out=t1, in0=qre, scalar1=sre, scalar2=None, op0=ALU.mult),
            reads=[bxi, bsacc], writes=[bKT])
        P.do("vector", lambda ns, qre=qre, sim=sim: ns.tensor_scalar(
            out=t2, in0=qre, scalar1=sim, scalar2=None, op0=ALU.mult),
            reads=[bxi, bsacc], writes=[bKT])
        P.do("vector", lambda ns, qre=qre, qim=qim, nre=nre: ns.scalar_tensor_tensor(
            out=qre, in0=qim, scalar=nre, in1=t2,
            op0=ALU.mult, op1=ALU.add), reads=[bxi, bsacc, bKT], writes=[bxi])
        P.do("vector", lambda ns, qim=qim, sim=sim: ns.scalar_tensor_tensor(
            out=qim, in0=qim, scalar=sim, in1=t1,
            op0=ALU.mult, op1=ALU.add), reads=[bxi, bsacc, bKT], writes=[bxi])
    P.do("vector", lambda ns: ns.tensor_scalar(
        out=Qre4, in0=Qre4, scalar1=sacc[0:1, 0, 4:5], scalar2=None,
        op0=ALU.mult), reads=[b4, bsacc], writes=[b4])
    Oim, Ore = Qre, Qim
    Ore4 = Qre4
    idfA4 = xib[0:1, 30720:31744]
    P.dma("gpsimd", "l0idfA4", lambda ns: ns.dma_start(
        out=idfA4, in_=inp["c_idftA"][512:513, :]), writes=[bi4])

    for g in range(2):
        for q in range(4):
            mt = g * 2 + q // 2
            tab = "c_idftA" if q % 2 == 0 else "c_idftB"
            P.dma("gpsimd", f"l0idf{q}", lambda ns, mt=mt, tab=tab, q=q: ns.dma_start(
                out=idfst[q], in_=inp[tab][mt * 128:(mt + 1) * 128, :],
                max_dma_last_dim=4096), writes=[barC])
        for t in range(NT):
            ps, bps = (psA, bA) if t % 2 == 0 else (psB, bB)
            def mk(ns, t=t, g=g, ps=ps):
                inst = None
                for q2 in range(2):
                    mt = g * 2 + q2
                    inst = mm_chunks(ns, lambda o, w: ps[:, o:o + w],
                                     Ore[:, mt, t * 128:(t + 1) * 128],
                                     lambda o, w: idfst[q2 * 2][:, o:o + w], D,
                                     start=(q2 == 0), stop=False)
                    inst = mm_chunks(ns, lambda o, w: ps[:, o:o + w],
                                     Oim[:, mt, t * 128:(t + 1) * 128],
                                     lambda o, w: idfst[q2 * 2 + 1][:, o:o + w], D,
                                     start=False, stop=(g == 0 and q2 == 1))
                if g == 1:
                    inst = mm_chunks(ns, lambda o, w: ps[:, o:o + w],
                                     Ore4[:, t * 128:(t + 1) * 128],
                                     lambda o, w: idfA4[:, o:o + w], D,
                                     start=False, stop=True)
                return inst
            P.do("tensor", mk, reads=[bxi, b4, barC, bi4],
                 writes=bps[0:2])
            P.do("vector", lambda ns, t=t, ps=ps: ns.tensor_tensor(
                out=x[:, t, :], in0=ps[:, 0:1024], in1=x[:, t, :], op=ALU.add),
                reads=bps[0:2] + [bx], writes=[bx])

    ctx.moe_arena = KTt[:].rearrange("p a b -> p (a b)")
    ctx.moe_arena_buf = bKT


def emit_moe(ctx, names, act_fn, n_exp=8, has_br=False):
    nc, P, st = ctx.nc, ctx.P, ctx.st
    lid = ctx.lid
    xnT, bxnT = ctx.xnT, ctx.bxnT
    x, bx = ctx.x, ctx.bx
    inp = ctx.inp
    NF = 32

    xif = ctx.xi[:].rearrange("p a b -> p (a b)")
    xib = xif.bitcast(BF16)
    hT = xib[:, 0:32768].rearrange("p (f c) -> p f c", f=NF)
    bhT = Buf("hT")
    fork_arena(ctx, "xi", bhT)

    ma = ctx.moe_arena
    maf = ma.bitcast(F32)
    w1v = [ma[:, i * 4096:(i + 1) * 4096].rearrange("p (dt f) -> p dt f", dt=8)
           for i in range(2)]
    w2v = [ma[:, 8192 + i * 512:8192 + (i + 1) * 512] for i in range(2)]
    cT = ma[0:n_exp, 10240:12288]
    b2all = ma[0:n_exp, 12288:13312]
    wgt = maf[:, 6656:6656 + NT * n_exp].rearrange("p (t e) -> p t e", t=NT)
    b1sb = maf[:, 6784:6816]
    gsc = maf[:, 6816:6864].rearrange("p (a e) -> p a e", a=6)
    b1stage = maf[0:32, 6864:6992]
    gate_w = ma[:, 13984:14048].rearrange("p (dt e) -> p dt e", dt=8)
    br_row = ma[0:1, 14048:14048 + n_exp]
    bw1 = [Buf("w1a"), Buf("w1b")]
    bw2 = [Buf("w2a"), Buf("w2b")]
    bcT = Buf("cT")
    bb2 = Buf("b2")
    bwgt = Buf("wgt")
    bb1 = Buf("b1")
    bb1s = Buf("b1s")
    bgsc = Buf("gsc")
    bgate = Buf("gw")
    bbr = Buf("br")
    handoff(ctx.moe_arena_buf, bw1[0], bw1[1], bw2[0], bw2[1], bcT, bb2,
            bwgt, bb1, bb1s, bgsc, bgate, bbr)

    P.dma("gpsimd", f"moeGW{lid}", lambda ns: ns.dma_start(
        out=gate_w, in_=inp[names["gate_w"]].rearrange("(dt p) e -> p dt e", p=128)),
        writes=[bgate])
    P.dma("gpsimd", f"moeB2{lid}", lambda ns: ns.dma_start(
        out=b2all, in_=inp[names["b2"]]), writes=[bb2])
    if has_br:
        P.dma("gpsimd", f"moeBR{lid}", lambda ns: ns.dma_start(
            out=br_row, in_=inp[names["br"]].unsqueeze(0)), writes=[bbr])

    psA, psB = ctx.psA, ctx.psB
    bA, bB = ctx.psA_b, ctx.psB_b
    ones1 = ctx.ones1

    for t in range(NT):
        def gmk(ns, t=t):
            inst = None
            for dt in range(ND):
                inst = ns.matmul(psA[:, 0:n_exp], xnT[:, dt, t * 128:(t + 1) * 128],
                                 gate_w[:, dt, :], start=(dt == 0), stop=not has_br)
            if has_br:
                inst = ns.matmul(psA[:, 0:n_exp], ones1[:, 0:128], br_row,
                                 start=False, stop=True)
            return inst
        P.do("tensor", gmk, reads=[bxnT, bgate, ctx.bones1] + ([bbr] if has_br else []),
             writes=[bA[0]])
        P.do("vector", lambda ns, t=t: ns.tensor_scalar(
            out=gsc[:, 0, :], in0=psA[:, 0:n_exp], scalar1=1.0, scalar2=None,
            op0=ALU.mult), reads=[bA[0]], writes=[bgsc])
        P.do("vector", lambda ns: ns.tensor_reduce(
            out=gsc[:, 1, 0:1], in_=gsc[:, 0, :], axis=AX.X, op=ALU.max,
            negate=True), reads=[bgsc], writes=[bgsc])
        P.do("vector", lambda ns: ns.tensor_scalar(
            out=gsc[:, 2, :], in0=gsc[:, 0, :], scalar1=gsc[:, 1, 0:1],
            scalar2=None, op0=ALU.add), reads=[bgsc], writes=[bgsc])
        P.do("vector", lambda ns: ns.tensor_scalar(
            out=gsc[:, 3, :], in0=gsc[:, 2, :], scalar1=0.0, scalar2=-1e30,
            op0=ALU.is_equal, op1=ALU.mult), reads=[bgsc], writes=[bgsc])
        P.do("vector", lambda ns: ns.tensor_tensor(
            out=gsc[:, 3, :], in0=gsc[:, 2, :], in1=gsc[:, 3, :], op=ALU.add),
            reads=[bgsc], writes=[bgsc])
        P.do("vector", lambda ns: ns.tensor_reduce(
            out=gsc[:, 1, 1:2], in_=gsc[:, 3, :], axis=AX.X, op=ALU.max),
            reads=[bgsc], writes=[bgsc])
        P.do("scalar", lambda ns: ns.activation(
            out=gsc[:, 4, :], in_=gsc[:, 2, :], func=AF.Exp),
            reads=[bgsc], writes=[bgsc])
        P.do("vector", lambda ns: ns.tensor_scalar(
            out=gsc[:, 5, :], in0=gsc[:, 2, :], scalar1=gsc[:, 1, 1:2],
            scalar2=None, op0=ALU.is_ge), reads=[bgsc], writes=[bgsc])
        P.do("vector", lambda ns: ns.tensor_tensor(
            out=gsc[:, 4, :], in0=gsc[:, 4, :], in1=gsc[:, 5, :], op=ALU.mult),
            reads=[bgsc], writes=[bgsc])
        P.do("vector", lambda ns: ns.tensor_reduce(
            out=gsc[:, 1, 2:3], in_=gsc[:, 4, :], axis=AX.X, op=ALU.add),
            reads=[bgsc], writes=[bgsc])
        P.do("vector", lambda ns: ns.reciprocal(
            out=gsc[:, 1, 3:4], in_=gsc[:, 1, 2:3]), reads=[bgsc], writes=[bgsc])
        P.do("vector", lambda ns, t=t: ns.tensor_scalar(
            out=wgt[:, t, :], in0=gsc[:, 4, :], scalar1=gsc[:, 1, 3:4],
            scalar2=None, op0=ALU.mult), reads=[bgsc], writes=[bwgt])

    w1_hbm = inp[names["w1"]]
    w2_hbm = inp[names["w2"]]
    b1_hbm = inp[names["b1"]]

    for e in range(n_exp):
        P.dma("sync", f"moeb1_{lid}", lambda ns, e=e: ns.dma_start(
            out=b1stage, in_=b1_hbm[e].rearrange("(t p) -> t p", p=128)),
            writes=[bb1s])
        P.do("tensor", lambda ns: ns.transpose(
            ctx.psA[0:128, 1536:1568], b1stage, ctx.eye_f32[0:32, 0:32]),
            reads=[bb1s, ctx.beye_f32], writes=[ctx.psA_b[3]])
        P.do("vector", lambda ns: ns.tensor_copy(
            out=b1sb, in_=ctx.psA[:, 1536:1568]),
            reads=[ctx.psA_b[3]], writes=[bb1])
        for tc in range(2):
            for fc in range(8):
                wb, bwb = w1v[fc % 2], bw1[fc % 2]
                P.dma("gpsimd", f"moew1_{lid}{fc % 2}", lambda ns, e=e, fc=fc, wb=wb: ns.dma_start(
                    out=wb, in_=w1_hbm[e, :, fc * 512:(fc + 1) * 512].rearrange(
                        "(dt p) f -> p dt f", p=128), max_dma_last_dim=2048),
                    writes=[bwb])
                for fi in range(4):
                    f = fc * 4 + fi
                    pr = (f % 2) * 1024
                    bpr = [bB[(f % 2) * 2], bB[(f % 2) * 2 + 1]]
                    def hmk(ns, wb=wb, fi=fi, tc=tc, pr=pr):
                        inst = None
                        for dt in range(8):
                            inst = mm_chunks(
                                ns, lambda o, w, pr=pr: psB[:, pr + o:pr + o + w],
                                wb[:, dt, fi * 128:(fi + 1) * 128],
                                lambda o, w, tc=tc, dt=dt: xnT[:, dt, tc * 1024 + o:
                                                               tc * 1024 + o + w],
                                1024, start=(dt == 0), stop=(dt == 7))
                        return inst
                    P.do("tensor", hmk, reads=[bwb, bxnT], writes=bpr)
                    P.do("scalar", lambda ns, f=f, pr=pr: ns.activation(
                        out=hT[:, f, :], in_=psB[:, pr:pr + 1024],
                        func=act_fn, bias=b1sb[:, f:f + 1]),
                        reads=bpr + [bb1], writes=[bhT])
            for dh in range(2):
                for f in range(NF):
                    wb2, bwb2 = w2v[f % 2], bw2[f % 2]
                    P.dma("gpsimd", f"moew2_{lid}{f % 2}", lambda ns, e=e, f=f, dh=dh, wb2=wb2: ns.dma_start(
                        out=wb2, in_=w2_hbm[e, f * 128:(f + 1) * 128,
                                            dh * 512:(dh + 1) * 512],
                        max_dma_last_dim=2048), writes=[bwb2])
                    def ymk(ns, f=f, wb2=wb2):
                        inst = None
                        for tt in range(8):
                            ps = psA if tt < 4 else psB
                            inst = ns.matmul(
                                ps[:, (tt % 4) * 512:(tt % 4 + 1) * 512],
                                hT[:, f, tt * 128:(tt + 1) * 128],
                                wb2, start=(f == 0), stop=(f == NF - 1))
                        return inst
                    P.do("tensor", ymk, reads=[bwb2, bhT], writes=bA + bB)
                for tt in range(8):
                    t = tc * 8 + tt
                    ps = psA if tt < 4 else psB
                    bp = bA[tt % 4] if tt < 4 else bB[tt % 4]
                    P.do("vector", lambda ns, t=t, tt=tt, dh=dh, e=e, ps=ps: ns.scalar_tensor_tensor(
                        out=x[:, t, dh * 512:(dh + 1) * 512],
                        in0=ps[:, (tt % 4) * 512:(tt % 4 + 1) * 512],
                        scalar=wgt[:, t, e:e + 1],
                        in1=x[:, t, dh * 512:(dh + 1) * 512],
                        op0=ALU.mult, op1=ALU.add),
                        reads=[bp, bwgt, bx], writes=[bx])

    eye = ctx.eye_f32
    for t in range(NT):
        bank = t % 4
        P.do("tensor", lambda ns, t=t, bank=bank: ns.transpose(
            psA[0:n_exp, bank * 512:bank * 512 + 128],
            wgt[:, t, :], eye[:]), reads=[bwgt, ctx.beye_f32], writes=[bA[bank]])
        P.do("vector", lambda ns, t=t, bank=bank: ns.tensor_copy(
            out=cT[:, t * 128:(t + 1) * 128],
            in_=psA[0:n_exp, bank * 512:bank * 512 + 128]),
            reads=[bA[bank]], writes=[bcT])
    for t in range(NT):
        ps, bps = (psA, bA[0:2]) if t % 2 == 0 else (psB, bB[0:2])
        def bmk(ns, t=t, ps=ps):
            return mm_chunks(ns, lambda o, w: ps[:, o:o + w],
                             cT[:, t * 128:(t + 1) * 128],
                             lambda o, w: b2all[:, o:o + w],
                             D, start=True, stop=True)
        P.do("tensor", bmk, reads=[bcT, bb2], writes=bps)
        P.do("vector", lambda ns, t=t, ps=ps: ns.tensor_tensor(
            out=x[:, t, :], in0=ps[:, 0:1024], in1=x[:, t, :], op=ALU.add),
            reads=bps + [bx], writes=[bx])


def emit_l1_attn(ctx):
    nc, P, st = ctx.nc, ctx.P, ctx.st
    xnT, bxnT = ctx.xnT, ctx.bxnT
    x, bx = ctx.x, ctx.bx
    inp = ctx.inp
    psA, psB = ctx.psA, ctx.psB
    bA, bB = ctx.psA_b, ctx.psB_b
    psAb = psA[:].bitcast(BF16)

    xif = ctx.xi[:].rearrange("p a b -> p (a b)")
    xib = xif.bitcast(BF16)
    oT = xib[:, 0:16384].rearrange("p (dt t) -> p dt t", dt=8)
    qh = xib[0:64, 16384:18432]
    kh = xib[0:64, 18432:20480]
    sq = xib[0:64, 20480:22528]
    Ut = [xib[:, 22528:24576], xib[:, 24576:26624]]
    Osb = xib[0:65, 26624:28672]
    Vext = xib[:, 28672:29712].rearrange("p (t c) -> p t c", t=16)
    rowbuf = xib[0:1, 29712:31760]
    boT, bqh, bkh, bsq, bO, bV, brow = (Buf("oT"), Buf("qh1"), Buf("kh1"),
                                        Buf("sq1"), Buf("Os1"), Buf("Vx1"),
                                        Buf("row1"))
    bUt = [Buf("U0a"), Buf("U1a")]
    fork_arena(ctx, "xi", boT, bqh, bkh, bsq, bUt[0], bUt[1], bO, bV, brow)

    wst = st.enter_context(nc.sbuf_tensor("l1wst", [64, 1024], BF16))
    bwst = Buf("l1wst")
    wT = st.enter_context(nc.sbuf_tensor("l1wT", [128, 3, 8, 64], BF16))
    bwT = Buf("l1wT")
    bst = st.enter_context(nc.sbuf_tensor("l1bst", [16, 3, 64], F32))
    bbst = Buf("l1bst")
    bqT = st.enter_context(nc.sbuf_tensor("l1bqT", [64, 3, 16], F32))
    bbqT = Buf("l1bqT")
    ccol = st.enter_context(nc.sbuf_tensor("l1cc", [128, 1], F32))
    bcc = Buf("l1cc")
    crow = st.enter_context(nc.sbuf_tensor("l1cr", [1, 4], F32))
    bcr = Buf("l1cr")

    for i, nm in enumerate(["L1_attn_bq", "L1_attn_bk", "L1_attn_bv"]):
        P.dma("sync", "l1bst", lambda ns, i=i, nm=nm: ns.dma_start(
            out=bst[:, i, :], in_=inp[nm]), writes=[bbst])
        P.do("tensor", lambda ns, i=i: ns.transpose(
            psA[0:64, 1536:1552], bst[:, i, :], ctx.eye_f32[0:16, 0:16]),
            reads=[bbst, ctx.beye_f32], writes=[bA[3]])
        P.do("vector", lambda ns, i=i: ns.tensor_copy(
            out=bqT[:, i, :], in_=psA[0:64, 1536:1552]),
            reads=[bA[3]], writes=[bbqT])
    P.do("gpsimd", lambda ns: ns.memset(Vext[:, :, 64:65], 1.0), writes=[bV])

    for h in range(16):
        for wi, wname in enumerate(["L1_attn_Wq", "L1_attn_Wk", "L1_attn_Wv"]):
            P.dma("gpsimd", "l1wst", lambda ns, wname=wname, h=h: ns.dma_start(
                out=wst[:], in_=inp[wname][h], max_dma_last_dim=4096),
                writes=[bwst])
            def trk(ns):
                inst = None
                for et in range(8):
                    inst = ns.transpose(
                        psAb[:, 1024 + et * 64:1024 + (et + 1) * 64],
                        wst[:, et * 128:(et + 1) * 128], ctx.eye_bf[0:64, 0:64])
                return inst
            P.do("tensor", trk, reads=[bwst, ctx.beye_bf], writes=[bA[0], bA[1]])
            P.do("vector", lambda ns, wi=wi: ns.tensor_copy(
                out=wT[:, wi, :, :], in_=psAb[:, 1024:1536].rearrange(
                    "p (et d) -> p et d", et=8)),
                reads=[bA[0], bA[1]], writes=[bwT])
        for wi, dst, bdst, ci in [(0, qh, bqh, 0), (1, kh, bkh, 1)]:
            def pmk(ns, wi=wi):
                inst = None
                for et in range(8):
                    inst = mm_chunks(ns, lambda o, w: psB[0:64, o:o + w],
                                     wT[:, wi, et, :],
                                     lambda o, w: xnT[:, et, o:o + w], T_TOK,
                                     start=(et == 0), stop=(et == 7))
                return inst
            P.do("tensor", pmk, reads=[bwT, bxnT], writes=bB)
            P.do("scalar", lambda ns, dst=dst, wi=wi, h=h: ns.activation(
                out=dst, in_=psB[0:64, 0:T_TOK], func=AF.Identity,
                bias=bqT[:, wi, h:h + 1]),
                reads=bB + [bbqT], writes=[bdst])
            P.do("scalar", lambda ns, dst=dst: ns.activation(
                out=sq, in_=dst, func=AF.Square), reads=[bdst], writes=[bsq])
            def ssk(ns):
                return mm_chunks(ns, lambda o, w: psA[0:1, o:o + w],
                                 ctx.onesc_b[0:64, :],
                                 lambda o, w: sq[:, o:o + w], T_TOK,
                                 start=True, stop=True)
            P.do("tensor", ssk, reads=[bsq, ctx.bonesc_b], writes=bA)
            P.do("scalar", lambda ns: ns.activation(
                out=rowbuf, in_=psA[0:1, 0:T_TOK], func=AF.Sqrt),
                reads=bA, writes=[brow])
            P.do("vector", lambda ns, ci=ci: ns.tensor_reduce(
                out=crow[:, ci:ci + 1], in_=rowbuf, axis=AX.X, op=ALU.max),
                reads=[brow], writes=[bcr])
        P.do("vector", lambda ns: ns.tensor_tensor(
            out=crow[:, 2:3], in0=crow[:, 0:1], in1=crow[:, 1:2], op=ALU.mult),
            reads=[bcr], writes=[bcr])
        P.do("vector", lambda ns: ns.tensor_scalar(
            out=crow[:, 2:3], in0=crow[:, 2:3], scalar1=-0.125, scalar2=None,
            op0=ALU.mult), reads=[bcr], writes=[bcr])
        P.do("tensor", lambda ns: ns.matmul(
            psA[:, 0:1], ctx.onesr_f[:], crow[:, 2:3], start=True, stop=True),
            reads=[bcr, ctx.bonesr_f], writes=[bA[0]])
        P.do("vector", lambda ns: ns.tensor_copy(out=ccol[:, 0:1], in_=psA[:, 0:1]),
            reads=[bA[0]], writes=[bcc])
        def vmk(ns):
            inst = None
            for et in range(8):
                inst = mm_chunks(ns, lambda o, w: psB[0:64, o:o + w],
                                 wT[:, 2, et, :],
                                 lambda o, w: xnT[:, et, o:o + w], T_TOK,
                                 start=(et == 0), stop=(et == 7))
            return inst
        P.do("tensor", vmk, reads=[bwT, bxnT], writes=bB)
        P.do("scalar", lambda ns, h=h: ns.activation(
            out=sq, in_=psB[0:64, 0:T_TOK], func=AF.Identity,
            bias=bqT[:, 2, h:h + 1]), reads=bB + [bbqT], writes=[bsq])
        def vtk(ns):
            inst = None
            for tt in range(16):
                inst = ns.transpose(psAb[:, tt * 64:(tt + 1) * 64],
                                    sq[:, tt * 128:(tt + 1) * 128],
                                    ctx.eye_bf[0:64, 0:64])
            return inst
        P.do("tensor", vtk, reads=[bsq, ctx.beye_bf], writes=[bA[0], bA[1]])
        P.do("vector", lambda ns: ns.tensor_copy(
            out=Vext[:, :, 0:64], in_=psAb[:, 0:1024].rearrange(
                "p (tt d) -> p tt d", tt=16)),
            reads=[bA[0], bA[1]], writes=[bV])
        for kt in range(16):
            ub, bub = Ut[kt % 2], bUt[kt % 2]
            def smk(ns, kt=kt):
                return mm_chunks(ns, lambda o, w: psA[:, o:o + w],
                                 kh[:, kt * 128:(kt + 1) * 128],
                                 lambda o, w: qh[:, o:o + w], T_TOK,
                                 start=True, stop=True)
            P.do("tensor", smk, reads=[bkh, bqh], writes=bA)
            P.do("scalar", lambda ns, ub=ub: ns.activation(
                out=ub, in_=psA[:, 0:T_TOK], func=AF.Exp,
                bias=ccol[:, 0:1], scale=0.125),
                reads=bA + [bcc], writes=[bub])
            def pvk(ns, kt=kt, ub=ub):
                return mm_chunks(ns, lambda o, w: psB[0:65, o:o + w],
                                 Vext[:, kt, :],
                                 lambda o, w: ub[:, o:o + w], T_TOK,
                                 start=(kt == 0), stop=(kt == 15))
            P.do("tensor", pvk, reads=[bV, bub], writes=bB)
        P.do("scalar", lambda ns: ns.activation(
            out=Osb, in_=psB[0:65, 0:T_TOK], func=AF.Identity),
            reads=bB, writes=[bO])
        def recip_row(ns):
            with nc.allow_low_precision(reason="bf16 softmax normalizer"):
                return ns.reciprocal(out=rowbuf, in_=Osb[64:65, :])
        P.do("vector", recip_row, reads=[bO], writes=[brow])
        def bck(ns):
            return mm_chunks(ns, lambda o, w: psA[0:64, o:o + w],
                             ctx.ones1[:, 0:64],
                             lambda o, w: rowbuf[:, o:o + w], T_TOK,
                             start=True, stop=True)
        P.do("tensor", bck, reads=[brow, ctx.bones1], writes=bA)
        base = (h % 2) * 64
        P.do("vector", lambda ns, h=h, base=base: ns.tensor_tensor(
            out=oT[base:base + 64, h // 2, :], in0=Osb[0:64, :],
            in1=psA[0:64, 0:T_TOK], op=ALU.mult),
            reads=bA + [bO], writes=[boT])

    WoT = xib[:, 16384:24576].rearrange("p (et d) -> p et d", et=8)
    wos = xib[:, 24576:25600].rearrange("p (dt c) -> p dt c", dt=8)
    borow_ap = xib[0:1, 25600:26624]
    bWoT, bwos, bbo = Buf("WoT"), Buf("wos"), Buf("borow")
    fork_arena(ctx, "xi", boT, bWoT, bwos, bbo)
    P.dma("gpsimd", "l1bo", lambda ns: ns.dma_start(
        out=borow_ap, in_=inp["L1_attn_bo"].unsqueeze(0)), writes=[bbo])
    for et in range(8):
        P.dma("gpsimd", "l1wos", lambda ns, et=et: ns.dma_start(
            out=wos, in_=inp["L1_attn_Wo"][:, et * 128:(et + 1) * 128].rearrange(
                "(dt p) c -> p dt c", p=128), max_dma_last_dim=512),
            writes=[bwos])
        def wtk(ns):
            inst = None
            for dt in range(8):
                inst = ns.transpose(psAb[:, dt * 128:(dt + 1) * 128],
                                    wos[:, dt, :], ctx.eye_bf[:])
            return inst
        P.do("tensor", wtk, reads=[bwos, ctx.beye_bf], writes=bA)
        P.do("vector", lambda ns, et=et: ns.tensor_copy(
            out=WoT[:, et, :], in_=psAb[:, 0:1024]),
            reads=bA, writes=[bWoT])
    for tt in range(NT):
        off = 0 if tt % 2 == 0 else 1024
        bps = bA[0:2] if tt % 2 == 0 else bA[2:4]
        def dmkk(ns, tt=tt, off=off):
            inst = None
            for et in range(8):
                inst = mm_chunks(ns, lambda o, w: psA[:, off + o:off + o + w],
                                 oT[:, et, tt * 128:(tt + 1) * 128],
                                 lambda o, w: WoT[:, et, o:o + w], D,
                                 start=(et == 0), stop=False)
            inst = mm_chunks(ns, lambda o, w: psA[:, off + o:off + o + w],
                             ctx.ones1[:, 0:128],
                             lambda o, w: borow_ap[:, o:o + w], D,
                             start=False, stop=True)
            return inst
        P.do("tensor", dmkk, reads=[boT, bWoT, bbo, ctx.bones1], writes=bps)
        P.do("vector", lambda ns, tt=tt, off=off: ns.tensor_tensor(
            out=x[:, tt, :], in0=psA[:, off:off + 1024], in1=x[:, tt, :],
            op=ALU.add), reads=bps + [bx], writes=[bx])


def emit_l1_ffn(ctx):
    nc, P, st = ctx.nc, ctx.P, ctx.st
    xnT, bxnT = ctx.xnT, ctx.bxnT
    inp = ctx.inp
    psA, psB = ctx.psA, ctx.psB
    bA, bB = ctx.psA_b, ctx.psB_b
    psAb = psA[:].bitcast(BF16)
    psBb = psB[:].bitcast(BF16)
    xi, bxi = ctx.xi, ctx.bxi
    NBLK = HIDP // 128

    spill_rec = P.dma("sync", "l1spill", lambda ns: ns.dma_start(
        out=inp["x_spill"], in_=ctx.x[:]), reads=[ctx.bx])

    xbf = ctx.x[:].rearrange("p a b -> p (a b)").bitcast(BF16)
    shiftb = xbf[:, 0:8192].rearrange("p (dt t) -> p dt t", dt=4)
    rtab = [xbf[:, 8192 + i * 1024:8192 + (i + 1) * 1024].rearrange(
        "p (t d) -> p t d", t=16) for i in range(4)]
    wqst = xbf[:, 12288:13824].rearrange("p (dt c) -> p dt c", dt=8)
    qpT = xbf[0:64, 13824:15872]
    kpT = xbf[0:64, 15872:17920]
    tokb = xbf[:, 17920:18944].rearrange("p (t d) -> p t d", t=16)
    rot1 = xbf[:, 18944:19968].rearrange("p (t d) -> p t d", t=16)
    vtok = xbf[:, 19968:20992].rearrange("p (t d) -> p t d", t=16)
    sqtok = xbf[:, 20992:23040]
    Ubuf = xbf[:, 23040:23680]
    UTb = xbf[:, 23680:24320].rearrange("p (k c) -> p k c", k=5)
    obuf = xbf[:, 24320:24384]
    qsb = [xbf[:, 24448 + i * 64:24448 + (i + 1) * 64] for i in range(2)]
    qsrow = [xbf[0:1, 24576 + i * 64:24576 + (i + 1) * 64] for i in range(3)]
    oTi = xbf[:, 24768:28864].rearrange("p (it t) -> p it t", it=2)
    wost = xbf[:, 28864:30912].rearrange("p (it d) -> p it d", it=2)
    (bsh, brt, bwq, bqpT, bkpT, btok, brot1, bvt, bsqt, bU, bUT, bob,
     bqs, boTi, bwos2) = [Buf(f"l1f{i}") for i in range(15)]
    ctx.groups["x"] = [ctx.bx]

    nrm = st.enter_context(nc.sbuf_tensor("l1nrm", [128, 3, 16], F32))
    bnrm = Buf("l1nrm")
    scol = st.enter_context(nc.sbuf_tensor("l1sc", [128, 4], F32))
    bscol = Buf("l1sc")
    masks = st.enter_context(nc.sbuf_tensor("l1msk", [128, 2, 128], F32))
    bmsk = Buf("l1msk")
    shm = st.enter_context(nc.sbuf_tensor("l1shm", [128, 2, 128], F32))
    bshm = Buf("l1shm")
    lnab = st.enter_context(nc.sbuf_tensor("l1ab", [128, 4, 16], F32))
    blnab = Buf("l1ab")
    gbb = st.enter_context(nc.sbuf_tensor("l1gb", [128, 2, 1024], BF16))
    bgbb = Buf("l1gb")
    b1T = st.enter_context(nc.sbuf_tensor("l1b1T", [128, 44], F32))
    bb1T = Buf("l1b1T")
    b1stg = st.enter_context(nc.sbuf_tensor("l1b1s", [44, 128], F32))
    bb1s = Buf("l1b1s")
    grow = st.enter_context(nc.sbuf_tensor("l1gr", [1, 2, 1024], BF16))
    bgrow = Buf("l1gr")
    b2row = st.enter_context(nc.sbuf_tensor("l1b2r", [1, 1024], BF16))
    bb2r = Buf("l1b2r")
    P.dma("sync", "l1msk", lambda ns: ns.dma_start(
        out=masks[:, 0, :], in_=inp["c_tri"]), writes=[bmsk])
    P.dma("sync", "l1msk", lambda ns: ns.dma_start(
        out=masks[:, 1, :], in_=inp["c_triu"]), writes=[bmsk])
    P.dma("sync", "l1shm", lambda ns: ns.dma_start(
        out=shm[:, 0, :], in_=inp["c_shift"]), writes=[bshm])
    P.dma("sync", "l1shm", lambda ns: ns.dma_start(
        out=shm[:, 1, :], in_=inp["c_bnd"]), writes=[bshm])

    fork_arena(ctx, "xi", bxi)
    for tt in range(NT):
        b01 = (tt % 2) * 2
        def itk(ns, tt=tt, b01=b01):
            inst = None
            for dt in range(8):
                inst = ns.transpose(
                    psAb[:, b01 * 1024 + dt * 128:b01 * 1024 + (dt + 1) * 128],
                    xnT[:, dt, tt * 128:(tt + 1) * 128], ctx.eye_bf[:])
            return inst
        P.do("tensor", itk, reads=[bxnT, ctx.beye_bf],
             writes=[bA[b01], bA[b01 + 1]])
        P.do("vector", lambda ns, tt=tt, b01=b01: ns.tensor_copy(
            out=xi[:, tt, :], in_=psAb[:, b01 * 1024:b01 * 1024 + 1024]),
            reads=[bA[b01], bA[b01 + 1]], writes=[bxi])

    def rebuild_shift():
        for i in range(4):
            P.do("vector", lambda ns, i=i: ns.tensor_copy(
                out=shiftb[:, i, 1:T_TOK], in_=xnT[:, 4 + i, 0:T_TOK - 1]),
                reads=[bxnT], writes=[bsh])
        P.do("gpsimd", lambda ns: ns.memset(shiftb[:, :, 0:1], 0.0),
            writes=[bsh])

    def h1T_tile(dt):
        return xnT[:, dt, :] if dt < 4 else shiftb[:, dt - 4, :]

    def regen_xinnerT():
        for tt in range(NT):
            b01 = (tt % 2) * 2
            def rtk_a(ns, tt=tt, b01=b01):
                inst = None
                for dt in range(4):
                    inst = ns.transpose(
                        psA[:, b01 * 512 + dt * 128:b01 * 512 + (dt + 1) * 128],
                        xi[:, tt, dt * 128:(dt + 1) * 128], ctx.eye_f32[:])
                return inst
            def rtk_b(ns, tt=tt, b01=b01):
                inst = None
                for dt in range(4):
                    inst = ns.transpose(
                        psB[:, b01 * 512 + dt * 128:b01 * 512 + (dt + 1) * 128],
                        xi[:, tt, (4 + dt) * 128:(5 + dt) * 128], ctx.eye_f32[:])
                return inst
            P.do("tensor", rtk_a, reads=[bxi, ctx.beye_f32],
                 writes=[bA[b01], bA[b01 + 1]])
            P.do("tensor", rtk_b, reads=[bxi, ctx.beye_f32],
                 writes=[bB[b01], bB[b01 + 1]])
            P.do("vector", lambda ns, tt=tt, b01=b01: ns.tensor_copy(
                out=xnT[:, 0:4, tt * 128:(tt + 1) * 128],
                in_=psA[:, b01 * 512:b01 * 512 + 512].rearrange(
                    "p (dt c) -> p dt c", dt=4)),
                reads=[bA[b01], bA[b01 + 1]], writes=[bxnT])
            P.do("vector", lambda ns, tt=tt, b01=b01: ns.tensor_copy(
                out=xnT[:, 4:8, tt * 128:(tt + 1) * 128],
                in_=psB[:, b01 * 512:b01 * 512 + 512].rearrange(
                    "p (dt c) -> p dt c", dt=4)),
                reads=[bB[b01], bB[b01 + 1]], writes=[bxnT])

    for il in range(2):
        fork_arena(ctx, "x", bsh, brt, bwq, bqpT, bkpT, btok, brot1, bvt,
                   bsqt, bU, bUT, bob, bqs, boTi, bwos2)
        for ti, cn in enumerate(["c_rotA", "c_rotB", "c_rotC", "c_rotD"]):
            P.dma("gpsimd", f"l1rt{ti}", lambda ns, ti=ti, cn=cn: ns.dma_start(
                out=rtab[ti][:], in_=inp[cn].rearrange("(t p) d -> p t d", p=128),
                max_dma_last_dim=256), writes=[brt])
        for si, nm in [(0, "L1_ffn_q_scale"), (1, "L1_ffn_k_scale")]:
            P.dma("gpsimd", "l1qsr", lambda ns, si=si, nm=nm, il=il: ns.dma_start(
                out=qsrow[si], in_=inp[nm][il].unsqueeze(0)), writes=[bqs])
        for si in range(2):
            P.do("vector", lambda ns, si=si: ns.tensor_copy(
                out=qsrow[2][:, 0:32], in_=qsrow[si][:, 32:64]),
                reads=[bqs], writes=[bqs])
            P.do("vector", lambda ns, si=si: ns.tensor_copy(
                out=qsrow[2][:, 32:64], in_=qsrow[si][:, 0:32]),
                reads=[bqs], writes=[bqs])
            P.do("tensor", lambda ns, si=si: ns.matmul(
                psA[:, 0:64], ctx.ones1[:, 0:128], qsrow[si],
                start=True, stop=True), reads=[bqs, ctx.bones1], writes=[bA[0]])
            P.do("vector", lambda ns, si=si: ns.tensor_copy(
                out=qsb[si], in_=psA[:, 0:64].bitcast(BF16)[:, 0:64])
                if False else ns.tensor_copy(out=qsb[si], in_=psA[:, 0:64]),
                reads=[bA[0]], writes=[bqs])
            tA = rtab[0] if si == 0 else rtab[2]
            tB = rtab[1] if si == 0 else rtab[3]
            P.do("vector", lambda ns, tA=tA, si=si: ns.tensor_tensor(
                out=tA[:], in0=tA[:],
                in1=qsb[si].unsqueeze(1).broadcast_to((128, 16, 64)),
                op=ALU.mult), reads=[brt, bqs], writes=[brt])
            P.do("tensor", lambda ns: ns.matmul(
                psA[:, 64:128], ctx.ones1[:, 0:128], qsrow[2],
                start=True, stop=True), reads=[bqs, ctx.bones1], writes=[bA[0]])
            P.do("vector", lambda ns, si=si: ns.tensor_copy(
                out=qsb[si], in_=psA[:, 64:128]), reads=[bA[0]], writes=[bqs])
            P.do("vector", lambda ns, tB=tB, si=si: ns.tensor_tensor(
                out=tB[:], in0=tB[:],
                in1=qsb[si].unsqueeze(1).broadcast_to((128, 16, 64)),
                op=ALU.mult), reads=[brt, bqs], writes=[brt])
        rebuild_shift()
        for h in range(8):
            if h % 4 == 0 and h > 0:
                emit_l1_wo_partial(ctx, il, h // 4 - 1, locals())
            for ci, coff in enumerate([h * 64, 512 + h * 64, 1024 + h * 64]):
                P.dma("gpsimd", "l1wq", lambda ns, ci=ci, coff=coff, il=il: ns.dma_start(
                    out=wqst[:, :, ci * 64:(ci + 1) * 64],
                    in_=inp["L1_ffn_wqkv"][il][:, coff:coff + 64].rearrange(
                        "(dt p) c -> p dt c", p=128), max_dma_last_dim=256),
                    writes=[bwq])
            for ci in range(3):
                def qmk(ns, ci=ci):
                    inst = None
                    for dt in range(8):
                        inst = mm_chunks(ns, lambda o, w: psB[0:64, o:o + w],
                                         wqst[:, dt, ci * 64:(ci + 1) * 64],
                                         lambda o, w: h1T_tile(dt)[:, o:o + w],
                                         T_TOK, start=(dt == 0), stop=(dt == 7))
                    return inst
                P.do("tensor", qmk, reads=[bwq, bxnT, bsh], writes=bB)
                P.do("scalar", lambda ns: ns.activation(
                    out=sqtok[0:64, 0:T_TOK], in_=psB[0:64, 0:T_TOK],
                    func=AF.Identity), reads=bB, writes=[bsqt])
                def t2tok(ns):
                    inst = None
                    for tt in range(16):
                        inst = ns.transpose(psAb[:, tt * 64:(tt + 1) * 64],
                                            sqtok[0:64, tt * 128:(tt + 1) * 128],
                                            ctx.eye_bf[0:64, 0:64])
                    return inst
                P.do("tensor", t2tok, reads=[bsqt, ctx.beye_bf],
                     writes=[bA[0], bA[1]])
                dst = vtok if ci == 2 else tokb
                dstb = bvt if ci == 2 else btok
                P.do("vector", lambda ns, dst=dst, dstb=dstb: ns.tensor_copy(
                    out=dst[:, :, :], in_=psAb[:, 0:1024].rearrange(
                        "p (tt d) -> p tt d", tt=16)),
                    reads=[bA[0], bA[1]], writes=[dstb])
                if ci == 2:
                    continue
                P.do("scalar", lambda ns: ns.activation(
                    out=sqtok[:, 0:1024],
                    in_=tokb[:].rearrange("p t d -> p (t d)"),
                    func=AF.Square), reads=[btok], writes=[bsqt])
                P.do("vector", lambda ns: ns.tensor_reduce(
                    out=nrm[:, 0, :], in_=sqtok[:, 0:1024].rearrange(
                        "p (t d) -> p t d", t=16), axis=AX.X, op=ALU.add),
                    reads=[bsqt], writes=[bnrm])
                P.do("scalar", lambda ns: ns.activation(
                    out=nrm[:, 1, :], in_=nrm[:, 0, :], func=AF.Sqrt),
                    reads=[bnrm], writes=[bnrm])
                P.do("vector", lambda ns: ns.tensor_scalar(
                    out=nrm[:, 1, :], in0=nrm[:, 1, :], scalar1=1.0e-12,
                    scalar2=None, op0=ALU.max), reads=[bnrm], writes=[bnrm])
                P.do("vector", lambda ns: ns.reciprocal(
                    out=nrm[:, 2, :], in_=nrm[:, 1, :]),
                    reads=[bnrm], writes=[bnrm])
                P.do("vector", lambda ns: ns.tensor_tensor(
                    out=tokb[:], in0=tokb[:],
                    in1=nrm[:, 2, :].unsqueeze(2).broadcast_to((128, 16, 64)),
                    op=ALU.mult), reads=[bnrm, btok], writes=[btok])
                tA = rtab[0] if ci == 0 else rtab[2]
                tB = rtab[1] if ci == 0 else rtab[3]
                P.do("vector", lambda ns: ns.tensor_copy(
                    out=rot1[:, :, 0:32], in_=tokb[:, :, 32:64]),
                    reads=[btok], writes=[brot1])
                P.do("vector", lambda ns: ns.tensor_copy(
                    out=rot1[:, :, 32:64], in_=tokb[:, :, 0:32]),
                    reads=[btok], writes=[brot1])
                P.do("vector", lambda ns, tB=tB: ns.tensor_tensor(
                    out=rot1[:], in0=rot1[:], in1=tB[:], op=ALU.mult),
                    reads=[brot1, brt], writes=[brot1])
                P.do("vector", lambda ns, tA=tA: ns.tensor_tensor(
                    out=tokb[:], in0=tokb[:], in1=tA[:], op=ALU.mult),
                    reads=[btok, brt], writes=[btok])
                P.do("vector", lambda ns: ns.tensor_tensor(
                    out=tokb[:], in0=tokb[:], in1=rot1[:], op=ALU.add),
                    reads=[btok, brot1], writes=[btok])
                def t2fm(ns):
                    inst = None
                    for tt in range(16):
                        inst = ns.transpose(
                            psAb[0:64, 2048 + tt * 128:2048 + (tt + 1) * 128],
                            tokb[:, tt, :], ctx.eye_bf[:])
                    return inst
                P.do("tensor", t2fm, reads=[btok, ctx.beye_bf],
                     writes=[bA[2], bA[3]])
                dst2, bdst2 = (qpT, bqpT) if ci == 0 else (kpT, bkpT)
                P.do("vector", lambda ns, dst2=dst2: ns.tensor_copy(
                    out=dst2, in_=psAb[0:64, 2048:4096]),
                    reads=[bA[2], bA[3]], writes=[bdst2])
            for qc in range(16):
                k0 = max(0, qc - 4)
                width = (qc + 1 - k0) * 128
                def smk(ns, qc=qc, k0=k0, width=width):
                    return mm_chunks(ns, lambda o, w: psA[:, o:o + w],
                                     qpT[:, qc * 128:(qc + 1) * 128],
                                     lambda o, w: kpT[:, k0 * 128 + o:k0 * 128 + o + w],
                                     width, start=True, stop=True)
                P.do("tensor", smk, reads=[bqpT, bkpT], writes=bA[0:2])
                P.do("vector", lambda ns, width=width: ns.tensor_tensor(
                    out=psA[:, width - 128:width], in0=psA[:, width - 128:width],
                    in1=masks[:, 0, :], op=ALU.add),
                    reads=[bmsk, bA[0], bA[1]], writes=[bA[0], bA[1]])
                if qc >= 4:
                    P.do("vector", lambda ns: ns.tensor_tensor(
                        out=psA[:, 0:128], in0=psA[:, 0:128],
                        in1=masks[:, 1, :], op=ALU.add),
                        reads=[bmsk, bA[0]], writes=[bA[0]])
                P.do("vector", lambda ns, width=width: ns.tensor_reduce(
                    out=scol[:, 0:1], in_=psA[:, 0:width], axis=AX.X,
                    op=ALU.max, negate=True), reads=bA[0:2], writes=[bscol])
                P.do("vector", lambda ns: ns.tensor_scalar(
                    out=scol[:, 1:2], in0=scol[:, 0:1], scalar1=8.0,
                    scalar2=None, op0=ALU.mult), reads=[bscol], writes=[bscol])
                P.do("scalar", lambda ns, width=width: ns.activation(
                    out=Ubuf[:, 0:width], in_=psA[:, 0:width], func=AF.Exp,
                    bias=scol[:, 1:2], scale=8.0, accum_out=scol[:, 2:3]),
                    reads=bA[0:2] + [bscol], writes=[bU, bscol])
                nk = width // 128
                def utk(ns, nk=nk):
                    inst = None
                    for j in range(nk):
                        inst = ns.transpose(
                            psBb[:, 2048 + j * 128:2048 + (j + 1) * 128],
                            Ubuf[:, j * 128:(j + 1) * 128], ctx.eye_bf[:])
                    return inst
                P.do("tensor", utk, reads=[bU, ctx.beye_bf],
                     writes=[bB[2], bB[3]])
                P.do("vector", lambda ns, nk=nk: ns.tensor_copy(
                    out=UTb[:, 0:nk, :],
                    in_=psBb[:, 2048:2048 + nk * 128].rearrange(
                        "p (k c) -> p k c", k=nk)),
                    reads=[bB[2], bB[3]], writes=[bUT])
                def pvk(ns, nk=nk, k0=k0):
                    inst = None
                    for j in range(nk):
                        inst = ns.matmul(psB[:, 0:64], UTb[:, j, :],
                                         vtok[:, k0 + j, :],
                                         start=(j == 0), stop=(j == nk - 1))
                    return inst
                P.do("tensor", pvk, reads=[bUT, bvt], writes=[bB[0]])
                P.do("vector", lambda ns: ns.reciprocal(
                    out=scol[:, 3:4], in_=scol[:, 2:3]),
                    reads=[bscol], writes=[bscol])
                P.do("scalar", lambda ns: ns.activation(
                    out=obuf, in_=psB[:, 0:64], func=AF.Identity,
                    scale=scol[:, 3:4]), reads=[bB[0], bscol], writes=[bob])
                base = (h % 2) * 64
                P.do("tensor", lambda ns: ns.transpose(
                    psBb[0:64, 1024:1152], obuf, ctx.eye_bf[:]),
                    reads=[bob, ctx.beye_bf], writes=[bB[1]])
                P.do("vector", lambda ns, h=h, qc=qc, base=base: ns.tensor_copy(
                    out=oTi[base:base + 64, (h % 4) // 2, qc * 128:(qc + 1) * 128],
                    in_=psBb[0:64, 1024:1152]),
                    reads=[bB[1]], writes=[boTi])
        emit_l1_wo_partial(ctx, il, 1, locals())
        regen_xinnerT()
        emit_l1_inner_mlp(ctx, il, locals())
        if il == 0:
            regen_xinnerT()

    bx2 = Buf("x2")
    fork_arena(ctx, "x", bx2)
    P.dma("sync", "l1xrld", lambda ns: ns.dma_start(
        out=ctx.x[:], in_=inp["x_spill"]), writes=[bx2])
    ctx.bx = bx2
    for tt in range(NT):
        P.do("vector", lambda ns, tt=tt: ns.tensor_tensor(
            out=ctx.x[:, tt, :], in0=xi[:, tt, :], in1=ctx.x[:, tt, :],
            op=ALU.add), reads=[bxi, bx2], writes=[bx2])


def emit_l1_wo_partial(ctx, il, grp, env):
    P = ctx.P
    inp = ctx.inp
    psA = ctx.psA
    bA = ctx.psA_b
    xi, bxi = ctx.xi, ctx.bxi
    oTi, boTi = env["oTi"], env["boTi"]
    wost, bwos2 = env["wost"], env["bwos2"]
    for it in range(2):
        P.dma("gpsimd", "l1wo", lambda ns, it=it, grp=grp, il=il: ns.dma_start(
            out=wost[:, it, :],
            in_=inp["L1_ffn_wo"][il][grp * 256 + it * 128:grp * 256 + (it + 1) * 128, :],
            max_dma_last_dim=4096), writes=[bwos2])
    for tt in range(NT):
        off = 0 if tt % 2 == 0 else 1024
        bps = bA[0:2] if tt % 2 == 0 else bA[2:4]
        def dmk2(ns, tt=tt, off=off):
            inst = None
            for it in range(2):
                inst = mm_chunks(ns, lambda o, w: psA[:, off + o:off + o + w],
                                 oTi[:, it, tt * 128:(tt + 1) * 128],
                                 lambda o, w: wost[:, it, o:o + w], D,
                                 start=(it == 0), stop=(it == 1))
            return inst
        P.do("tensor", dmk2, reads=[boTi, bwos2], writes=bps)
        P.do("vector", lambda ns, tt=tt, off=off: ns.tensor_tensor(
            out=xi[:, tt, :], in0=psA[:, off:off + 1024], in1=xi[:, tt, :],
            op=ALU.add), reads=bps + [bxi], writes=[bxi])


def emit_l1_inner_mlp(ctx, il, env):
    nc, P, st = ctx.nc, ctx.P, ctx.st
    inp = ctx.inp
    psA, psB = ctx.psA, ctx.psB
    psAb = psA[:].bitcast(BF16)
    bA, bB = ctx.psA_b, ctx.psB_b
    xi, bxi = ctx.xi, ctx.bxi
    xnT, bxnT = ctx.xnT, ctx.bxnT
    NBLK = HIDP // 128
    shm, bshm = env["shm"], env["bshm"]
    lnab, blnab = env["lnab"], env["blnab"]
    gbb, bgbb = env["gbb"], env["bgbb"]
    b1T, bb1T = env["b1T"], env["bb1T"]
    b1stg, bb1s = env["b1stg"], env["bb1s"]
    grow, bgrow = env["grow"], env["bgrow"]
    b2row, bb2r = env["b2row"], env["bb2r"]
    nrm, bnrm = env["nrm"], env["bnrm"]
    xbf = ctx.x[:].rearrange("p a b -> p (a b)").bitcast(BF16)
    hnT = xbf[:, 0:4096].rearrange("p (dt t) -> p dt t", dt=8)
    hpT = xbf[:, 4096:15360].rearrange("p (bk t) -> p bk t", bk=NBLK)
    w1st = [xbf[:, 15360 + i * 2048:15360 + (i + 1) * 2048].rearrange(
        "p (dt c) -> p dt c", dt=8) for i in range(2)]
    w2st = [xbf[:, 19456 + i * 1024:19456 + (i + 1) * 1024] for i in range(2)]
    hntile = xbf[:, 21504:23552]
    gelus = xbf[:, 23552:24064]
    bhn, bhp, bhnt, bgls = Buf("hnT"), Buf("hpT"), Buf("hnt"), Buf("gls")
    bw1 = [Buf("w1s0"), Buf("w1s1")]
    bw2 = [Buf("w2s0"), Buf("w2s1")]
    fork_arena(ctx, "x", bhn, bhp, bhnt, bgls, bw1[0], bw1[1], bw2[0], bw2[1])

    for gi, gnm in [(0, "L1_ffn_ln_g"), (1, "L1_ffn_ln_b")]:
        P.dma("gpsimd", "l1grow", lambda ns, gi=gi, gnm=gnm: ns.dma_start(
            out=grow[:, gi, :], in_=inp[gnm][il].unsqueeze(0)), writes=[bgrow])
        def gmk(ns, gi=gi):
            return mm_chunks(ns, lambda o, w: psA[:, o:o + w],
                             ctx.ones1[:, 0:128],
                             lambda o, w: grow[:, gi, o:o + w], D,
                             start=True, stop=True)
        P.do("tensor", gmk, reads=[bgrow, ctx.bones1], writes=bA[0:2])
        P.do("vector", lambda ns, gi=gi: ns.tensor_copy(
            out=gbb[:, gi, :], in_=psA[:, 0:1024]),
            reads=bA[0:2], writes=[bgbb])
    P.dma("sync", "l1b1s", lambda ns: ns.dma_start(
        out=b1stg[:], in_=inp["L1b1p"][il].rearrange("(a p) -> a p", p=128)),
        writes=[bb1s])
    P.do("tensor", lambda ns: ns.transpose(
        psA[:, 1536:1580], b1stg[:], ctx.eye_f32[0:44, 0:44]),
        reads=[bb1s, ctx.beye_f32], writes=[bA[3]])
    P.do("vector", lambda ns: ns.tensor_copy(
        out=b1T[:], in_=psA[:, 1536:1580]), reads=[bA[3]], writes=[bb1T])
    P.dma("gpsimd", "l1b2r", lambda ns: ns.dma_start(
        out=b2row[:], in_=inp["L1_ffn_b2"][il].unsqueeze(0)), writes=[bb2r])

    for tt in range(NT):
        P.do("vector", lambda ns, tt=tt: ns.tensor_reduce(
            out=lnab[:, 0, tt:tt + 1], in_=xi[:, tt, 0:512], axis=AX.X,
            op=ALU.add), reads=[bxi], writes=[blnab])
        P.do("vector", lambda ns, tt=tt: ns.tensor_reduce(
            out=lnab[:, 1, tt:tt + 1], in_=xi[:, tt, 512:1024], axis=AX.X,
            op=ALU.add), reads=[bxi], writes=[blnab])
        bank = 2 + (tt % 2)
        P.do("scalar", lambda ns, tt=tt, bank=bank: ns.activation(
            out=psB[:, bank * 512:bank * 512 + 512], in_=xi[:, tt, 0:512],
            func=AF.Square, accum_out=lnab[:, 2, tt:tt + 1]),
            reads=[bxi], writes=[bB[bank], blnab])
        P.do("scalar", lambda ns, tt=tt, bank=bank: ns.activation(
            out=psB[:, bank * 512:bank * 512 + 512], in_=xi[:, tt, 512:1024],
            func=AF.Square, accum_out=lnab[:, 3, tt:tt + 1]),
            reads=[bxi], writes=[bB[bank], blnab])
    for q_i, src_slot in [(0, 1), (1, 3)]:
        for tt in range(NT):
            def shk(ns, tt=tt, src_slot=src_slot, q_i=q_i):
                inst = ns.matmul(psA[:, 1792 + q_i:1793 + q_i],
                                 shm[:, 0, :], lnab[:, src_slot, tt:tt + 1],
                                 start=True, stop=(tt == 0))
                if tt > 0:
                    inst = ns.matmul(psA[:, 1792 + q_i:1793 + q_i],
                                     shm[:, 1, :], lnab[:, src_slot, tt - 1:tt],
                                     start=False, stop=True)
                return inst
            P.do("tensor", shk, reads=[bshm, blnab], writes=[bA[3]])
            P.do("vector", lambda ns, tt=tt, q_i=q_i: ns.tensor_copy(
                out=nrm[:, q_i, tt:tt + 1], in_=psA[:, 1792 + q_i:1793 + q_i]),
                reads=[bA[3]], writes=[bnrm])
    P.do("vector", lambda ns: ns.tensor_tensor(
        out=lnab[:, 0, :], in0=lnab[:, 0, :], in1=nrm[:, 0, :], op=ALU.add),
        reads=[blnab, bnrm], writes=[blnab])
    P.do("vector", lambda ns: ns.tensor_tensor(
        out=lnab[:, 2, :], in0=lnab[:, 2, :], in1=nrm[:, 1, :], op=ALU.add),
        reads=[blnab, bnrm], writes=[blnab])
    P.do("vector", lambda ns: ns.tensor_scalar(
        out=lnab[:, 0, :], in0=lnab[:, 0, :], scalar1=1.0 / D, scalar2=None,
        op0=ALU.mult), reads=[blnab], writes=[blnab])
    P.do("vector", lambda ns: ns.scalar_tensor_tensor(
        out=lnab[:, 1, :], in0=lnab[:, 0, :], scalar=-1.0, in1=lnab[:, 0, :],
        op0=ALU.mult, op1=ALU.mult), reads=[blnab], writes=[blnab])
    P.do("vector", lambda ns: ns.scalar_tensor_tensor(
        out=lnab[:, 1, :], in0=lnab[:, 2, :], scalar=1.0 / D, in1=lnab[:, 1, :],
        op0=ALU.mult, op1=ALU.add), reads=[blnab], writes=[blnab])
    P.do("vector", lambda ns: ns.tensor_scalar(
        out=lnab[:, 1, :], in0=lnab[:, 1, :], scalar1=1.0e-5, scalar2=None,
        op0=ALU.add), reads=[blnab], writes=[blnab])
    P.do("scalar", lambda ns: ns.activation(
        out=lnab[:, 1, :], in_=lnab[:, 1, :], func=AF.Sqrt),
        reads=[blnab], writes=[blnab])
    P.do("vector", lambda ns: ns.reciprocal(
        out=lnab[:, 1, :], in_=lnab[:, 1, :]), reads=[blnab], writes=[blnab])
    P.do("vector", lambda ns: ns.scalar_tensor_tensor(
        out=lnab[:, 0, :], in0=lnab[:, 0, :], scalar=-1.0, in1=lnab[:, 1, :],
        op0=ALU.mult, op1=ALU.mult), reads=[blnab], writes=[blnab])

    def build_hn_tile(tt):
        def shtile(ns, tt=tt):
            inst = None
            inst = ns.matmul(psB[:, 0:512], shm[:, 0, :], xi[:, tt, 512:1024],
                             start=True, stop=(tt == 0))
            if tt > 0:
                inst = ns.matmul(psB[:, 0:512], shm[:, 1, :],
                                 xi[:, tt - 1, 512:1024], start=False, stop=True)
            return inst
        P.do("tensor", shtile, reads=[bshm, bxi], writes=[bB[0]])
        P.do("vector", lambda ns, tt=tt: ns.tensor_scalar(
            out=hntile[:, 0:1024], in0=xi[:, tt, 0:512],
            scalar1=lnab[:, 1, tt:tt + 1], scalar2=lnab[:, 0, tt:tt + 1],
            op0=ALU.mult, op1=ALU.add) if False else ns.tensor_scalar(
            out=hntile[:, 0:512], in0=xi[:, tt, 0:512],
            scalar1=lnab[:, 1, tt:tt + 1], scalar2=lnab[:, 0, tt:tt + 1],
            op0=ALU.mult, op1=ALU.add),
            reads=[bxi, blnab], writes=[bhnt])
        P.do("vector", lambda ns, tt=tt: ns.tensor_scalar(
            out=hntile[:, 512:1024], in0=psB[:, 0:512],
            scalar1=lnab[:, 1, tt:tt + 1], scalar2=lnab[:, 0, tt:tt + 1],
            op0=ALU.mult, op1=ALU.add),
            reads=[bB[0], blnab], writes=[bhnt])
        P.do("vector", lambda ns: ns.tensor_tensor(
            out=hntile[:, 0:1024], in0=hntile[:, 0:1024], in1=gbb[:, 0, :],
            op=ALU.mult), reads=[bhnt, bgbb], writes=[bhnt])
        P.do("vector", lambda ns: ns.tensor_tensor(
            out=hntile[:, 0:1024], in0=hntile[:, 0:1024], in1=gbb[:, 1, :],
            op=ALU.add), reads=[bhnt, bgbb], writes=[bhnt])
        b01 = (tt % 2) * 2
        def hnt_tk(ns, tt=tt, b01=b01):
            inst = None
            for dt in range(8):
                inst = ns.transpose(
                    psAb[:, b01 * 1024 + dt * 128:b01 * 1024 + (dt + 1) * 128],
                    hntile[:, dt * 128:(dt + 1) * 128], ctx.eye_bf[:])
            return inst
        P.do("tensor", hnt_tk, reads=[bhnt, ctx.beye_bf],
             writes=[bA[b01], bA[b01 + 1]])
        P.do("vector", lambda ns, tt=tt, b01=b01: ns.tensor_copy(
            out=hnT[:, :, (tt % 4) * 128:(tt % 4 + 1) * 128],
            in_=psAb[:, b01 * 1024:b01 * 1024 + 1024].rearrange(
                "p (dt c) -> p dt c", dt=8)),
            reads=[bA[b01], bA[b01 + 1]], writes=[bhn])

    for tc in range(4):
        for tt in range(tc * 4, tc * 4 + 4):
            build_hn_tile(tt)
        for blk in range(NBLK):
            wb, bwb = w1st[blk % 2], bw1[blk % 2]
            P.dma("gpsimd", f"l1w1{blk % 2}", lambda ns, blk=blk, wb=wb, il=il: ns.dma_start(
                out=wb, in_=inp["L1w1p"][il][:, blk * 256:(blk + 1) * 256].rearrange(
                    "(dt p) c -> p dt c", p=128), max_dma_last_dim=1024),
                writes=[bwb])
            r = (blk % 2) * 2
            for half in range(2):
                def hk(ns, wb=wb, half=half, tc=tc, r=r):
                    inst = None
                    for dt in range(8):
                        inst = ns.matmul(
                            psA[:, (r + half) * 512:(r + half + 1) * 512],
                            wb[:, dt, half * 128:(half + 1) * 128],
                            hnT[:, dt, 0:512],
                            start=(dt == 0), stop=(dt == 7))
                    return inst
                P.do("tensor", hk, reads=[bwb, bhn], writes=[bA[r + half]])
            P.do("scalar", lambda ns, blk=blk, r=r: ns.activation(
                out=hpT[:, blk, :], in_=psA[:, r * 512:r * 512 + 512],
                func=AF.Identity, bias=b1T[:, 2 * blk:2 * blk + 1]),
                reads=[bA[r], bb1T], writes=[bhp])
            P.do("scalar", lambda ns, blk=blk, r=r: ns.activation(
                out=gelus, in_=psA[:, (r + 1) * 512:(r + 2) * 512],
                func=AF.Gelu, bias=b1T[:, 2 * blk + 1:2 * blk + 2]),
                reads=[bA[r + 1], bb1T], writes=[bgls])
            P.do("vector", lambda ns, blk=blk: ns.tensor_tensor(
                out=hpT[:, blk, :], in0=hpT[:, blk, :], in1=gelus,
                op=ALU.mult), reads=[bhp, bgls], writes=[bhp])
        for blk in range(NBLK):
            wb2, bwb2 = w2st[blk % 2], bw2[blk % 2]
            P.dma("gpsimd", f"l1w2{blk % 2}", lambda ns, blk=blk, wb2=wb2, il=il: ns.dma_start(
                out=wb2, in_=inp["L1w2p"][il][blk * 128:(blk + 1) * 128, :],
                max_dma_last_dim=4096), writes=[bwb2])
            def yk2(ns, blk=blk, wb2=wb2):
                inst = None
                for tt in range(4):
                    inst = ns.matmul(psA[:, tt * 512:(tt + 1) * 512],
                                     hpT[:, blk, tt * 128:(tt + 1) * 128],
                                     wb2[:, 0:512],
                                     start=(blk == 0), stop=False)
                    inst = ns.matmul(psB[:, tt * 512:(tt + 1) * 512],
                                     hpT[:, blk, tt * 128:(tt + 1) * 128],
                                     wb2[:, 512:1024],
                                     start=(blk == 0), stop=False)
                return inst
            P.do("tensor", yk2, reads=[bwb2, bhp], writes=bA + bB)
        def bk2(ns):
            inst = None
            for tt in range(4):
                inst = ns.matmul(psA[:, tt * 512:(tt + 1) * 512],
                                 ctx.ones1[:, 0:128], b2row[:, 0:512],
                                 start=False, stop=True)
                inst = ns.matmul(psB[:, tt * 512:(tt + 1) * 512],
                                 ctx.ones1[:, 0:128], b2row[:, 512:1024],
                                 start=False, stop=True)
            return inst
        P.do("tensor", bk2, reads=[bb2r, ctx.bones1], writes=bA + bB)
        for tt in range(4):
            t = tc * 4 + tt
            for dh, ps, bpp in [(0, psA, bA), (1, psB, bB)]:
                P.do("vector", lambda ns, t=t, tt=tt, dh=dh, ps=ps: ns.tensor_tensor(
                    out=xi[:, t, dh * 512:(dh + 1) * 512],
                    in0=ps[:, tt * 512:(tt + 1) * 512],
                    in1=xi[:, t, dh * 512:(dh + 1) * 512], op=ALU.add),
                    reads=[bpp[tt], bxi], writes=[bxi])




def emit_l2_attn(ctx):
    nc, P, st = ctx.nc, ctx.P, ctx.st
    xnT, bxnT = ctx.xnT, ctx.bxnT
    x, bx = ctx.x, ctx.bx
    inp = ctx.inp
    psA, psB = ctx.psA, ctx.psB
    bA, bB = ctx.psA_b, ctx.psB_b
    psAb = psA[:].bitcast(BF16)

    xif = ctx.xi[:].rearrange("p a b -> p (a b)")
    xib = xif.bitcast(BF16)
    qh = xib[0:64, 0:4096].rearrange("p (a t) -> p a t", a=2)[:, 0, :]
    kh = xib[0:64, 4096:8192].rearrange("p (a t) -> p a t", a=2)[:, 0, :]
    sq = xib[0:64, 8192:12288].rearrange("p (a t) -> p a t", a=2)[:, 0, :]
    Ut = [xib[:, 12288:14336], xib[:, 14336:16384]]
    Osb = xib[0:65, 20480:22528]
    Vext = xib[:, 22528:23568].rearrange("p (t c) -> p t c", t=16)
    rowv = xif[0:1, 11840:15936].rearrange("p (a t) -> p a t", a=2)
    bqh, bkh, bsq, bO, bV, brow = (Buf("qh"), Buf("kh"), Buf("sq"),
                                   Buf("Osb"), Buf("Vx"), Buf("rowv"))
    bUt = [Buf("U0"), Buf("U1")]
    fork_arena(ctx, "xi", bqh, bkh, bsq, bUt[0], bUt[1], bO, bV, brow)

    wst = st.enter_context(nc.sbuf_tensor("l2wst", [64, 1024], BF16))
    bwst = Buf("l2wst")
    wT = st.enter_context(nc.sbuf_tensor("l2wT", [128, 3, 8, 64], BF16))
    bwT = Buf("l2wT")
    gcol = st.enter_context(nc.sbuf_tensor("l2g", [128, 4], F32))
    bg = Buf("l2g")
    gst = st.enter_context(nc.sbuf_tensor("l2gs", [1, 1], F32))
    bgst = Buf("l2gs")

    P.dma("sync", "l2gs", lambda ns: ns.dma_start(
        out=gst[:], in_=inp["L2_attn_g_scale"].unsqueeze(0)),
        writes=[bgst])
    P.do("tensor", lambda ns: ns.matmul(
        psA[:, 0:1], ctx.onesr_f[:], gst[:], start=True, stop=True),
        reads=[bgst, ctx.bonesr_f], writes=[bA[0]])
    P.do("vector", lambda ns: ns.tensor_copy(out=gcol[:, 0:1], in_=psA[:, 0:1]),
        reads=[bA[0]], writes=[bg])
    P.do("scalar", lambda ns: ns.activation(
        out=gcol[:, 1:2], in_=gcol[:, 0:1], func=AF.Abs),
        reads=[bg], writes=[bg])
    P.do("vector", lambda ns: ns.tensor_scalar(
        out=gcol[:, 1:2], in0=gcol[:, 1:2], scalar1=-1.0, scalar2=None,
        op0=ALU.mult), reads=[bg], writes=[bg])
    P.do("gpsimd", lambda ns: ns.memset(Vext[:, :, 64:65], 1.0), writes=[bV])

    for h in range(16):
        for wi, wname in enumerate(["L2_attn_Wq", "L2_attn_Wk", "L2_attn_Wv"]):
            P.dma("gpsimd", "l2wst", lambda ns, wname=wname, h=h: ns.dma_start(
                out=wst[:], in_=inp[wname][h * 64:(h + 1) * 64, :],
                max_dma_last_dim=4096), writes=[bwst])
            def trk(ns):
                inst = None
                for et in range(8):
                    inst = ns.transpose(
                        psAb[:, 1024 + et * 64:1024 + (et + 1) * 64],
                        wst[:, et * 128:(et + 1) * 128], ctx.eye_bf[0:64, 0:64])
                return inst
            P.do("tensor", trk, reads=[bwst, ctx.beye_bf], writes=[bA[0], bA[1]])
            P.do("vector", lambda ns, wi=wi: ns.tensor_copy(
                out=wT[:, wi, :, :], in_=psAb[:, 1024:1536].rearrange(
                    "p (et d) -> p et d", et=8)),
                reads=[bA[0], bA[1]], writes=[bwT])
        for wi, dst, bdst in [(0, qh, bqh), (1, kh, bkh)]:
            def pmk(ns, wi=wi):
                inst = None
                for et in range(8):
                    inst = mm_chunks(ns, lambda o, w: psB[0:64, o:o + w],
                                     wT[:, wi, et, :],
                                     lambda o, w: xnT[:, et, o:o + w], T_TOK,
                                     start=(et == 0), stop=(et == 7))
                return inst
            P.do("tensor", pmk, reads=[bwT, bxnT], writes=bB)
            P.do("scalar", lambda ns, dst=dst: ns.activation(
                out=dst, in_=psB[0:64, 0:T_TOK], func=AF.Identity),
                reads=bB, writes=[bdst])
            P.do("scalar", lambda ns, dst=dst: ns.activation(
                out=sq, in_=dst, func=AF.Square),
                reads=[bdst], writes=[bsq])
            def ssk(ns):
                return mm_chunks(ns, lambda o, w: psA[0:1, o:o + w],
                                 ctx.onesc_b[0:64, :],
                                 lambda o, w: sq[:, o:o + w], T_TOK,
                                 start=True, stop=True)
            P.do("tensor", ssk, reads=[bsq, ctx.bonesc_b], writes=bA)
            P.do("scalar", lambda ns, wi=wi: ns.activation(
                out=rowv[:, wi, :], in_=psA[0:1, 0:T_TOK], func=AF.Sqrt),
                reads=bA, writes=[brow])
            P.do("vector", lambda ns, wi=wi: ns.tensor_scalar(
                out=rowv[:, wi, :], in0=rowv[:, wi, :], scalar1=1.0e-12,
                scalar2=None, op0=ALU.max), reads=[brow], writes=[brow])
            P.do("vector", lambda ns, wi=wi: ns.reciprocal(
                out=rowv[:, wi, :], in_=rowv[:, wi, :]), reads=[brow], writes=[brow])
            def bck(ns, wi=wi):
                return mm_chunks(ns, lambda o, w: psA[0:64, o:o + w],
                                 ctx.onesr_f[:, 0:64],
                                 lambda o, w: rowv[:, wi, o:o + w], T_TOK,
                                 start=True, stop=True)
            P.do("tensor", bck, reads=[brow, ctx.bonesr_f], writes=bA)
            P.do("vector", lambda ns, dst=dst: ns.tensor_tensor(
                out=dst, in0=dst, in1=psA[0:64, 0:T_TOK], op=ALU.mult),
                reads=bA + [bdst], writes=[bdst])
        def vmk(ns):
            inst = None
            for et in range(8):
                inst = mm_chunks(ns, lambda o, w: psB[0:64, o:o + w],
                                 wT[:, 2, et, :],
                                 lambda o, w: xnT[:, et, o:o + w], T_TOK,
                                 start=(et == 0), stop=(et == 7))
            return inst
        P.do("tensor", vmk, reads=[bwT, bxnT], writes=bB)
        P.do("scalar", lambda ns: ns.activation(
            out=sq, in_=psB[0:64, 0:T_TOK], func=AF.Identity),
            reads=bB, writes=[bsq])
        def vtk(ns):
            inst = None
            for tt in range(16):
                inst = ns.transpose(psAb[:, tt * 64:(tt + 1) * 64],
                                    sq[:, tt * 128:(tt + 1) * 128],
                                    ctx.eye_bf[0:64, 0:64])
            return inst
        P.do("tensor", vtk, reads=[bsq, ctx.beye_bf], writes=[bA[0], bA[1]])
        P.do("vector", lambda ns: ns.tensor_copy(
            out=Vext[:, :, 0:64], in_=psAb[:, 0:1024].rearrange(
                "p (tt d) -> p tt d", tt=16)),
            reads=[bA[0], bA[1]], writes=[bV])
        for kt in range(16):
            ub, bub = Ut[kt % 2], bUt[kt % 2]
            def smk(ns, kt=kt):
                return mm_chunks(ns, lambda o, w: psA[:, o:o + w],
                                 kh[:, kt * 128:(kt + 1) * 128],
                                 lambda o, w: qh[:, o:o + w], T_TOK,
                                 start=True, stop=True)
            P.do("tensor", smk, reads=[bkh, bqh], writes=bA)
            P.do("scalar", lambda ns, ub=ub: ns.activation(
                out=ub, in_=psA[:, 0:T_TOK], func=AF.Exp,
                bias=gcol[:, 1:2], scale=gcol[:, 0:1]),
                reads=bA + [bg], writes=[bub])
            def pvk(ns, kt=kt, ub=ub):
                return mm_chunks(ns, lambda o, w: psB[0:65, o:o + w],
                                 Vext[:, kt, :],
                                 lambda o, w: ub[:, o:o + w], T_TOK,
                                 start=(kt == 0), stop=(kt == 15))
            P.do("tensor", pvk, reads=[bV, bub], writes=bB)
        P.do("scalar", lambda ns: ns.activation(
            out=Osb, in_=psB[0:65, 0:T_TOK], func=AF.Identity),
            reads=bB, writes=[bO])
        def otk(ns):
            inst = None
            for qt in range(16):
                inst = ns.transpose(psAb[:, qt * 128:qt * 128 + 65],
                                    Osb[:, qt * 128:(qt + 1) * 128],
                                    ctx.eye_bf[0:65, 0:65])
            return inst
        P.do("tensor", otk, reads=[bO, ctx.beye_bf], writes=bA)
        for qt in range(16):
            P.do("vector", lambda ns, qt=qt: ns.reciprocal(
                out=gcol[:, 2:3], in_=psAb[:, qt * 128 + 64:qt * 128 + 65]),
                reads=bA, writes=[bg])
            P.do("vector", lambda ns, qt=qt, h=h: ns.scalar_tensor_tensor(
                out=x[:, qt, h * 64:(h + 1) * 64],
                in0=psAb[:, qt * 128:qt * 128 + 64],
                scalar=gcol[:, 2:3], in1=x[:, qt, h * 64:(h + 1) * 64],
                op0=ALU.mult, op1=ALU.add),
                reads=bA + [bg, bx], writes=[bx])


def emit_l2_ffn(ctx):
    nc, P, st = ctx.nc, ctx.P, ctx.st
    xnT, bxnT = ctx.xnT, ctx.bxnT
    x, bx = ctx.x, ctx.bx
    inp = ctx.inp
    psA, psB = ctx.psA, ctx.psB
    bA, bB = ctx.psA_b, ctx.psB_b
    n_exp = 16

    xif = ctx.xi[:].rearrange("p a b -> p (a b)")
    xib = xif.bitcast(BF16)
    hT2 = xib[:, 0:8192].rearrange("p (f t) -> p f t", f=4)
    wup = xib[:, 8192:12288].rearrange("p (dt f) -> p dt f", dt=8)
    wga = xib[:, 12288:16384].rearrange("p (dt f) -> p dt f", dt=8)
    wdn = xib[:, 16384:20480].rearrange("p (f d) -> p f d", f=4)
    rwTv = xib[:, 20480:20608].rearrange("p (dt e) -> p dt e", dt=8)
    rbias = xib[0:1, 20608:20624]
    rstage = xib[0:16, 20640:21664]
    wgt = xif[:, 10944:10944 + NT * 16].rearrange("p (t e) -> p t e", t=NT)
    gsc = xif[:, 11200:11200 + 6 * 16].rearrange("p (a e) -> p a e", a=6)
    bh, brw, brb, brs, bwgt, bgsc = (Buf("h2"), Buf("rw"), Buf("rb"),
                                     Buf("rs"), Buf("wgt2"), Buf("gsc2"))
    bup, bga2, bdn = Buf("up"), Buf("ga"), Buf("dn")
    fork_arena(ctx, "xi", bh, brw, brb, brs, bwgt, bgsc, bup, bga2, bdn)

    P.dma("gpsimd", "l2rs", lambda ns: ns.dma_start(
        out=rstage, in_=inp["L2_ffn_router_w"], max_dma_last_dim=4096),
        writes=[brs])
    P.dma("gpsimd", "l2rb", lambda ns: ns.dma_start(
        out=rbias, in_=inp["L2_ffn_router_bias"].unsqueeze(0)), writes=[brb])
    psAb = psA[:].bitcast(BF16)
    def rtk(ns):
        inst = None
        for et in range(8):
            inst = ns.transpose(psAb[:, et * 16:(et + 1) * 16],
                                rstage[:, et * 128:(et + 1) * 128],
                                ctx.eye_bf[0:16, 0:16])
        return inst
    P.do("tensor", rtk, reads=[brs, ctx.beye_bf], writes=[bA[0]])
    P.do("vector", lambda ns: ns.tensor_copy(
        out=rwTv[:], in_=psAb[:, 0:128].rearrange("p (dt e) -> p dt e", dt=8)),
        reads=[bA[0]], writes=[brw])

    for t in range(NT):
        def gmk(ns, t=t):
            inst = None
            for dt in range(ND):
                inst = ns.matmul(psA[:, 0:n_exp], xnT[:, dt, t * 128:(t + 1) * 128],
                                 rwTv[:, dt, :], start=(dt == 0), stop=False)
            inst = ns.matmul(psA[:, 0:n_exp], ctx.ones1[:, 0:128], rbias,
                             start=False, stop=True)
            return inst
        P.do("tensor", gmk, reads=[bxnT, brw, brb, ctx.bones1], writes=[bA[0]])
        P.do("vector", lambda ns, t=t: ns.tensor_scalar(
            out=gsc[:, 0, :], in0=psA[:, 0:n_exp], scalar1=1.0, scalar2=None,
            op0=ALU.mult), reads=[bA[0]], writes=[bgsc])
        P.do("vector", lambda ns: ns.tensor_reduce(
            out=gsc[:, 1, 0:1], in_=gsc[:, 0, :], axis=AX.X, op=ALU.max,
            negate=True), reads=[bgsc], writes=[bgsc])
        P.do("vector", lambda ns: ns.tensor_scalar(
            out=gsc[:, 2, :], in0=gsc[:, 0, :], scalar1=gsc[:, 1, 0:1],
            scalar2=None, op0=ALU.add), reads=[bgsc], writes=[bgsc])
        P.do("vector", lambda ns: ns.tensor_scalar(
            out=gsc[:, 3, :], in0=gsc[:, 2, :], scalar1=0.0, scalar2=-1e30,
            op0=ALU.is_equal, op1=ALU.mult), reads=[bgsc], writes=[bgsc])
        P.do("vector", lambda ns: ns.tensor_tensor(
            out=gsc[:, 3, :], in0=gsc[:, 2, :], in1=gsc[:, 3, :], op=ALU.add),
            reads=[bgsc], writes=[bgsc])
        P.do("vector", lambda ns: ns.tensor_reduce(
            out=gsc[:, 1, 1:2], in_=gsc[:, 3, :], axis=AX.X, op=ALU.max),
            reads=[bgsc], writes=[bgsc])
        P.do("scalar", lambda ns: ns.activation(
            out=gsc[:, 4, :], in_=gsc[:, 2, :], func=AF.Exp),
            reads=[bgsc], writes=[bgsc])
        P.do("vector", lambda ns: ns.tensor_scalar(
            out=gsc[:, 5, :], in0=gsc[:, 2, :], scalar1=gsc[:, 1, 1:2],
            scalar2=None, op0=ALU.is_ge), reads=[bgsc], writes=[bgsc])
        P.do("vector", lambda ns: ns.tensor_tensor(
            out=gsc[:, 4, :], in0=gsc[:, 4, :], in1=gsc[:, 5, :], op=ALU.mult),
            reads=[bgsc], writes=[bgsc])
        P.do("vector", lambda ns: ns.tensor_reduce(
            out=gsc[:, 1, 2:3], in_=gsc[:, 4, :], axis=AX.X, op=ALU.add),
            reads=[bgsc], writes=[bgsc])
        P.do("vector", lambda ns: ns.reciprocal(
            out=gsc[:, 1, 3:4], in_=gsc[:, 1, 2:3]), reads=[bgsc], writes=[bgsc])
        P.do("vector", lambda ns, t=t: ns.tensor_scalar(
            out=wgt[:, t, :], in0=gsc[:, 4, :], scalar1=gsc[:, 1, 3:4],
            scalar2=None, op0=ALU.mult), reads=[bgsc], writes=[bwgt])

    for e in range(n_exp):
        for wnm, dst, bdst, ld in [("L2_ffn_up_proj", wup, bup, 2048),
                                   ("L2_ffn_gate_proj", wga, bga2, 2048),
                                   ("L2_ffn_down_proj", wdn, bdn, 4096)]:
            P.dma("gpsimd", f"l2w_{wnm}", lambda ns, wnm=wnm, dst=dst, e=e, ld=ld: ns.dma_start(
                out=dst, in_=inp[wnm][e].rearrange("(a p) b -> p a b", p=128),
                max_dma_last_dim=ld), writes=[bdst])
        for f in range(4):
            for wsrc, bsrc, pdst, bpdst in [(wup, bup, psA, bA),
                                            (wga, bga2, psB, bB)]:
                def hk(ns, wsrc=wsrc, f=f, pdst=pdst):
                    inst = None
                    for dt in range(8):
                        inst = mm_chunks(ns, lambda o, w: pdst[:, o:o + w],
                                         wsrc[:, dt, f * 128:(f + 1) * 128],
                                         lambda o, w: xnT[:, dt, o:o + w], T_TOK,
                                         start=(dt == 0), stop=(dt == 7))
                    return inst
                P.do("tensor", hk, reads=[bsrc, bxnT], writes=bpdst)
            P.do("scalar", lambda ns, f=f: ns.activation(
                out=hT2[:, f, :], in_=psB[:, 0:T_TOK], func=AF.Silu),
                reads=bB, writes=[bh])
            P.do("vector", lambda ns, f=f: ns.tensor_tensor(
                out=hT2[:, f, :], in0=hT2[:, f, :], in1=psA[:, 0:T_TOK],
                op=ALU.mult), reads=[bh] + bA, writes=[bh])
        for tq in range(4):
            def yk(ns, tq=tq):
                inst = None
                for f in range(4):
                    for tt in range(4):
                        hs = hT2[:, f, tq * 512 + tt * 128:tq * 512 + (tt + 1) * 128]
                        inst = ns.matmul(psA[:, tt * 512:(tt + 1) * 512], hs,
                                         wdn[:, f, 0:512],
                                         start=(f == 0), stop=(f == 3))
                        inst = ns.matmul(psB[:, tt * 512:(tt + 1) * 512], hs,
                                         wdn[:, f, 512:1024],
                                         start=(f == 0), stop=(f == 3))
                return inst
            P.do("tensor", yk, reads=[bdn, bh], writes=bA + bB)
            for tt in range(4):
                t = tq * 4 + tt
                for dh, ps, bpp in [(0, psA, bA), (1, psB, bB)]:
                    P.do("vector", lambda ns, t=t, tt=tt, dh=dh, e=e, ps=ps: ns.scalar_tensor_tensor(
                        out=x[:, t, dh * 512:(dh + 1) * 512],
                        in0=ps[:, tt * 512:(tt + 1) * 512],
                        scalar=wgt[:, t, e:e + 1],
                        in1=x[:, t, dh * 512:(dh + 1) * 512],
                        op0=ALU.mult, op1=ALU.add),
                        reads=[bpp[tt], bwgt, bx], writes=[bx])

    hsh2 = xib[:, 0:8192].rearrange("p (f t) -> p f t", f=4)
    shgT = xib[:, 8192:16384].rearrange("p (dt d) -> p dt d", dt=8)
    shuT = xib[:, 16384:24576].rearrange("p (dt d) -> p dt d", dt=8)
    hsh = xib[:, 24576:32768].rearrange("p (f t) -> p f t", f=4)
    bh2, bshg, bshu, bhsh = Buf("h2b"), Buf("shg"), Buf("shu"), Buf("hsh")
    fork_arena(ctx, "xi", bh2, bshg, bshu, bhsh)

    def load_and_transpose(wname, stage_ap, bstage, dstT, bdstT):
        P.dma("gpsimd", "l2shA", lambda ns, wname=wname, stage_ap=stage_ap: ns.dma_start(
            out=stage_ap, in_=inp[wname].rearrange("(st p) d -> p st d", p=128),
            max_dma_last_dim=4096), writes=[bstage])
        for s8 in range(8):
            def trk2(ns, s8=s8, stage_ap=stage_ap):
                inst = None
                for dtc in range(8):
                    inst = ns.transpose(
                        psAb[:, dtc * 128:(dtc + 1) * 128],
                        stage_ap[:, s8, dtc * 128:(dtc + 1) * 128],
                        ctx.eye_bf[:])
                return inst
            P.do("tensor", trk2, reads=[bstage, ctx.beye_bf], writes=[bA[0], bA[1]])
            P.do("vector", lambda ns, s8=s8, dstT=dstT: ns.tensor_copy(
                out=dstT[:, :, s8 * 128:(s8 + 1) * 128],
                in_=psAb[:, 0:1024].rearrange("p (dtc d) -> p dtc d", dtc=8)),
                reads=[bA[0], bA[1]], writes=[bdstT])

    load_and_transpose("L2_ffn_sh_gate",
                       xib[:, 24576:32768].rearrange("p (st d) -> p st d", st=8),
                       bhsh, shgT, bshg)
    load_and_transpose("L2_ffn_sh_up",
                       xib[:, 0:8192].rearrange("p (st d) -> p st d", st=8),
                       bh2, shuT, bshu)
    for f in range(8):
        for wsrc, bsrc, pdst, bpdst in [(shuT, bshu, psA, bA), (shgT, bshg, psB, bB)]:
            def hk(ns, wsrc=wsrc, f=f, pdst=pdst):
                inst = None
                for dt in range(8):
                    inst = mm_chunks(ns, lambda o, w: pdst[:, o:o + w],
                                     wsrc[:, dt, f * 128:(f + 1) * 128],
                                     lambda o, w: xnT[:, dt, o:o + w], T_TOK,
                                     start=(dt == 0), stop=(dt == 7))
                return inst
            P.do("tensor", hk, reads=[bsrc, bxnT], writes=bpdst)
        hdst, bhd = (hsh[:, f, :], bhsh) if f < 4 else (hsh2[:, f - 4, :], bh2)
        P.do("scalar", lambda ns, hdst=hdst: ns.activation(
            out=hdst, in_=psB[:, 0:T_TOK], func=AF.Silu), reads=bB, writes=[bhd])
        P.do("vector", lambda ns, hdst=hdst: ns.tensor_tensor(
            out=hdst, in0=hdst, in1=psA[:, 0:T_TOK], op=ALU.mult),
            reads=[bhd] + bA, writes=[bhd])
    load_and_transpose("L2_ffn_sh_down",
                       xib[:, 8192:16384].rearrange("p (st d) -> p st d", st=8),
                       bshg, shuT, bshu)
    shdT = shuT
    for tq in range(4):
        def yk(ns, tq=tq):
            inst = None
            for fr in range(8):
                hsrc = hsh if fr < 4 else hsh2
                f = fr % 4
                for tt in range(4):
                    hs = hsrc[:, f, tq * 512 + tt * 128:tq * 512 + (tt + 1) * 128]
                    inst = ns.matmul(psA[:, tt * 512:(tt + 1) * 512], hs,
                                     shdT[:, fr, 0:512],
                                     start=(fr == 0), stop=(fr == 7))
                    inst = ns.matmul(psB[:, tt * 512:(tt + 1) * 512], hs,
                                     shdT[:, fr, 512:1024],
                                     start=(fr == 0), stop=(fr == 7))
            return inst
        P.do("tensor", yk, reads=[bshu, bhsh, bh2], writes=bA + bB)
        for tt in range(4):
            t = tq * 4 + tt
            for dh, ps, bpp in [(0, psA, bA), (1, psB, bB)]:
                P.do("vector", lambda ns, t=t, tt=tt, dh=dh, ps=ps: ns.tensor_tensor(
                    out=x[:, t, dh * 512:(dh + 1) * 512],
                    in0=ps[:, tt * 512:(tt + 1) * 512],
                    in1=x[:, t, dh * 512:(dh + 1) * 512], op=ALU.add),
                    reads=[bpp[tt], bx], writes=[bx])


def emit_l3_attn(ctx):
    nc, P, st = ctx.nc, ctx.P, ctx.st
    xnT, bxnT = ctx.xnT, ctx.bxnT
    x, bx = ctx.x, ctx.bx
    inp = ctx.inp
    psA, psB = ctx.psA, ctx.psB
    bA, bB = ctx.psA_b, ctx.psB_b
    psBb = psB[:].bitcast(BF16)

    xib = ctx.xi[:].rearrange("p a b -> p (a b)").bitcast(BF16)
    bxi = ctx.bxi
    KmT = xib[:, 0:16384].rearrange("p (dt t) -> p dt t", dt=8)
    Vtm = xib[:, 16384:32768].rearrange("p (t d) -> p t d", t=16)
    bK = Buf("l3K")
    bV = Buf("l3V")
    fork_arena(ctx, "xi", bK, bV)

    Dacc = st.enter_context(nc.sbuf_tensor("l3D", [128, NT, D], BF16))
    bD = Buf("l3D")
    Qbuf = st.enter_context(nc.sbuf_tensor("l3Q", [128, 8, 128], BF16))
    bQ = Buf("l3Q")
    Ubuf = st.enter_context(nc.sbuf_tensor("l3U", [128, T_TOK], BF16))
    bU = Buf("l3U")
    UT = st.enter_context(nc.sbuf_tensor("l3UT", [128, 16, 128], BF16))
    bUT = Buf("l3UT")
    wstage = st.enter_context(nc.sbuf_tensor("l3ws", [128, 8, 128], BF16))
    bws = Buf("l3ws")
    lamst = st.enter_context(nc.sbuf_tensor("l3lam", [16, 4, 64], F32))
    blam = Buf("l3lam")
    cols = st.enter_context(nc.sbuf_tensor("l3col", [128, 8], F32))
    bcol = Buf("l3col")
    onesc, bonec = ctx.onesc_f, ctx.bonesc_f
    onesr, boner = ctx.onesr_f, ctx.bonesr_f

    SCL = float(D) ** -0.25

    for i, (na, nb) in enumerate([("L3_attn_lambda_q1", "L3_attn_lambda_k1"),
                                  ("L3_attn_lambda_q2", "L3_attn_lambda_k2")]):
        P.dma("sync", "l3lam", lambda ns, i=i, na=na: ns.dma_start(
            out=lamst[:, 0, :] if i == 0 else lamst[:, 1, :],
            in_=inp[na].rearrange("(a p) -> a p", p=64)), writes=[blam])
        P.dma("sync", "l3lam", lambda ns, i=i, nb=nb: ns.dma_start(
            out=lamst[:, 2, :], in_=inp[nb].rearrange("(a p) -> a p", p=64)),
            writes=[blam])
        P.do("vector", lambda ns, i=i: ns.tensor_tensor(
            out=lamst[:, 0 if i == 0 else 1, :],
            in0=lamst[:, 0 if i == 0 else 1, :], in1=lamst[:, 2, :],
            op=ALU.mult), reads=[blam], writes=[blam])
        P.do("vector", lambda ns, i=i: ns.tensor_reduce(
            out=lamst[:, 3, i:i + 1], in_=lamst[:, 0 if i == 0 else 1, :],
            axis=AX.X, op=ALU.add), reads=[blam], writes=[blam])
    P.do("tensor", lambda ns: ns.matmul(
        psA[0:1, 0:2], onesc[0:16, :], lamst[:, 3, 0:2],
        start=True, stop=True),
        reads=[blam, bonec], writes=[bA[0]])
    P.do("scalar", lambda ns: ns.activation(
        out=cols[0:1, 0:2], in_=psA[0:1, 0:2], func=AF.Exp),
        reads=[bA[0]], writes=[bcol])
    P.do("vector", lambda ns: ns.tensor_tensor(
        out=cols[0:1, 2:3], in0=cols[0:1, 1:2], in1=cols[0:1, 0:1],
        op=ALU.subtract), reads=[bcol], writes=[bcol])
    P.do("vector", lambda ns: ns.tensor_scalar(
        out=cols[0:1, 2:3], in0=cols[0:1, 2:3], scalar1=0.8, scalar2=None,
        op0=ALU.subtract), reads=[bcol], writes=[bcol])
    P.do("tensor", lambda ns: ns.matmul(
        psA[:, 4:5], onesr[:], cols[0:1, 2:3],
        start=True, stop=True), reads=[bcol, boner], writes=[bA[0]])
    P.do("vector", lambda ns: ns.tensor_copy(out=cols[:, 3:4], in_=psA[:, 4:5]),
        reads=[bA[0]], writes=[bcol])

    wv = Dacc[:].rearrange("p a b -> p (a b)")[:, 0:8192].rearrange(
        "p (dt d) -> p dt d", dt=8)
    bwv = bD
    P.dma("gpsimd", "l3wv", lambda ns: ns.dma_start(
        out=wv[:], in_=inp["L3_attn_w_v"].rearrange("(dt p) d -> p dt d", p=128),
        max_dma_last_dim=8192), writes=[bwv])
    for t in range(NT):
        bps = bA[0:2] if t % 2 == 0 else bA[2:4]
        off = 0 if t % 2 == 0 else 1024
        def vmk(ns, t=t, off=off):
            inst = None
            for dt in range(8):
                inst = mm_chunks(ns, lambda o, w: psA[:, off + o:off + o + w],
                                 xnT[:, dt, t * 128:(t + 1) * 128],
                                 lambda o, w: wv[:, dt, o:o + w], D,
                                 start=(dt == 0), stop=(dt == 7))
            return inst
        P.do("tensor", vmk, reads=[bwv, bxnT], writes=bps)
        P.do("scalar", lambda ns, t=t, off=off: ns.activation(
            out=Vtm[:, t, :], in_=psA[:, off:off + 1024], func=AF.Identity),
            reads=bps, writes=[bV])

    for m in range(2):
        for quarter in range(8):
            P.dma("gpsimd", "l3ws", lambda ns, m=m, quarter=quarter: ns.dma_start(
                out=wstage[:],
                in_=inp["L3_attn_w_k12"][:, m * 1024 + quarter * 128:
                                         m * 1024 + (quarter + 1) * 128].rearrange(
                    "(dt p) e -> p dt e", p=128), max_dma_last_dim=1024),
                writes=[bws])
            et = quarter
            ps, bps = (psA, bA) if et % 2 == 0 else (psB, bB)
            def kmk(ns, ps=ps):
                inst = None
                for dt in range(8):
                    inst = mm_chunks(ns, lambda o, w: ps[:, o:o + w],
                                     wstage[:, dt, :],
                                     lambda o, w: xnT[:, dt, o:o + w], T_TOK,
                                     start=(dt == 0), stop=(dt == 7))
                return inst
            P.do("tensor", kmk, reads=[bws, bxnT], writes=bps)
            P.do("scalar", lambda ns, et=et, ps=ps: ns.activation(
                out=KmT[:, et, :], in_=ps[:], func=AF.Identity),
                reads=bps, writes=[bK])
        dbg_now(ctx, f"l3_xk{m}", x[:], [bx], [128, NT, D])
        for qc in range(16):
            for quarter in range(8):
                P.dma("gpsimd", "l3ws", lambda ns, m=m, quarter=quarter: ns.dma_start(
                    out=wstage[:],
                    in_=inp["L3_attn_w_q12"][:, m * 1024 + quarter * 128:
                                             m * 1024 + (quarter + 1) * 128].rearrange(
                        "(dt p) e -> p dt e", p=128), max_dma_last_dim=1024),
                    writes=[bws])
                et = quarter
                off = 0 if et % 2 == 0 else 512
                bps = bA[0:1] if et % 2 == 0 else bA[1:2]
                def qmk(ns, qc=qc, off=off):
                    inst = None
                    for dt in range(8):
                        inst = ns.matmul(psA[:, off:off + 128],
                                         wstage[:, dt, :],
                                         xnT[:, dt, qc * 128:(qc + 1) * 128],
                                         start=(dt == 0), stop=(dt == 7))
                    return inst
                P.do("tensor", qmk, reads=[bws, bxnT], writes=bps)
                P.do("scalar", lambda ns, et=et, off=off: ns.activation(
                    out=Qbuf[:, et, :], in_=psA[:, off:off + 128],
                    func=AF.Identity), reads=bps, writes=[bQ])
            qt = qc
            def smk(ns):
                inst = None
                for dt in range(8):
                    inst = mm_chunks(ns, lambda o, w: psA[:, o:o + w],
                                     Qbuf[:, dt, :],
                                     lambda o, w: KmT[:, dt, o:o + w], T_TOK,
                                     start=(dt == 0), stop=(dt == 7))
                return inst
            P.do("tensor", smk, reads=[bQ, bK], writes=bA)
            P.do("vector", lambda ns: ns.tensor_reduce(
                out=cols[:, 4:5], in_=psA[:, 0:T_TOK], axis=AX.X, op=ALU.max,
                negate=True), reads=bA, writes=[bcol])
            P.do("vector", lambda ns: ns.tensor_scalar(
                out=cols[:, 5:6], in0=cols[:, 4:5], scalar1=SCL, scalar2=None,
                op0=ALU.mult), reads=[bcol], writes=[bcol])
            P.do("scalar", lambda ns: ns.activation(
                out=Ubuf[:], in_=psA[:, 0:T_TOK], func=AF.Exp,
                bias=cols[:, 5:6], scale=SCL, accum_out=cols[:, 6:7]),
                reads=bA + [bcol], writes=[bU, bcol])
            def tmk(ns):
                inst = None
                for kt in range(16):
                    inst = ns.transpose(psBb[:, kt * 128:(kt + 1) * 128],
                                        Ubuf[:, kt * 128:(kt + 1) * 128],
                                        ctx.eye_bf[:])
                return inst
            P.do("tensor", tmk, reads=[bU, ctx.beye_bf], writes=[bB[0], bB[1]])
            P.do("vector", lambda ns: ns.tensor_copy(
                out=UT[:], in_=psBb[:, 0:2048].rearrange(
                    "p (kt c) -> p kt c", kt=16)),
                reads=[bB[0], bB[1]], writes=[bUT])
            def pvk(ns):
                inst = None
                for kt in range(16):
                    for o in (0, 512):
                        inst = ns.matmul(psA[:, o:o + 512], UT[:, kt, :],
                                         Vtm[:, kt, o:o + 512],
                                         start=(kt == 0), stop=(kt == 15))
                return inst
            P.do("tensor", pvk, reads=[bUT, bV], writes=bA[0:2])
            P.do("vector", lambda ns: ns.reciprocal(
                out=cols[:, 7:8], in_=cols[:, 6:7]), reads=[bcol], writes=[bcol])
            if m == 0:
                P.do("scalar", lambda ns, qt=qt: ns.activation(
                    out=Dacc[:, qt, :], in_=psA[:, 0:1024], func=AF.Identity,
                    scale=cols[:, 7:8]),
                    reads=bA[0:2] + [bcol], writes=[bD])
            else:
                P.do("vector", lambda ns: ns.tensor_tensor(
                    out=cols[:, 7:8], in0=cols[:, 7:8], in1=cols[:, 3:4],
                    op=ALU.mult), reads=[bcol], writes=[bcol])
                P.do("vector", lambda ns, qt=qt: ns.scalar_tensor_tensor(
                    out=Dacc[:, qt, :], in0=psA[:, 0:1024],
                    scalar=cols[:, 7:8], in1=Dacc[:, qt, :],
                    op0=ALU.mult, op1=ALU.add),
                    reads=bA[0:2] + [bcol, bD], writes=[bD])

    ctx.dbg["l3_Vtm"] = (Vtm, [bV], [128, NT, D])
    ctx.dbg["l3_K"] = (KmT, [bK], [128, 8, T_TOK])
    ctx.dbg["l3_Dacc"] = (Dacc[:], [bD], [128, NT, D])
    ctx.dbg["l3_lam"] = (cols[:, 3:4], [bcol], [128, 1])

    dbg_now(ctx, "l3_xpre", x[:], [bx], [128, NT, D])
    msarr = st.enter_context(nc.sbuf_tensor("l3ms", [128, 2, 16], F32))
    bms = Buf("l3ms")
    for t in range(NT):
        P.do("scalar", lambda ns, t=t: ns.activation(
            out=psB[:, (t % 2) * 1024:(t % 2) * 1024 + 1024],
            in_=Dacc[:, t, :], func=AF.Square, accum_out=msarr[:, 0, t:t + 1]),
            reads=[bD], writes=[bB[(t % 2) * 2], bB[(t % 2) * 2 + 1], bms])
    P.do("vector", lambda ns: ns.tensor_scalar(
        out=msarr[:, 1, :], in0=msarr[:, 0, :], scalar1=1.0 / D, scalar2=1.0e-5,
        op0=ALU.mult, op1=ALU.add), reads=[bms], writes=[bms])
    P.do("scalar", lambda ns: ns.activation(
        out=msarr[:, 1, :], in_=msarr[:, 1, :], func=AF.Sqrt),
        reads=[bms], writes=[bms])
    P.do("vector", lambda ns: ns.reciprocal(
        out=msarr[:, 1, :], in_=msarr[:, 1, :]), reads=[bms], writes=[bms])
    P.do("vector", lambda ns: ns.tensor_scalar(
        out=msarr[:, 1, :], in0=msarr[:, 1, :], scalar1=0.2, scalar2=None,
        op0=ALU.mult), reads=[bms], writes=[bms])
    ctx.dbg["l3_ms"] = (msarr[:], [bms], [128, 2, 16])
    for t in range(NT):
        P.do("vector", lambda ns, t=t: ns.scalar_tensor_tensor(
            out=x[:, t, :], in0=Dacc[:, t, :], scalar=msarr[:, 1, t:t + 1],
            in1=x[:, t, :], op0=ALU.mult, op1=ALU.add),
            reads=[bD, bms, bx], writes=[bx])
    ctx.moe_arena = Dacc[:].rearrange("p a b -> p (a b)")
    ctx.moe_arena_buf = bD


def emit_layer(ctx, l):
    emit_outer_ln(ctx)
    if l == 0:
        emit_l0_attn(ctx)
        emit_moe(ctx, {"gate_w": "L0_ffn_gate_w", "w1": "L0_ffn_w1",
                       "b1": "L0_ffn_b1", "w2": "L0_ffn_w2", "b2": "L0_ffn_b2"},
                 AF.Gelu, n_exp=8, has_br=False)
    elif l == 1:
        emit_l1_attn(ctx)
        emit_l1_ffn(ctx)
    elif l == 2:
        emit_l2_attn(ctx)
        emit_l2_ffn(ctx)
    elif l == 3:
        emit_l3_attn(ctx)
        emit_moe(ctx, {"gate_w": "L3_ffn_Wr", "w1": "L3_ffn_W1", "b1": "L3_ffn_b1",
                       "w2": "L3_ffn_W2", "b2": "L3_ffn_b2", "br": "L3_ffn_br"},
                 AF.Relu, n_exp=8, has_br=True)


LAYER_INPUTS = {
    0: ["L0_attn_Wq", "L0_attn_bq", "L0_attn_Wk", "L0_attn_bk", "L0_attn_Wv",
        "L0_attn_bv", "L0_ffn_gate_w", "L0_ffn_w1", "L0_ffn_b1", "L0_ffn_w2",
        "L0_ffn_b2"],
    1: ["L1_attn_Wq", "L1_attn_bq", "L1_attn_Wk", "L1_attn_bk", "L1_attn_Wv",
        "L1_attn_bv", "L1_attn_Wo", "L1_attn_bo", "L1_ffn_wqkv", "L1_ffn_q_scale",
        "L1_ffn_k_scale", "L1_ffn_wo", "L1_ffn_ln_g", "L1_ffn_ln_b", "L1_ffn_w1",
        "L1_ffn_b1", "L1_ffn_w2", "L1_ffn_b2"],
    2: ["L2_attn_Wq", "L2_attn_Wk", "L2_attn_Wv", "L2_attn_g_scale",
        "L2_ffn_router_w", "L2_ffn_router_bias", "L2_ffn_up_proj",
        "L2_ffn_gate_proj", "L2_ffn_down_proj", "L2_ffn_sh_gate", "L2_ffn_sh_up",
        "L2_ffn_sh_down"],
    3: ["L3_attn_w_q12", "L3_attn_w_k12", "L3_attn_w_v", "L3_attn_lambda_q1",
        "L3_attn_lambda_k1", "L3_attn_lambda_q2", "L3_attn_lambda_k2",
        "L3_ffn_Wr", "L3_ffn_br", "L3_ffn_W1", "L3_ffn_b1", "L3_ffn_W2",
        "L3_ffn_b2"],
}
LAYER_CONSTS = {
    0: ["c_dftcos", "c_dftsin", "c_idftA", "c_idftB", "c_eye_f32"],
    1: ["c_eye_f32", "c_eye_bf", "c_rotA", "c_rotB", "c_rotC", "c_rotD",
        "c_tri", "c_triu", "c_shift", "c_bnd"],
    2: ["c_eye_f32", "c_eye_bf"],
    3: ["c_eye_f32", "c_eye_bf"],
}


def _build_and_run(layers, in_maps, shapes):
    consts = host_constants()
    nc = bass.Bass("TRN2", target_bir_lowering=False, dynamic_dma_scratch_size=16384)
    inp = {}
    inp["x_in"] = nc.dram_tensor("x_in", [T_TOK, D], F32, kind="ExternalInput").ap()
    need_w = sorted({n for l in layers for n in LAYER_INPUTS[l]})
    need_c = sorted({n for l in layers for n in LAYER_CONSTS[l]})
    for n in need_w:
        shp = list(shapes[n]) if len(shapes[n]) else [1]
        inp[n] = nc.dram_tensor(n, shp, F32, kind="ExternalInput").ap()
    if 1 in layers:
        inp["L1w1p"] = nc.dram_tensor("L1w1p", [2, 1024, 2 * HIDP], F32,
                                      kind="ExternalInput").ap()
        inp["L1b1p"] = nc.dram_tensor("L1b1p", [2, 2 * HIDP], F32,
                                      kind="ExternalInput").ap()
        inp["L1w2p"] = nc.dram_tensor("L1w2p", [2, HIDP, 1024], F32,
                                      kind="ExternalInput").ap()
        inp["x_spill"] = nc.dram_tensor("x_spill", [128, NT, D], F32,
                                        kind="Internal").ap()
    for n in need_c:
        inp[n] = nc.dram_tensor(n, list(consts[n].shape), F32, kind="ExternalInput").ap()
    x_out = nc.dram_tensor("x_out", [T_TOK, D], F32, kind="ExternalOutput").ap()

    with contextlib.ExitStack() as st:
        P = Prog(nc)
        ctx = Ctx()
        ctx.nc, ctx.P, ctx.st, ctx.inp = nc, P, st, inp
        ctx.x = st.enter_context(nc.sbuf_tensor("x", [128, NT, D], F32))
        ctx.xi = st.enter_context(nc.sbuf_tensor("xi", [128, NT, D], F32))
        ctx.xnT = st.enter_context(nc.sbuf_tensor("xnT", [128, ND, T_TOK], BF16))
        ctx.bx, ctx.bxi, ctx.bxnT = Buf("x"), Buf("xi"), Buf("xnT")
        ctx.groups = {"xi": [ctx.bxi]}
        ctx.ones1 = st.enter_context(nc.sbuf_tensor("ones1", [1, 512], BF16))
        ctx.bones1 = Buf("ones1")
        P.do("gpsimd", lambda ns: ns.memset(ctx.ones1[:], 1.0), writes=[ctx.bones1])
        ctx.onesc_f = st.enter_context(nc.sbuf_tensor("onescf", [128, 1], F32))
        ctx.bonesc_f = Buf("onescf")
        P.do("gpsimd", lambda ns: ns.memset(ctx.onesc_f[:], 1.0), writes=[ctx.bonesc_f])
        ctx.onesr_f = st.enter_context(nc.sbuf_tensor("onesrf", [1, 128], F32))
        ctx.bonesr_f = Buf("onesrf")
        P.do("gpsimd", lambda ns: ns.memset(ctx.onesr_f[:], 1.0), writes=[ctx.bonesr_f])
        ctx.onesc_b = st.enter_context(nc.sbuf_tensor("onescb", [128, 1], BF16))
        ctx.bonesc_b = Buf("onescb")
        P.do("gpsimd", lambda ns: ns.memset(ctx.onesc_b[:], 1.0), writes=[ctx.bonesc_b])

        if "c_eye_f32" in need_c:
            ctx.eye_f32 = st.enter_context(nc.sbuf_tensor("eyef", [128, 128], F32))
            ctx.beye_f32 = Buf("eyef")
            P.dma("sync", "eyef", lambda ns: ns.dma_start(
                out=ctx.eye_f32[:], in_=inp["c_eye_f32"]), writes=[ctx.beye_f32])
        if "c_eye_bf" in need_c:
            ctx.eye_bf = st.enter_context(nc.sbuf_tensor("eyeb", [128, 128], BF16))
            ctx.beye_bf = Buf("eyeb")
            P.dma("gpsimd", "eyeb", lambda ns: ns.dma_start(
                out=ctx.eye_bf[:], in_=inp["c_eye_bf"]), writes=[ctx.beye_bf])
        ctx.psA = st.enter_context(nc.psum_tensor("psA", [128, 2048], F32))
        ctx.psB = st.enter_context(nc.psum_tensor("psB", [128, 2048], F32))
        ctx.psA_b = [Buf(f"psA{i}") for i in range(4)]
        ctx.psB_b = [Buf(f"psB{i}") for i in range(4)]

        P.dma("sync", "xin", lambda ns: ns.dma_start(
            out=ctx.x[:], in_=inp["x_in"].rearrange("(t p) d -> p t d", p=128)),
            writes=[ctx.bx])

        dbg_outs = {}
        for li, l in enumerate(layers):
            if li > 0:
                P.set_fence()
            ctx.lid = l
            ctx.dbg = {}
            with contextlib.ExitStack() as lst:
                ctx.st = lst
                emit_layer(ctx, l)
                if DEBUG_TAPS:
                    for nm, (ap, bufs, shape) in list(ctx.dbg.items()):
                        if nm not in DEBUG_TAPS:
                            continue
                        od = nc.dram_tensor(f"dbg_{nm}", shape, F32,
                                            kind="ExternalOutput").ap()
                        dbg_outs[nm] = od
                        rec = P.dma("gpsimd", f"dbg_{nm}",
                                    lambda ns, od=od, ap=ap: ns.dma_start(
                                        out=od, in_=ap), reads=list(bufs))
                        P.finish([rec])
            ctx.st = st

        rec = P.dma("sync", "xout", lambda ns: ns.dma_start(
            out=x_out.rearrange("(t p) d -> p t d", p=128), in_=ctx.x[:]),
            reads=[ctx.bx])
        P.finish([rec])
        P.finalize(st)

    res = run_bass_kernel_spmd(nc, in_maps, core_ids=list(range(len(in_maps))))
    global _LAST_RES
    _LAST_RES = res.results
    return [r["x_out"] for r in res.results]


HIDP = 2816


def prep_l1_mlp(inputs):
    w1 = np.asarray(inputs["L1_ffn_w1"], np.float32)
    b1 = np.asarray(inputs["L1_ffn_b1"], np.float32)
    w2 = np.asarray(inputs["L1_ffn_w2"], np.float32)
    HID = 2730
    w1p = np.zeros((2, 1024, 2 * HIDP), np.float32)
    b1p = np.zeros((2, 2 * HIDP), np.float32)
    w2p = np.zeros((2, HIDP, 1024), np.float32)
    for blk in range(HIDP // 128):
        lo = blk * 128
        hi = min(lo + 128, HID)
        n = hi - lo
        if n <= 0:
            continue
        w1p[:, :, blk * 256:blk * 256 + n] = w1[:, :, lo:hi]
        w1p[:, :, blk * 256 + 128:blk * 256 + 128 + n] = w1[:, :, HID + lo:HID + hi]
        b1p[:, blk * 256:blk * 256 + n] = b1[:, lo:hi]
        b1p[:, blk * 256 + 128:blk * 256 + 128 + n] = b1[:, HID + lo:HID + hi]
        w2p[:, blk * 128:blk * 128 + n, :] = w2[:, lo:hi, :]
    return {"L1w1p": np.ascontiguousarray(w1p), "L1b1p": np.ascontiguousarray(b1p),
            "L1w2p": np.ascontiguousarray(w2p)}


def kernel(**inputs):
    consts = host_constants()
    x_full = np.ascontiguousarray(inputs["L0_attn_x"], dtype=np.float32)
    n_cores = 8
    shapes = {k: np.asarray(v).shape for k, v in inputs.items()}
    x_cur = [x_full[c] for c in range(n_cores)]

    for layers in LAUNCH_PLAN:
        need_w = sorted({n for l in layers for n in LAYER_INPUTS[l]})
        need_c = sorted({n for l in layers for n in LAYER_CONSTS[l]})
        extra = prep_l1_mlp(inputs) if 1 in layers else {}
        in_maps = []
        for c in range(n_cores):
            m = {"x_in": np.ascontiguousarray(x_cur[c], dtype=np.float32)}
            for n in need_w:
                v = np.asarray(inputs[n], dtype=np.float32)
                if v.ndim == 0:
                    v = v.reshape(1)
                m[n] = np.ascontiguousarray(v)
            m.update(extra)
            for n in need_c:
                m[n] = consts[n]
            in_maps.append(m)
        x_cur = _build_and_run(layers, in_maps, shapes)

    return np.stack(x_cur, axis=0).astype(np.float32)


# version 07
# speedup vs baseline: 1.1768x; 1.1768x over previous
import contextlib

import numpy as np
import ml_dtypes

import concourse.bass as bass
import concourse.mybir as mybir
import concourse.bass_utils as _bass_utils
from concourse.bass_utils import run_bass_kernel_spmd

if not getattr(_bass_utils, "_ldw_opt_patched", False):
    _orig_run_command = _bass_utils.run_command

    def _run_command_ldw(cmd, **kw):
        if isinstance(cmd, list):
            cmd = ["--enable-ldw-opt=true" if c == "--enable-ldw-opt=false"
                   else c for c in cmd]
        return _orig_run_command(cmd, **kw)

    _bass_utils.run_command = _run_command_ldw
    _bass_utils._ldw_opt_patched = True

F32 = mybir.dt.float32
BF16 = mybir.dt.bfloat16
AF = mybir.ActivationFunctionType
ALU = mybir.AluOpType
AX = mybir.AxisListType

T_TOK = 2048
D = 1024
NT = 16
ND = 8
NM = 513

LAUNCH_PLAN = [[0, 1, 2, 3]]
DEBUG_TAPS = ()
_LAST_RES = None

ENGINES = ("tensor", "vector", "scalar", "gpsimd", "sync")


class Rec:
    __slots__ = ("eng", "fn", "deps", "needed", "idx", "inc_no", "dma")

    def __init__(self, eng, fn, deps, idx, dma=None):
        self.eng, self.fn, self.deps, self.idx, self.dma = eng, fn, deps, idx, dma
        self.needed = False
        self.inc_no = None


class Buf:
    __slots__ = ("w", "rs", "name")

    def __init__(self, name=""):
        self.w = None
        self.rs = []
        self.name = name


def handoff(frm, *tos):
    for t in tos:
        t.w = frm.w
        t.rs = list(frm.rs)


def fork_arena(ctx, key, *newbufs):
    group = ctx.groups.setdefault(key, [])
    haz = []
    for b in group:
        if b.w is not None:
            haz.append(b.w)
        haz.extend(b.rs)
    for nb in newbufs:
        nb.w = None
        nb.rs = list(haz)
    ctx.groups[key] = list(newbufs)


class Prog:
    def __init__(self, nc):
        self.nc = nc
        self.per_eng = {e: [] for e in ENGINES}
        self.dma_sems = {}
        self.final_deps = []
        self.fence = []

    def set_fence(self):
        self.fence = [q[-1] for q in self.per_eng.values() if q]

    def do(self, eng, fn, reads=(), writes=(), extra=()):
        deps = list(self.fence)
        for b in reads:
            if b.w is not None:
                deps.append(b.w)
        for b in writes:
            if b.w is not None:
                deps.append(b.w)
            deps.extend(b.rs)
        deps.extend(extra)
        rec = Rec(eng, fn, deps, len(self.per_eng[eng]))
        self.per_eng[eng].append(rec)
        for b in reads:
            b.rs.append(rec)
        for b in writes:
            b.w = rec
            b.rs = []
        return rec

    def dma(self, queue, sem_key, fn, reads=(), writes=(), extra=()):
        cnt = self.dma_sems.get(sem_key, 0) + 16
        self.dma_sems[sem_key] = cnt
        rec = self.do(queue, fn, reads=reads, writes=writes, extra=extra)
        rec.dma = (sem_key, cnt)
        return rec

    def finish(self, recs):
        self.final_deps.extend(recs)

    def finalize(self, st):
        nc = self.nc
        for eng in ENGINES:
            for rec in self.per_eng[eng]:
                for dep in rec.deps:
                    if dep.dma is None and dep.eng == "tensor" and rec.eng == "tensor":
                        continue
                    if dep.dma is None:
                        dep.needed = True
        for dep in self.final_deps:
            if dep.dma is None:
                dep.needed = True
        eng_total = {}
        for eng in ENGINES:
            n = 0
            for rec in self.per_eng[eng]:
                if rec.dma is None and rec.needed:
                    n += 1
                    rec.inc_no = n
            eng_total[eng] = n
        eng_sem = {e: st.enter_context(nc.semaphore(f"se_{e}"))
                   for e in ENGINES if eng_total[e]}
        dma_sem = {k: st.enter_context(nc.semaphore(f"sd_{k}"))
                   for k in self.dma_sems}

        def tok(dep):
            if dep.dma is not None:
                return ("d", dep.dma[0], dep.dma[1])
            assert dep.inc_no is not None, "dep without inc"
            return ("e", dep.eng, dep.inc_no)

        def emit_for(eng, ns):
            water = {}
            for rec in self.per_eng[eng]:
                for dep in rec.deps:
                    if dep.dma is None and dep.eng == "tensor" and eng == "tensor":
                        continue
                    kind, key, val = tok(dep)
                    wkey = (kind, key)
                    if water.get(wkey, -1) >= val:
                        continue
                    sem = dma_sem[key] if kind == "d" else eng_sem[key]
                    ns.wait_ge(sem, val)
                    water[wkey] = val
                inst = rec.fn(ns)
                if rec.dma is not None:
                    inst.then_inc(dma_sem[rec.dma[0]], 16)
                elif rec.needed:
                    if inst is None:
                        inst = ns.nop()
                    inst.then_inc(eng_sem[eng], 1)
            if eng == "sync":
                water = {}
                for dep in self.final_deps:
                    kind, key, val = tok(dep)
                    wkey = (kind, key)
                    if water.get(wkey, -1) >= val:
                        continue
                    sem = dma_sem[key] if kind == "d" else eng_sem[key]
                    ns.wait_ge(sem, val)
                    water[wkey] = val

        block = st.enter_context(nc.Block())

        @block.tensor
        def _(ns):
            emit_for("tensor", ns)

        @block.vector
        def _(ns):
            emit_for("vector", ns)

        @block.scalar
        def _(ns):
            emit_for("scalar", ns)

        @block.gpsimd
        def _(ns):
            emit_for("gpsimd", ns)

        @block.sync
        def _(ns):
            emit_for("sync", ns)


_CONST_CACHE = {}


def host_constants():
    if _CONST_CACHE:
        return _CONST_CACHE
    c = _CONST_CACHE
    N = 1024
    m = np.arange(NM, dtype=np.float64)
    d = np.arange(N, dtype=np.float64)
    ang = 2.0 * np.pi * np.outer(d, m) / N
    c["c_dftcos"] = np.cos(ang).astype(np.float32)
    c["c_dftsin"] = (-np.sin(ang)).astype(np.float32)
    w = np.where((m == 0) | (m == N // 2), 1.0, 2.0)
    angT = ang.T
    c["c_idftA"] = ((w[:, None] * np.cos(angT)) / N).astype(np.float32)
    c["c_idftB"] = ((-w[:, None] * np.sin(angT)) / N).astype(np.float32)
    c["c_eye_f32"] = np.eye(128, dtype=np.float32)
    c["c_eye_bf"] = np.eye(128, dtype=np.float32)
    DH, WIN, n = 64, 512, T_TOK
    half = np.arange(0, DH, 2, dtype=np.float64)
    inv_freq = 1.0 / 10000.0 ** (half / DH)
    t = np.arange(n, dtype=np.float64)
    freqs = t[:, None] * inv_freq[None, :]
    freqs = np.concatenate([freqs, freqs], axis=-1)
    base = (half + 0.4 * DH) / (1.4 * DH)
    power = (t - n // 2) / float(WIN // 2)
    scale = base[None, :] ** power[:, None]
    scale = np.concatenate([scale, scale], axis=-1)
    cos, sin = np.cos(freqs), np.sin(freqs)
    sgn = np.where(np.arange(DH) < DH // 2, -1.0, 1.0)[None, :]
    c["c_rotA"] = (cos * scale).astype(np.float32)
    c["c_rotB"] = (sin * scale * sgn).astype(np.float32)
    c["c_rotC"] = (cos / scale).astype(np.float32)
    c["c_rotD"] = (sin / scale * sgn).astype(np.float32)
    i = np.arange(128)[:, None]
    j = np.arange(128)[None, :]
    c["c_tri"] = np.where(j <= i, 0.0, -1.0e30).astype(np.float32)
    c["c_triu"] = np.where(j >= i, 0.0, -1.0e30).astype(np.float32)
    c["c_shift"] = np.zeros((128, 128), np.float32)
    for jj in range(127):
        c["c_shift"][jj, jj + 1] = 1.0
    c["c_bnd"] = np.zeros((128, 128), np.float32)
    c["c_bnd"][127, 0] = 1.0
    hs = np.zeros((8, 128, 16), dtype=np.float32)
    for dt in range(8):
        for p in range(128):
            hs[dt, p, (dt * 128 + p) // 64] = 1.0
    c["c_hsum16"] = hs
    c["c_eye16"] = np.eye(16, dtype=np.float32)
    return c


class Ctx:
    pass


def mm_chunks(ns, out_fn, lhsT, rhs_fn, nfree, start, stop, chunk=512):
    inst = None
    off = 0
    while off < nfree:
        w = min(chunk, nfree - off)
        inst = ns.matmul(out_fn(off, w), lhsT, rhs_fn(off, w),
                         start=start, stop=stop and off + w >= nfree)
        off += w
    return inst


def dbg_now(ctx, nm, ap, bufs, shape):
    if nm not in DEBUG_TAPS:
        return
    od = ctx.nc.dram_tensor(f"dbg_{nm}", shape, F32, kind="ExternalOutput").ap()
    rec = ctx.P.dma("gpsimd", f"dbg_{nm}", lambda ns: ns.dma_start(out=od, in_=ap),
                    reads=list(bufs))
    ctx.P.finish([rec])


def wload(ctx, name, src_ap, shape, dtype=BF16, queue="gpsimd", last_dim=8192):
    nc, P = ctx.nc, ctx.P
    t = ctx.st.enter_context(nc.sbuf_tensor(name, shape, dtype))
    b = Buf(name)
    ctx.P.dma(queue, name, lambda ns: ns.dma_start(
        out=t[:], in_=src_ap, max_dma_last_dim=last_dim), writes=[b])
    return t, b


def emit_outer_ln(ctx):
    nc, P, st = ctx.nc, ctx.P, ctx.st
    x, xi, xnT = ctx.x, ctx.xi, ctx.xnT
    bx, bxi, bxnT = ctx.bx, ctx.bxi, ctx.bxnT
    fork_arena(ctx, "xi", bxi)
    stats = st.enter_context(nc.sbuf_tensor(f"lnst{ctx.lid}", [128, 4, 16], F32))
    bstats = Buf("lnst")
    psAl, bAl = ctx.psA, ctx.psA_b
    for t in range(NT):
        P.do("vector", lambda ns, t=t: ns.tensor_reduce(
            out=stats[:, 0, t:t + 1], in_=x[:, t, :], axis=AX.X, op=ALU.add),
            reads=[bx], writes=[bstats])
        bank = (t % 2) * 2
        P.do("scalar", lambda ns, t=t, bank=bank: ns.activation(
            out=psAl[:, bank * 512:bank * 512 + 1024], in_=x[:, t, :],
            func=AF.Square, accum_out=stats[:, 1, t:t + 1]),
            reads=[bx], writes=[bstats, bAl[bank], bAl[bank + 1]])
    P.do("vector", lambda ns: ns.tensor_scalar(
        out=stats[:, 0, :], in0=stats[:, 0, :], scalar1=1.0 / D, scalar2=None,
        op0=ALU.mult), reads=[bstats], writes=[bstats])
    P.do("vector", lambda ns: ns.scalar_tensor_tensor(
        out=stats[:, 2, :], in0=stats[:, 0, :], scalar=-1.0, in1=stats[:, 0, :],
        op0=ALU.mult, op1=ALU.mult), reads=[bstats], writes=[bstats])
    P.do("vector", lambda ns: ns.scalar_tensor_tensor(
        out=stats[:, 2, :], in0=stats[:, 1, :], scalar=1.0 / D, in1=stats[:, 2, :],
        op0=ALU.mult, op1=ALU.add), reads=[bstats], writes=[bstats])
    P.do("vector", lambda ns: ns.tensor_scalar(
        out=stats[:, 2, :], in0=stats[:, 2, :], scalar1=1.0e-5, scalar2=None,
        op0=ALU.add), reads=[bstats], writes=[bstats])
    P.do("scalar", lambda ns: ns.activation(
        out=stats[:, 3, :], in_=stats[:, 2, :], func=AF.Sqrt),
        reads=[bstats], writes=[bstats])
    P.do("vector", lambda ns: ns.reciprocal(
        out=stats[:, 3, :], in_=stats[:, 3, :]), reads=[bstats], writes=[bstats])
    for t in range(NT):
        P.do("vector", lambda ns, t=t: ns.tensor_scalar(
            out=xi[:, t, :], in0=x[:, t, :], scalar1=stats[:, 0, t:t + 1],
            scalar2=stats[:, 3, t:t + 1], op0=ALU.subtract, op1=ALU.mult),
            reads=[bx, bstats], writes=[bxi])
    eye = ctx.eye_f32
    psA, bA = ctx.psA, ctx.psA_b
    for t in range(NT):
        for dp in range(4):
            bank = (t * 4 + dp) % 4
            P.do("tensor", lambda ns, t=t, dp=dp, bank=bank: ns.transpose(
                psA[:, bank * 512:bank * 512 + 128],
                xi[:, t, 2 * dp * 128:(2 * dp + 1) * 128], eye[:]),
                reads=[bxi, ctx.beye_f32], writes=[bA[bank]])
            P.do("tensor", lambda ns, t=t, dp=dp, bank=bank: ns.transpose(
                psA[:, bank * 512 + 128:bank * 512 + 256],
                xi[:, t, (2 * dp + 1) * 128:(2 * dp + 2) * 128], eye[:]),
                reads=[bxi, ctx.beye_f32], writes=[bA[bank]])
            P.do("vector", lambda ns, t=t, dp=dp, bank=bank: ns.tensor_copy(
                out=xnT[:, 2 * dp:2 * dp + 2, t * 128:(t + 1) * 128],
                in_=psA[:, bank * 512:bank * 512 + 256].rearrange(
                    "p (a c) -> p a c", a=2)),
                reads=[bA[bank]], writes=[bxnT])


def emit_l0_attn(ctx):
    nc, P, st = ctx.nc, ctx.P, ctx.st
    xnT, bxnT = ctx.xnT, ctx.bxnT
    x, bx = ctx.x, ctx.bx
    inp = ctx.inp
    psA, psB = ctx.psA, ctx.psB
    bA, bB = ctx.psA_b, ctx.psB_b

    xib = ctx.xi[:].rearrange("p a b -> p (a b)").bitcast(BF16)
    bxi = ctx.bxi
    Kre = xib[:, 0:8192].rearrange("p (mt t) -> p mt t", mt=4)
    Kim = xib[:, 8192:16384].rearrange("p (mt t) -> p mt t", mt=4)
    Qre, Qim = Kre, Kim
    Vre_mt = xib[:, 16384:18432]
    Vim_mt = xib[:, 18432:20480]
    wstage = xib[:, 20480:22528].rearrange("p (dt e) -> p dt e", dt=8)
    Kre4 = xib[0:1, 24576:26624]
    Qre4 = Kre4
    Vre4 = xib[0:1, 26624:28672]
    junkb = xib[:, 28672:30720]
    idfA4 = xib[0:1, 30720:31744]
    bV, bw, b4, bj, bi4 = (Buf("l0V"), Buf("l0ws"), Buf("l0b4"), Buf("l0junk"),
                           Buf("l0i4"))
    fork_arena(ctx, "xi", bxi, bV, bw, b4, bj, bi4)

    KTt = st.enter_context(nc.sbuf_tensor("l0KT", [128, 8, T_TOK], BF16))
    bKT = Buf("l0KT")
    tmpv = KTt[:].rearrange("p a b -> p (a b)").bitcast(F32)
    arena = st.enter_context(nc.sbuf_tensor("l0ar", [128, 4096], BF16))
    barC = Buf("l0ar")
    dftst = [arena[:, i * 1024:(i + 1) * 1024].rearrange("p (dt m) -> p dt m", dt=8)
             for i in range(2)]
    dft4 = arena[:, 2048:2056].rearrange("p (dt m) -> p dt m", dt=8)
    idfst = [arena[:, i * 1024:(i + 1) * 1024] for i in range(4)]

    bias_aps = [xib[0:1, 22528:23552], xib[0:1, 23552:24576],
                xib[0:1, 31744:32768]]
    bbias = Buf("l0bias")
    ctx.groups["xi"].append(bbias)
    bbias.rs = list(bxi.rs)
    bbias.w = bxi.w
    for i, nm in enumerate(["L0_attn_bq", "L0_attn_bk", "L0_attn_bv"]):
        P.dma("gpsimd", "l0bias", lambda ns, i=i, nm=nm: ns.dma_start(
            out=bias_aps[i], in_=inp[nm].unsqueeze(0)),
            writes=[bbias])
    P.dma("gpsimd", "l0dft4", lambda ns: ns.dma_start(
        out=dft4, in_=inp["c_dftcos"][:, 512:513].rearrange(
            "(dt p) m -> p dt m", p=128),
        allow_slow_non_contiguous=True), writes=[barC])

    sacc = st.enter_context(nc.sbuf_tensor("l0sacc", [128, 8, 5], F32))
    bsacc = Buf("sacc")
    pp = [0]

    def proj(wname, bi):
        for quarter in range(4):
            P.dma("gpsimd", "l0ws", lambda ns, wname=wname, quarter=quarter: ns.dma_start(
                out=wstage,
                in_=inp[wname][:, quarter * 256:(quarter + 1) * 256].rearrange(
                    "(dt p) e -> p dt e", p=128), max_dma_last_dim=1024),
                writes=[bw])
            for el in range(2):
                et = quarter * 2 + el
                ps, bps = (psA, bA) if pp[0] % 2 == 0 else (psB, bB)
                pp[0] += 1
                def mk(ns, el=el, et=et, bi=bi, ps=ps):
                    inst = None
                    for dt in range(8):
                        inst = mm_chunks(ns, lambda o, w: ps[:, o:o + w],
                                         wstage[:, dt, el * 128:(el + 1) * 128],
                                         lambda o, w: xnT[:, dt, o:o + w], T_TOK,
                                         start=(dt == 0), stop=False)
                    inst = mm_chunks(ns, lambda o, w: ps[:, o:o + w],
                                     bias_aps[bi][:, et * 128:(et + 1) * 128],
                                     lambda o, w: ctx.ones1[:, 0:w], T_TOK,
                                     start=False, stop=True)
                    return inst
                P.do("tensor", mk, reads=[bw, bxnT, bbias, ctx.bones1], writes=bps)
                P.do("scalar", lambda ns, et=et, ps=ps: ns.activation(
                    out=KTt[:, et, :], in_=ps[:], func=AF.Identity),
                    reads=bps, writes=[bKT])

    def rfft_tile(table, msl, out_ap, bout, rows=128):
        sb = pp[0] % 2
        if rows == 128:
            P.dma("gpsimd", f"l0dft{sb}", lambda ns, table=table, msl=msl, sb=sb: ns.dma_start(
                out=dftst[sb], in_=inp[table][:, msl].rearrange(
                    "(dt p) m -> p dt m", p=128), max_dma_last_dim=1024),
                writes=[barC])
            lhs_src, blhs = dftst[sb], barC
        else:
            lhs_src, blhs = dft4, barC
        ps, bps = (psA, bA) if pp[0] % 2 == 0 else (psB, bB)
        pp[0] += 1
        def mk(ns, lhs_src=lhs_src, ps=ps, rows=rows):
            inst = None
            for dt in range(8):
                inst = mm_chunks(ns, lambda o, w: ps[0:rows, o:o + w],
                                 lhs_src[:, dt, :],
                                 lambda o, w: KTt[:, dt, o:o + w], T_TOK,
                                 start=(dt == 0), stop=(dt == 7))
            return inst
        P.do("tensor", mk, reads=[blhs, bKT], writes=bps)
        P.do("scalar", lambda ns, out_ap=out_ap, ps=ps, rows=rows: ns.activation(
            out=out_ap, in_=ps[0:rows, 0:T_TOK], func=AF.Identity),
            reads=bps, writes=[bout])

    proj("L0_attn_Wk", 1)
    for mt in range(4):
        rfft_tile("c_dftcos", slice(mt * 128, (mt + 1) * 128), Kre[:, mt, :], bxi)
        rfft_tile("c_dftsin", slice(mt * 128, (mt + 1) * 128), Kim[:, mt, :], bxi)
    rfft_tile("c_dftcos", None, Kre4, b4, rows=1)

    proj("L0_attn_Wv", 2)
    for mt in range(4):
        rfft_tile("c_dftcos", slice(mt * 128, (mt + 1) * 128), Vre_mt, bV)
        rfft_tile("c_dftsin", slice(mt * 128, (mt + 1) * 128), Vim_mt, bV)
        for slot, A, bAb, Vb in [(0, Kre[:, mt, :], bxi, Vre_mt),
                                 (1, Kim[:, mt, :], bxi, Vim_mt),
                                 (2, Kre[:, mt, :], bxi, Vim_mt),
                                 (3, Kim[:, mt, :], bxi, Vre_mt)]:
            P.do("vector", lambda ns, A=A, Vb=Vb, mt=mt, slot=slot: ns.scalar_tensor_tensor(
                out=junkb, in0=A, scalar=1.0, in1=Vb,
                op0=ALU.mult, op1=ALU.mult, accum_out=sacc[:, slot, mt:mt + 1]),
                reads=[bAb, bV], writes=[bj, bsacc])
    rfft_tile("c_dftcos", None, Vre4, b4, rows=1)
    P.do("vector", lambda ns: ns.scalar_tensor_tensor(
        out=junkb[0:1, :], in0=Kre4, scalar=1.0, in1=Vre4,
        op0=ALU.mult, op1=ALU.mult, accum_out=sacc[0:1, 0, 4:5]),
        reads=[b4], writes=[bj, bsacc])
    P.do("vector", lambda ns: ns.tensor_tensor(
        out=sacc[:, 4, 0:4], in0=sacc[:, 0, 0:4], in1=sacc[:, 1, 0:4],
        op=ALU.subtract), reads=[bsacc], writes=[bsacc])
    P.do("vector", lambda ns: ns.tensor_tensor(
        out=sacc[:, 5, 0:4], in0=sacc[:, 2, 0:4], in1=sacc[:, 3, 0:4],
        op=ALU.add), reads=[bsacc], writes=[bsacc])
    P.do("vector", lambda ns: ns.tensor_scalar(
        out=sacc[:, 6, 0:4], in0=sacc[:, 4, 0:4], scalar1=-1.0, scalar2=None,
        op0=ALU.mult), reads=[bsacc], writes=[bsacc])

    proj("L0_attn_Wq", 0)
    for mt in range(4):
        rfft_tile("c_dftcos", slice(mt * 128, (mt + 1) * 128), Qre[:, mt, :], bxi)
        rfft_tile("c_dftsin", slice(mt * 128, (mt + 1) * 128), Qim[:, mt, :], bxi)
    rfft_tile("c_dftcos", None, Qre4, b4, rows=1)

    t1 = tmpv[:, 0:2048]
    t2 = tmpv[:, 2048:4096]
    for mt in range(4):
        qre, qim = Qre[:, mt, :], Qim[:, mt, :]
        sre = sacc[:, 4, mt:mt + 1]
        sim = sacc[:, 5, mt:mt + 1]
        nre = sacc[:, 6, mt:mt + 1]
        P.do("vector", lambda ns, qre=qre, sre=sre: ns.tensor_scalar(
            out=t1, in0=qre, scalar1=sre, scalar2=None, op0=ALU.mult),
            reads=[bxi, bsacc], writes=[bKT])
        P.do("vector", lambda ns, qre=qre, sim=sim: ns.tensor_scalar(
            out=t2, in0=qre, scalar1=sim, scalar2=None, op0=ALU.mult),
            reads=[bxi, bsacc], writes=[bKT])
        P.do("vector", lambda ns, qre=qre, qim=qim, nre=nre: ns.scalar_tensor_tensor(
            out=qre, in0=qim, scalar=nre, in1=t2,
            op0=ALU.mult, op1=ALU.add), reads=[bxi, bsacc, bKT], writes=[bxi])
        P.do("vector", lambda ns, qim=qim, sim=sim: ns.scalar_tensor_tensor(
            out=qim, in0=qim, scalar=sim, in1=t1,
            op0=ALU.mult, op1=ALU.add), reads=[bxi, bsacc, bKT], writes=[bxi])
    P.do("vector", lambda ns: ns.tensor_scalar(
        out=Qre4, in0=Qre4, scalar1=sacc[0:1, 0, 4:5], scalar2=None,
        op0=ALU.mult), reads=[b4, bsacc], writes=[b4])
    Oim, Ore = Qre, Qim
    Ore4 = Qre4
    idfA4 = xib[0:1, 30720:31744]
    P.dma("gpsimd", "l0idfA4", lambda ns: ns.dma_start(
        out=idfA4, in_=inp["c_idftA"][512:513, :]), writes=[bi4])

    for g in range(2):
        for q in range(4):
            mt = g * 2 + q // 2
            tab = "c_idftA" if q % 2 == 0 else "c_idftB"
            P.dma("gpsimd", f"l0idf{q}", lambda ns, mt=mt, tab=tab, q=q: ns.dma_start(
                out=idfst[q], in_=inp[tab][mt * 128:(mt + 1) * 128, :],
                max_dma_last_dim=4096), writes=[barC])
        for t in range(NT):
            ps, bps = (psA, bA) if t % 2 == 0 else (psB, bB)
            def mk(ns, t=t, g=g, ps=ps):
                inst = None
                for q2 in range(2):
                    mt = g * 2 + q2
                    inst = mm_chunks(ns, lambda o, w: ps[:, o:o + w],
                                     Ore[:, mt, t * 128:(t + 1) * 128],
                                     lambda o, w: idfst[q2 * 2][:, o:o + w], D,
                                     start=(q2 == 0), stop=False)
                    inst = mm_chunks(ns, lambda o, w: ps[:, o:o + w],
                                     Oim[:, mt, t * 128:(t + 1) * 128],
                                     lambda o, w: idfst[q2 * 2 + 1][:, o:o + w], D,
                                     start=False, stop=(g == 0 and q2 == 1))
                if g == 1:
                    inst = mm_chunks(ns, lambda o, w: ps[:, o:o + w],
                                     Ore4[:, t * 128:(t + 1) * 128],
                                     lambda o, w: idfA4[:, o:o + w], D,
                                     start=False, stop=True)
                return inst
            P.do("tensor", mk, reads=[bxi, b4, barC, bi4],
                 writes=bps[0:2])
            P.do("vector", lambda ns, t=t, ps=ps: ns.tensor_tensor(
                out=x[:, t, :], in0=ps[:, 0:1024], in1=x[:, t, :], op=ALU.add),
                reads=bps[0:2] + [bx], writes=[bx])

    ctx.moe_arena = KTt[:].rearrange("p a b -> p (a b)")
    ctx.moe_arena_buf = bKT


def emit_moe(ctx, names, act_fn, n_exp=8, has_br=False):
    nc, P, st = ctx.nc, ctx.P, ctx.st
    lid = ctx.lid
    xnT, bxnT = ctx.xnT, ctx.bxnT
    x, bx = ctx.x, ctx.bx
    inp = ctx.inp
    NF = 32

    xif = ctx.xi[:].rearrange("p a b -> p (a b)")
    xib = xif.bitcast(BF16)
    hT = xib[:, 0:32768].rearrange("p (f c) -> p f c", f=NF)
    bhT = Buf("hT")
    fork_arena(ctx, "xi", bhT)

    ma = ctx.moe_arena
    maf = ma.bitcast(F32)
    w1v = [ma[:, i * 4096:(i + 1) * 4096].rearrange("p (dt f) -> p dt f", dt=8)
           for i in range(2)]
    w2v = [ma[:, 8192 + i * 512:8192 + (i + 1) * 512] for i in range(2)]
    cT = ma[0:n_exp, 10240:12288]
    b2all = ma[0:n_exp, 12288:13312]
    wgt = maf[:, 6656:6656 + NT * n_exp].rearrange("p (t e) -> p t e", t=NT)
    b1sb = maf[:, 6784:6816]
    gsc = maf[:, 6816:6864].rearrange("p (a e) -> p a e", a=6)
    b1stage = maf[0:32, 6864:6992]
    gate_w = ma[:, 13984:14048].rearrange("p (dt e) -> p dt e", dt=8)
    br_row = ma[0:1, 14048:14048 + n_exp]
    bw1 = [Buf("w1a"), Buf("w1b")]
    bw2 = [Buf("w2a"), Buf("w2b")]
    bcT = Buf("cT")
    bb2 = Buf("b2")
    bwgt = Buf("wgt")
    bb1 = Buf("b1")
    bb1s = Buf("b1s")
    bgsc = Buf("gsc")
    bgate = Buf("gw")
    bbr = Buf("br")
    handoff(ctx.moe_arena_buf, bw1[0], bw1[1], bw2[0], bw2[1], bcT, bb2,
            bwgt, bb1, bb1s, bgsc, bgate, bbr)

    P.dma("gpsimd", f"moeGW{lid}", lambda ns: ns.dma_start(
        out=gate_w, in_=inp[names["gate_w"]].rearrange("(dt p) e -> p dt e", p=128)),
        writes=[bgate])
    P.dma("gpsimd", f"moeB2{lid}", lambda ns: ns.dma_start(
        out=b2all, in_=inp[names["b2"]]), writes=[bb2])
    if has_br:
        P.dma("gpsimd", f"moeBR{lid}", lambda ns: ns.dma_start(
            out=br_row, in_=inp[names["br"]].unsqueeze(0)), writes=[bbr])

    psA, psB = ctx.psA, ctx.psB
    bA, bB = ctx.psA_b, ctx.psB_b
    ones1 = ctx.ones1

    for t in range(NT):
        def gmk(ns, t=t):
            inst = None
            for dt in range(ND):
                inst = ns.matmul(psA[:, 0:n_exp], xnT[:, dt, t * 128:(t + 1) * 128],
                                 gate_w[:, dt, :], start=(dt == 0), stop=not has_br)
            if has_br:
                inst = ns.matmul(psA[:, 0:n_exp], ones1[:, 0:128], br_row,
                                 start=False, stop=True)
            return inst
        P.do("tensor", gmk, reads=[bxnT, bgate, ctx.bones1] + ([bbr] if has_br else []),
             writes=[bA[0]])
        P.do("vector", lambda ns, t=t: ns.tensor_scalar(
            out=gsc[:, 0, :], in0=psA[:, 0:n_exp], scalar1=1.0, scalar2=None,
            op0=ALU.mult), reads=[bA[0]], writes=[bgsc])
        P.do("vector", lambda ns: ns.tensor_reduce(
            out=gsc[:, 1, 0:1], in_=gsc[:, 0, :], axis=AX.X, op=ALU.max,
            negate=True), reads=[bgsc], writes=[bgsc])
        P.do("vector", lambda ns: ns.tensor_scalar(
            out=gsc[:, 2, :], in0=gsc[:, 0, :], scalar1=gsc[:, 1, 0:1],
            scalar2=None, op0=ALU.add), reads=[bgsc], writes=[bgsc])
        P.do("vector", lambda ns: ns.tensor_scalar(
            out=gsc[:, 3, :], in0=gsc[:, 2, :], scalar1=0.0, scalar2=-1e30,
            op0=ALU.is_equal, op1=ALU.mult), reads=[bgsc], writes=[bgsc])
        P.do("vector", lambda ns: ns.tensor_tensor(
            out=gsc[:, 3, :], in0=gsc[:, 2, :], in1=gsc[:, 3, :], op=ALU.add),
            reads=[bgsc], writes=[bgsc])
        P.do("vector", lambda ns: ns.tensor_reduce(
            out=gsc[:, 1, 1:2], in_=gsc[:, 3, :], axis=AX.X, op=ALU.max),
            reads=[bgsc], writes=[bgsc])
        P.do("scalar", lambda ns: ns.activation(
            out=gsc[:, 4, :], in_=gsc[:, 2, :], func=AF.Exp),
            reads=[bgsc], writes=[bgsc])
        P.do("vector", lambda ns: ns.tensor_scalar(
            out=gsc[:, 5, :], in0=gsc[:, 2, :], scalar1=gsc[:, 1, 1:2],
            scalar2=None, op0=ALU.is_ge), reads=[bgsc], writes=[bgsc])
        P.do("vector", lambda ns: ns.tensor_tensor(
            out=gsc[:, 4, :], in0=gsc[:, 4, :], in1=gsc[:, 5, :], op=ALU.mult),
            reads=[bgsc], writes=[bgsc])
        P.do("vector", lambda ns: ns.tensor_reduce(
            out=gsc[:, 1, 2:3], in_=gsc[:, 4, :], axis=AX.X, op=ALU.add),
            reads=[bgsc], writes=[bgsc])
        P.do("vector", lambda ns: ns.reciprocal(
            out=gsc[:, 1, 3:4], in_=gsc[:, 1, 2:3]), reads=[bgsc], writes=[bgsc])
        P.do("vector", lambda ns, t=t: ns.tensor_scalar(
            out=wgt[:, t, :], in0=gsc[:, 4, :], scalar1=gsc[:, 1, 3:4],
            scalar2=None, op0=ALU.mult), reads=[bgsc], writes=[bwgt])

    w1_hbm = inp[names["w1"]]
    w2_hbm = inp[names["w2"]]
    b1_hbm = inp[names["b1"]]

    for e in range(n_exp):
        P.dma("sync", f"moeb1_{lid}", lambda ns, e=e: ns.dma_start(
            out=b1stage, in_=b1_hbm[e].rearrange("(t p) -> t p", p=128)),
            writes=[bb1s])
        P.do("tensor", lambda ns: ns.transpose(
            ctx.psA[0:128, 1536:1568], b1stage, ctx.eye_f32[0:32, 0:32]),
            reads=[bb1s, ctx.beye_f32], writes=[ctx.psA_b[3]])
        P.do("vector", lambda ns: ns.tensor_copy(
            out=b1sb, in_=ctx.psA[:, 1536:1568]),
            reads=[ctx.psA_b[3]], writes=[bb1])
        for tc in range(2):
            for fc in range(8):
                wb, bwb = w1v[fc % 2], bw1[fc % 2]
                P.dma("gpsimd", f"moew1_{lid}{fc % 2}", lambda ns, e=e, fc=fc, wb=wb: ns.dma_start(
                    out=wb, in_=w1_hbm[e, :, fc * 512:(fc + 1) * 512].rearrange(
                        "(dt p) f -> p dt f", p=128), max_dma_last_dim=2048),
                    writes=[bwb])
                for fi in range(4):
                    f = fc * 4 + fi
                    pr = (f % 2) * 1024
                    bpr = [bB[(f % 2) * 2], bB[(f % 2) * 2 + 1]]
                    def hmk(ns, wb=wb, fi=fi, tc=tc, pr=pr):
                        inst = None
                        for dt in range(8):
                            inst = mm_chunks(
                                ns, lambda o, w, pr=pr: psB[:, pr + o:pr + o + w],
                                wb[:, dt, fi * 128:(fi + 1) * 128],
                                lambda o, w, tc=tc, dt=dt: xnT[:, dt, tc * 1024 + o:
                                                               tc * 1024 + o + w],
                                1024, start=(dt == 0), stop=(dt == 7))
                        return inst
                    P.do("tensor", hmk, reads=[bwb, bxnT], writes=bpr)
                    P.do("scalar", lambda ns, f=f, pr=pr: ns.activation(
                        out=hT[:, f, :], in_=psB[:, pr:pr + 1024],
                        func=act_fn, bias=b1sb[:, f:f + 1]),
                        reads=bpr + [bb1], writes=[bhT])
            for dh in range(2):
                for f in range(NF):
                    wb2, bwb2 = w2v[f % 2], bw2[f % 2]
                    P.dma("gpsimd", f"moew2_{lid}{f % 2}", lambda ns, e=e, f=f, dh=dh, wb2=wb2: ns.dma_start(
                        out=wb2, in_=w2_hbm[e, f * 128:(f + 1) * 128,
                                            dh * 512:(dh + 1) * 512],
                        max_dma_last_dim=2048), writes=[bwb2])
                    def ymk(ns, f=f, wb2=wb2):
                        inst = None
                        for tt in range(8):
                            ps = psA if tt < 4 else psB
                            inst = ns.matmul(
                                ps[:, (tt % 4) * 512:(tt % 4 + 1) * 512],
                                hT[:, f, tt * 128:(tt + 1) * 128],
                                wb2, start=(f == 0), stop=(f == NF - 1))
                        return inst
                    P.do("tensor", ymk, reads=[bwb2, bhT], writes=bA + bB)
                for tt in range(8):
                    t = tc * 8 + tt
                    ps = psA if tt < 4 else psB
                    bp = bA[tt % 4] if tt < 4 else bB[tt % 4]
                    P.do("vector", lambda ns, t=t, tt=tt, dh=dh, e=e, ps=ps: ns.scalar_tensor_tensor(
                        out=x[:, t, dh * 512:(dh + 1) * 512],
                        in0=ps[:, (tt % 4) * 512:(tt % 4 + 1) * 512],
                        scalar=wgt[:, t, e:e + 1],
                        in1=x[:, t, dh * 512:(dh + 1) * 512],
                        op0=ALU.mult, op1=ALU.add),
                        reads=[bp, bwgt, bx], writes=[bx])

    eye = ctx.eye_f32
    for t in range(NT):
        bank = t % 4
        P.do("tensor", lambda ns, t=t, bank=bank: ns.transpose(
            psA[0:n_exp, bank * 512:bank * 512 + 128],
            wgt[:, t, :], eye[:]), reads=[bwgt, ctx.beye_f32], writes=[bA[bank]])
        P.do("vector", lambda ns, t=t, bank=bank: ns.tensor_copy(
            out=cT[:, t * 128:(t + 1) * 128],
            in_=psA[0:n_exp, bank * 512:bank * 512 + 128]),
            reads=[bA[bank]], writes=[bcT])
    for t in range(NT):
        ps, bps = (psA, bA[0:2]) if t % 2 == 0 else (psB, bB[0:2])
        def bmk(ns, t=t, ps=ps):
            return mm_chunks(ns, lambda o, w: ps[:, o:o + w],
                             cT[:, t * 128:(t + 1) * 128],
                             lambda o, w: b2all[:, o:o + w],
                             D, start=True, stop=True)
        P.do("tensor", bmk, reads=[bcT, bb2], writes=bps)
        P.do("vector", lambda ns, t=t, ps=ps: ns.tensor_tensor(
            out=x[:, t, :], in0=ps[:, 0:1024], in1=x[:, t, :], op=ALU.add),
            reads=bps + [bx], writes=[bx])


def emit_l1_attn(ctx):
    nc, P, st = ctx.nc, ctx.P, ctx.st
    xnT, bxnT = ctx.xnT, ctx.bxnT
    x, bx = ctx.x, ctx.bx
    inp = ctx.inp
    psA, psB = ctx.psA, ctx.psB
    bA, bB = ctx.psA_b, ctx.psB_b
    psAb = psA[:].bitcast(BF16)

    xif = ctx.xi[:].rearrange("p a b -> p (a b)")
    xib = xif.bitcast(BF16)
    oT = xib[:, 0:16384].rearrange("p (dt t) -> p dt t", dt=8)
    qh = xib[0:64, 16384:18432]
    kh = xib[0:64, 18432:20480]
    sq = xib[0:64, 20480:22528]
    Ut = [xib[:, 22528:24576], xib[:, 24576:26624]]
    Osb = xib[0:65, 26624:28672]
    Vext = xib[:, 28672:29712].rearrange("p (t c) -> p t c", t=16)
    rowbuf = xib[0:1, 29712:31760]
    boT, bqh, bkh, bsq, bO, bV, brow = (Buf("oT"), Buf("qh1"), Buf("kh1"),
                                        Buf("sq1"), Buf("Os1"), Buf("Vx1"),
                                        Buf("row1"))
    bUt = [Buf("U0a"), Buf("U1a")]
    fork_arena(ctx, "xi", boT, bqh, bkh, bsq, bUt[0], bUt[1], bO, bV, brow)

    wst = st.enter_context(nc.sbuf_tensor("l1wst", [64, 1024], BF16))
    bwst = Buf("l1wst")
    wT = st.enter_context(nc.sbuf_tensor("l1wT", [128, 3, 8, 64], BF16))
    bwT = Buf("l1wT")
    bst = st.enter_context(nc.sbuf_tensor("l1bst", [16, 3, 64], F32))
    bbst = Buf("l1bst")
    bqT = st.enter_context(nc.sbuf_tensor("l1bqT", [64, 3, 16], F32))
    bbqT = Buf("l1bqT")
    ccol = st.enter_context(nc.sbuf_tensor("l1cc", [128, 1], F32))
    bcc = Buf("l1cc")
    crow = st.enter_context(nc.sbuf_tensor("l1cr", [1, 4], F32))
    bcr = Buf("l1cr")

    for i, nm in enumerate(["L1_attn_bq", "L1_attn_bk", "L1_attn_bv"]):
        P.dma("sync", "l1bst", lambda ns, i=i, nm=nm: ns.dma_start(
            out=bst[:, i, :], in_=inp[nm]), writes=[bbst])
        P.do("tensor", lambda ns, i=i: ns.transpose(
            psA[0:64, 1536:1552], bst[:, i, :], ctx.eye_f32[0:16, 0:16]),
            reads=[bbst, ctx.beye_f32], writes=[bA[3]])
        P.do("vector", lambda ns, i=i: ns.tensor_copy(
            out=bqT[:, i, :], in_=psA[0:64, 1536:1552]),
            reads=[bA[3]], writes=[bbqT])
    P.do("gpsimd", lambda ns: ns.memset(Vext[:, :, 64:65], 1.0), writes=[bV])

    for h in range(16):
        for wi, wname in enumerate(["L1_attn_Wq", "L1_attn_Wk", "L1_attn_Wv"]):
            P.dma("gpsimd", "l1wst", lambda ns, wname=wname, h=h: ns.dma_start(
                out=wst[:], in_=inp[wname][h], max_dma_last_dim=4096),
                writes=[bwst])
            def trk(ns):
                inst = None
                for et in range(8):
                    inst = ns.transpose(
                        psAb[:, 1024 + et * 64:1024 + (et + 1) * 64],
                        wst[:, et * 128:(et + 1) * 128], ctx.eye_bf[0:64, 0:64])
                return inst
            P.do("tensor", trk, reads=[bwst, ctx.beye_bf], writes=[bA[0], bA[1]])
            P.do("vector", lambda ns, wi=wi: ns.tensor_copy(
                out=wT[:, wi, :, :], in_=psAb[:, 1024:1536].rearrange(
                    "p (et d) -> p et d", et=8)),
                reads=[bA[0], bA[1]], writes=[bwT])
        for wi, dst, bdst, ci in [(0, qh, bqh, 0), (1, kh, bkh, 1)]:
            def pmk(ns, wi=wi):
                inst = None
                for et in range(8):
                    inst = mm_chunks(ns, lambda o, w: psB[0:64, o:o + w],
                                     wT[:, wi, et, :],
                                     lambda o, w: xnT[:, et, o:o + w], T_TOK,
                                     start=(et == 0), stop=(et == 7))
                return inst
            P.do("tensor", pmk, reads=[bwT, bxnT], writes=bB)
            P.do("scalar", lambda ns, dst=dst, wi=wi, h=h: ns.activation(
                out=dst, in_=psB[0:64, 0:T_TOK], func=AF.Identity,
                bias=bqT[:, wi, h:h + 1]),
                reads=bB + [bbqT], writes=[bdst])
            P.do("scalar", lambda ns, dst=dst: ns.activation(
                out=sq, in_=dst, func=AF.Square), reads=[bdst], writes=[bsq])
            def ssk(ns):
                return mm_chunks(ns, lambda o, w: psA[0:1, o:o + w],
                                 ctx.onesc_b[0:64, :],
                                 lambda o, w: sq[:, o:o + w], T_TOK,
                                 start=True, stop=True)
            P.do("tensor", ssk, reads=[bsq, ctx.bonesc_b], writes=bA)
            P.do("scalar", lambda ns: ns.activation(
                out=rowbuf, in_=psA[0:1, 0:T_TOK], func=AF.Sqrt),
                reads=bA, writes=[brow])
            P.do("vector", lambda ns, ci=ci: ns.tensor_reduce(
                out=crow[:, ci:ci + 1], in_=rowbuf, axis=AX.X, op=ALU.max),
                reads=[brow], writes=[bcr])
        P.do("vector", lambda ns: ns.tensor_tensor(
            out=crow[:, 2:3], in0=crow[:, 0:1], in1=crow[:, 1:2], op=ALU.mult),
            reads=[bcr], writes=[bcr])
        P.do("vector", lambda ns: ns.tensor_scalar(
            out=crow[:, 2:3], in0=crow[:, 2:3], scalar1=-0.125, scalar2=None,
            op0=ALU.mult), reads=[bcr], writes=[bcr])
        P.do("tensor", lambda ns: ns.matmul(
            psA[:, 0:1], ctx.onesr_f[:], crow[:, 2:3], start=True, stop=True),
            reads=[bcr, ctx.bonesr_f], writes=[bA[0]])
        P.do("vector", lambda ns: ns.tensor_copy(out=ccol[:, 0:1], in_=psA[:, 0:1]),
            reads=[bA[0]], writes=[bcc])
        def vmk(ns):
            inst = None
            for et in range(8):
                inst = mm_chunks(ns, lambda o, w: psB[0:64, o:o + w],
                                 wT[:, 2, et, :],
                                 lambda o, w: xnT[:, et, o:o + w], T_TOK,
                                 start=(et == 0), stop=(et == 7))
            return inst
        P.do("tensor", vmk, reads=[bwT, bxnT], writes=bB)
        P.do("scalar", lambda ns, h=h: ns.activation(
            out=sq, in_=psB[0:64, 0:T_TOK], func=AF.Identity,
            bias=bqT[:, 2, h:h + 1]), reads=bB + [bbqT], writes=[bsq])
        def vtk(ns):
            inst = None
            for tt in range(16):
                inst = ns.transpose(psAb[:, tt * 64:(tt + 1) * 64],
                                    sq[:, tt * 128:(tt + 1) * 128],
                                    ctx.eye_bf[0:64, 0:64])
            return inst
        P.do("tensor", vtk, reads=[bsq, ctx.beye_bf], writes=[bA[0], bA[1]])
        P.do("vector", lambda ns: ns.tensor_copy(
            out=Vext[:, :, 0:64], in_=psAb[:, 0:1024].rearrange(
                "p (tt d) -> p tt d", tt=16)),
            reads=[bA[0], bA[1]], writes=[bV])
        for kt in range(16):
            ub, bub = Ut[kt % 2], bUt[kt % 2]
            def smk(ns, kt=kt):
                return mm_chunks(ns, lambda o, w: psA[:, o:o + w],
                                 kh[:, kt * 128:(kt + 1) * 128],
                                 lambda o, w: qh[:, o:o + w], T_TOK,
                                 start=True, stop=True)
            P.do("tensor", smk, reads=[bkh, bqh], writes=bA)
            P.do("scalar", lambda ns, ub=ub: ns.activation(
                out=ub, in_=psA[:, 0:T_TOK], func=AF.Exp,
                bias=ccol[:, 0:1], scale=0.125),
                reads=bA + [bcc], writes=[bub])
            def pvk(ns, kt=kt, ub=ub):
                return mm_chunks(ns, lambda o, w: psB[0:65, o:o + w],
                                 Vext[:, kt, :],
                                 lambda o, w: ub[:, o:o + w], T_TOK,
                                 start=(kt == 0), stop=(kt == 15))
            P.do("tensor", pvk, reads=[bV, bub], writes=bB)
        P.do("scalar", lambda ns: ns.activation(
            out=Osb, in_=psB[0:65, 0:T_TOK], func=AF.Identity),
            reads=bB, writes=[bO])
        def recip_row(ns):
            with nc.allow_low_precision(reason="bf16 softmax normalizer"):
                return ns.reciprocal(out=rowbuf, in_=Osb[64:65, :])
        P.do("vector", recip_row, reads=[bO], writes=[brow])
        def bck(ns):
            return mm_chunks(ns, lambda o, w: psA[0:64, o:o + w],
                             ctx.ones1[:, 0:64],
                             lambda o, w: rowbuf[:, o:o + w], T_TOK,
                             start=True, stop=True)
        P.do("tensor", bck, reads=[brow, ctx.bones1], writes=bA)
        base = (h % 2) * 64
        P.do("vector", lambda ns, h=h, base=base: ns.tensor_tensor(
            out=oT[base:base + 64, h // 2, :], in0=Osb[0:64, :],
            in1=psA[0:64, 0:T_TOK], op=ALU.mult),
            reads=bA + [bO], writes=[boT])

    WoT = xib[:, 16384:24576].rearrange("p (et d) -> p et d", et=8)
    wos = xib[:, 24576:25600].rearrange("p (dt c) -> p dt c", dt=8)
    borow_ap = xib[0:1, 25600:26624]
    bWoT, bwos, bbo = Buf("WoT"), Buf("wos"), Buf("borow")
    fork_arena(ctx, "xi", boT, bWoT, bwos, bbo)
    P.dma("gpsimd", "l1bo", lambda ns: ns.dma_start(
        out=borow_ap, in_=inp["L1_attn_bo"].unsqueeze(0)), writes=[bbo])
    for et in range(8):
        P.dma("gpsimd", "l1wos", lambda ns, et=et: ns.dma_start(
            out=wos, in_=inp["L1_attn_Wo"][:, et * 128:(et + 1) * 128].rearrange(
                "(dt p) c -> p dt c", p=128), max_dma_last_dim=512),
            writes=[bwos])
        def wtk(ns):
            inst = None
            for dt in range(8):
                inst = ns.transpose(psAb[:, dt * 128:(dt + 1) * 128],
                                    wos[:, dt, :], ctx.eye_bf[:])
            return inst
        P.do("tensor", wtk, reads=[bwos, ctx.beye_bf], writes=bA)
        P.do("vector", lambda ns, et=et: ns.tensor_copy(
            out=WoT[:, et, :], in_=psAb[:, 0:1024]),
            reads=bA, writes=[bWoT])
    for tt in range(NT):
        off = 0 if tt % 2 == 0 else 1024
        bps = bA[0:2] if tt % 2 == 0 else bA[2:4]
        def dmkk(ns, tt=tt, off=off):
            inst = None
            for et in range(8):
                inst = mm_chunks(ns, lambda o, w: psA[:, off + o:off + o + w],
                                 oT[:, et, tt * 128:(tt + 1) * 128],
                                 lambda o, w: WoT[:, et, o:o + w], D,
                                 start=(et == 0), stop=False)
            inst = mm_chunks(ns, lambda o, w: psA[:, off + o:off + o + w],
                             ctx.ones1[:, 0:128],
                             lambda o, w: borow_ap[:, o:o + w], D,
                             start=False, stop=True)
            return inst
        P.do("tensor", dmkk, reads=[boT, bWoT, bbo, ctx.bones1], writes=bps)
        P.do("vector", lambda ns, tt=tt, off=off: ns.tensor_tensor(
            out=x[:, tt, :], in0=psA[:, off:off + 1024], in1=x[:, tt, :],
            op=ALU.add), reads=bps + [bx], writes=[bx])


def emit_l1_ffn(ctx):
    nc, P, st = ctx.nc, ctx.P, ctx.st
    xnT, bxnT = ctx.xnT, ctx.bxnT
    inp = ctx.inp
    psA, psB = ctx.psA, ctx.psB
    bA, bB = ctx.psA_b, ctx.psB_b
    psAb = psA[:].bitcast(BF16)
    psBb = psB[:].bitcast(BF16)
    xi, bxi = ctx.xi, ctx.bxi
    NBLK = HIDP // 128

    spill_rec = P.dma("sync", "l1spill", lambda ns: ns.dma_start(
        out=inp["x_spill"], in_=ctx.x[:]), reads=[ctx.bx])

    xbf = ctx.x[:].rearrange("p a b -> p (a b)").bitcast(BF16)
    shiftb = xbf[:, 0:8192].rearrange("p (dt t) -> p dt t", dt=4)
    rtab = [xbf[:, 8192 + i * 1024:8192 + (i + 1) * 1024].rearrange(
        "p (t d) -> p t d", t=16) for i in range(4)]
    wqst = xbf[:, 12288:13824].rearrange("p (dt c) -> p dt c", dt=8)
    qpT = xbf[0:64, 13824:15872]
    kpT = xbf[0:64, 15872:17920]
    tokb = xbf[:, 17920:18944].rearrange("p (t d) -> p t d", t=16)
    rot1 = xbf[:, 18944:19968].rearrange("p (t d) -> p t d", t=16)
    vtok = xbf[:, 19968:20992].rearrange("p (t d) -> p t d", t=16)
    sqtok = xbf[:, 20992:23040]
    Ubuf = xbf[:, 23040:23680]
    UTb = xbf[:, 23680:24320].rearrange("p (k c) -> p k c", k=5)
    obuf = xbf[:, 24320:24384]
    qsb = [xbf[:, 24448 + i * 64:24448 + (i + 1) * 64] for i in range(2)]
    qsrow = [xbf[0:1, 24576 + i * 64:24576 + (i + 1) * 64] for i in range(3)]
    oTi = xbf[:, 24768:28864].rearrange("p (it t) -> p it t", it=2)
    wost = xbf[:, 28864:30912].rearrange("p (it d) -> p it d", it=2)
    (bsh, brt, bwq, bqpT, bkpT, btok, brot1, bvt, bsqt, bU, bUT, bob,
     bqs, boTi, bwos2) = [Buf(f"l1f{i}") for i in range(15)]
    ctx.groups["x"] = [ctx.bx]

    nrm = st.enter_context(nc.sbuf_tensor("l1nrm", [128, 3, 16], F32))
    bnrm = Buf("l1nrm")
    scol = st.enter_context(nc.sbuf_tensor("l1sc", [128, 4], F32))
    bscol = Buf("l1sc")
    masks = st.enter_context(nc.sbuf_tensor("l1msk", [128, 2, 128], F32))
    bmsk = Buf("l1msk")
    shm = st.enter_context(nc.sbuf_tensor("l1shm", [128, 2, 128], F32))
    bshm = Buf("l1shm")
    lnab = st.enter_context(nc.sbuf_tensor("l1ab", [128, 4, 16], F32))
    blnab = Buf("l1ab")
    gbb = st.enter_context(nc.sbuf_tensor("l1gb", [128, 2, 1024], BF16))
    bgbb = Buf("l1gb")
    b1T = st.enter_context(nc.sbuf_tensor("l1b1T", [128, 44], F32))
    bb1T = Buf("l1b1T")
    b1stg = st.enter_context(nc.sbuf_tensor("l1b1s", [44, 128], F32))
    bb1s = Buf("l1b1s")
    grow = st.enter_context(nc.sbuf_tensor("l1gr", [1, 2, 1024], BF16))
    bgrow = Buf("l1gr")
    b2row = st.enter_context(nc.sbuf_tensor("l1b2r", [1, 1024], BF16))
    bb2r = Buf("l1b2r")
    P.dma("sync", "l1msk", lambda ns: ns.dma_start(
        out=masks[:, 0, :], in_=inp["c_tri"]), writes=[bmsk])
    P.dma("sync", "l1msk", lambda ns: ns.dma_start(
        out=masks[:, 1, :], in_=inp["c_triu"]), writes=[bmsk])
    P.dma("sync", "l1shm", lambda ns: ns.dma_start(
        out=shm[:, 0, :], in_=inp["c_shift"]), writes=[bshm])
    P.dma("sync", "l1shm", lambda ns: ns.dma_start(
        out=shm[:, 1, :], in_=inp["c_bnd"]), writes=[bshm])

    fork_arena(ctx, "xi", bxi)
    for tt in range(NT):
        b01 = (tt % 2) * 2
        def itk(ns, tt=tt, b01=b01):
            inst = None
            for dt in range(8):
                inst = ns.transpose(
                    psAb[:, b01 * 1024 + dt * 128:b01 * 1024 + (dt + 1) * 128],
                    xnT[:, dt, tt * 128:(tt + 1) * 128], ctx.eye_bf[:])
            return inst
        P.do("tensor", itk, reads=[bxnT, ctx.beye_bf],
             writes=[bA[b01], bA[b01 + 1]])
        P.do("vector", lambda ns, tt=tt, b01=b01: ns.tensor_copy(
            out=xi[:, tt, :], in_=psAb[:, b01 * 1024:b01 * 1024 + 1024]),
            reads=[bA[b01], bA[b01 + 1]], writes=[bxi])

    def rebuild_shift():
        for i in range(4):
            P.do("vector", lambda ns, i=i: ns.tensor_copy(
                out=shiftb[:, i, 1:T_TOK], in_=xnT[:, 4 + i, 0:T_TOK - 1]),
                reads=[bxnT], writes=[bsh])
        P.do("gpsimd", lambda ns: ns.memset(shiftb[:, :, 0:1], 0.0),
            writes=[bsh])

    def h1T_tile(dt):
        return xnT[:, dt, :] if dt < 4 else shiftb[:, dt - 4, :]

    def regen_xinnerT():
        for tt in range(NT):
            b01 = (tt % 2) * 2
            def rtk_a(ns, tt=tt, b01=b01):
                inst = None
                for dt in range(4):
                    inst = ns.transpose(
                        psA[:, b01 * 512 + dt * 128:b01 * 512 + (dt + 1) * 128],
                        xi[:, tt, dt * 128:(dt + 1) * 128], ctx.eye_f32[:])
                return inst
            def rtk_b(ns, tt=tt, b01=b01):
                inst = None
                for dt in range(4):
                    inst = ns.transpose(
                        psB[:, b01 * 512 + dt * 128:b01 * 512 + (dt + 1) * 128],
                        xi[:, tt, (4 + dt) * 128:(5 + dt) * 128], ctx.eye_f32[:])
                return inst
            P.do("tensor", rtk_a, reads=[bxi, ctx.beye_f32],
                 writes=[bA[b01], bA[b01 + 1]])
            P.do("tensor", rtk_b, reads=[bxi, ctx.beye_f32],
                 writes=[bB[b01], bB[b01 + 1]])
            P.do("vector", lambda ns, tt=tt, b01=b01: ns.tensor_copy(
                out=xnT[:, 0:4, tt * 128:(tt + 1) * 128],
                in_=psA[:, b01 * 512:b01 * 512 + 512].rearrange(
                    "p (dt c) -> p dt c", dt=4)),
                reads=[bA[b01], bA[b01 + 1]], writes=[bxnT])
            P.do("vector", lambda ns, tt=tt, b01=b01: ns.tensor_copy(
                out=xnT[:, 4:8, tt * 128:(tt + 1) * 128],
                in_=psB[:, b01 * 512:b01 * 512 + 512].rearrange(
                    "p (dt c) -> p dt c", dt=4)),
                reads=[bB[b01], bB[b01 + 1]], writes=[bxnT])

    for il in range(2):
        fork_arena(ctx, "x", bsh, brt, bwq, bqpT, bkpT, btok, brot1, bvt,
                   bsqt, bU, bUT, bob, bqs, boTi, bwos2)
        for ti, cn in enumerate(["c_rotA", "c_rotB", "c_rotC", "c_rotD"]):
            P.dma("gpsimd", f"l1rt{ti}", lambda ns, ti=ti, cn=cn: ns.dma_start(
                out=rtab[ti][:], in_=inp[cn].rearrange("(t p) d -> p t d", p=128),
                max_dma_last_dim=256), writes=[brt])
        for si, nm in [(0, "L1_ffn_q_scale"), (1, "L1_ffn_k_scale")]:
            P.dma("gpsimd", "l1qsr", lambda ns, si=si, nm=nm, il=il: ns.dma_start(
                out=qsrow[si], in_=inp[nm][il].unsqueeze(0)), writes=[bqs])
        for si in range(2):
            P.do("vector", lambda ns, si=si: ns.tensor_copy(
                out=qsrow[2][:, 0:32], in_=qsrow[si][:, 32:64]),
                reads=[bqs], writes=[bqs])
            P.do("vector", lambda ns, si=si: ns.tensor_copy(
                out=qsrow[2][:, 32:64], in_=qsrow[si][:, 0:32]),
                reads=[bqs], writes=[bqs])
            P.do("tensor", lambda ns, si=si: ns.matmul(
                psA[:, 0:64], ctx.ones1[:, 0:128], qsrow[si],
                start=True, stop=True), reads=[bqs, ctx.bones1], writes=[bA[0]])
            P.do("vector", lambda ns, si=si: ns.tensor_copy(
                out=qsb[si], in_=psA[:, 0:64].bitcast(BF16)[:, 0:64])
                if False else ns.tensor_copy(out=qsb[si], in_=psA[:, 0:64]),
                reads=[bA[0]], writes=[bqs])
            tA = rtab[0] if si == 0 else rtab[2]
            tB = rtab[1] if si == 0 else rtab[3]
            P.do("vector", lambda ns, tA=tA, si=si: ns.tensor_tensor(
                out=tA[:], in0=tA[:],
                in1=qsb[si].unsqueeze(1).broadcast_to((128, 16, 64)),
                op=ALU.mult), reads=[brt, bqs], writes=[brt])
            P.do("tensor", lambda ns: ns.matmul(
                psA[:, 64:128], ctx.ones1[:, 0:128], qsrow[2],
                start=True, stop=True), reads=[bqs, ctx.bones1], writes=[bA[0]])
            P.do("vector", lambda ns, si=si: ns.tensor_copy(
                out=qsb[si], in_=psA[:, 64:128]), reads=[bA[0]], writes=[bqs])
            P.do("vector", lambda ns, tB=tB, si=si: ns.tensor_tensor(
                out=tB[:], in0=tB[:],
                in1=qsb[si].unsqueeze(1).broadcast_to((128, 16, 64)),
                op=ALU.mult), reads=[brt, bqs], writes=[brt])
        rebuild_shift()
        for h in range(8):
            if h % 4 == 0 and h > 0:
                emit_l1_wo_partial(ctx, il, h // 4 - 1, locals())
            for ci, coff in enumerate([h * 64, 512 + h * 64, 1024 + h * 64]):
                P.dma("gpsimd", "l1wq", lambda ns, ci=ci, coff=coff, il=il: ns.dma_start(
                    out=wqst[:, :, ci * 64:(ci + 1) * 64],
                    in_=inp["L1_ffn_wqkv"][il][:, coff:coff + 64].rearrange(
                        "(dt p) c -> p dt c", p=128), max_dma_last_dim=256),
                    writes=[bwq])
            for ci in range(3):
                def qmk(ns, ci=ci):
                    inst = None
                    for dt in range(8):
                        inst = mm_chunks(ns, lambda o, w: psB[0:64, o:o + w],
                                         wqst[:, dt, ci * 64:(ci + 1) * 64],
                                         lambda o, w: h1T_tile(dt)[:, o:o + w],
                                         T_TOK, start=(dt == 0), stop=(dt == 7))
                    return inst
                P.do("tensor", qmk, reads=[bwq, bxnT, bsh], writes=bB)
                P.do("scalar", lambda ns: ns.activation(
                    out=sqtok[0:64, 0:T_TOK], in_=psB[0:64, 0:T_TOK],
                    func=AF.Identity), reads=bB, writes=[bsqt])
                def t2tok(ns):
                    inst = None
                    for tt in range(16):
                        inst = ns.transpose(psAb[:, tt * 64:(tt + 1) * 64],
                                            sqtok[0:64, tt * 128:(tt + 1) * 128],
                                            ctx.eye_bf[0:64, 0:64])
                    return inst
                P.do("tensor", t2tok, reads=[bsqt, ctx.beye_bf],
                     writes=[bA[0], bA[1]])
                dst = vtok if ci == 2 else tokb
                dstb = bvt if ci == 2 else btok
                P.do("vector", lambda ns, dst=dst, dstb=dstb: ns.tensor_copy(
                    out=dst[:, :, :], in_=psAb[:, 0:1024].rearrange(
                        "p (tt d) -> p tt d", tt=16)),
                    reads=[bA[0], bA[1]], writes=[dstb])
                if ci == 2:
                    continue
                P.do("scalar", lambda ns: ns.activation(
                    out=sqtok[:, 0:1024],
                    in_=tokb[:].rearrange("p t d -> p (t d)"),
                    func=AF.Square), reads=[btok], writes=[bsqt])
                P.do("vector", lambda ns: ns.tensor_reduce(
                    out=nrm[:, 0, :], in_=sqtok[:, 0:1024].rearrange(
                        "p (t d) -> p t d", t=16), axis=AX.X, op=ALU.add),
                    reads=[bsqt], writes=[bnrm])
                P.do("scalar", lambda ns: ns.activation(
                    out=nrm[:, 1, :], in_=nrm[:, 0, :], func=AF.Sqrt),
                    reads=[bnrm], writes=[bnrm])
                P.do("vector", lambda ns: ns.tensor_scalar(
                    out=nrm[:, 1, :], in0=nrm[:, 1, :], scalar1=1.0e-12,
                    scalar2=None, op0=ALU.max), reads=[bnrm], writes=[bnrm])
                P.do("vector", lambda ns: ns.reciprocal(
                    out=nrm[:, 2, :], in_=nrm[:, 1, :]),
                    reads=[bnrm], writes=[bnrm])
                P.do("vector", lambda ns: ns.tensor_tensor(
                    out=tokb[:], in0=tokb[:],
                    in1=nrm[:, 2, :].unsqueeze(2).broadcast_to((128, 16, 64)),
                    op=ALU.mult), reads=[bnrm, btok], writes=[btok])
                tA = rtab[0] if ci == 0 else rtab[2]
                tB = rtab[1] if ci == 0 else rtab[3]
                P.do("vector", lambda ns: ns.tensor_copy(
                    out=rot1[:, :, 0:32], in_=tokb[:, :, 32:64]),
                    reads=[btok], writes=[brot1])
                P.do("vector", lambda ns: ns.tensor_copy(
                    out=rot1[:, :, 32:64], in_=tokb[:, :, 0:32]),
                    reads=[btok], writes=[brot1])
                P.do("vector", lambda ns, tB=tB: ns.tensor_tensor(
                    out=rot1[:], in0=rot1[:], in1=tB[:], op=ALU.mult),
                    reads=[brot1, brt], writes=[brot1])
                P.do("vector", lambda ns, tA=tA: ns.tensor_tensor(
                    out=tokb[:], in0=tokb[:], in1=tA[:], op=ALU.mult),
                    reads=[btok, brt], writes=[btok])
                P.do("vector", lambda ns: ns.tensor_tensor(
                    out=tokb[:], in0=tokb[:], in1=rot1[:], op=ALU.add),
                    reads=[btok, brot1], writes=[btok])
                def t2fm(ns):
                    inst = None
                    for tt in range(16):
                        inst = ns.transpose(
                            psAb[0:64, 2048 + tt * 128:2048 + (tt + 1) * 128],
                            tokb[:, tt, :], ctx.eye_bf[:])
                    return inst
                P.do("tensor", t2fm, reads=[btok, ctx.beye_bf],
                     writes=[bA[2], bA[3]])
                dst2, bdst2 = (qpT, bqpT) if ci == 0 else (kpT, bkpT)
                P.do("vector", lambda ns, dst2=dst2: ns.tensor_copy(
                    out=dst2, in_=psAb[0:64, 2048:4096]),
                    reads=[bA[2], bA[3]], writes=[bdst2])
            for qc in range(16):
                k0 = max(0, qc - 4)
                width = (qc + 1 - k0) * 128
                def smk(ns, qc=qc, k0=k0, width=width):
                    return mm_chunks(ns, lambda o, w: psA[:, o:o + w],
                                     qpT[:, qc * 128:(qc + 1) * 128],
                                     lambda o, w: kpT[:, k0 * 128 + o:k0 * 128 + o + w],
                                     width, start=True, stop=True)
                P.do("tensor", smk, reads=[bqpT, bkpT], writes=bA[0:2])
                P.do("vector", lambda ns, width=width: ns.tensor_tensor(
                    out=psA[:, width - 128:width], in0=psA[:, width - 128:width],
                    in1=masks[:, 0, :], op=ALU.add),
                    reads=[bmsk, bA[0], bA[1]], writes=[bA[0], bA[1]])
                if qc >= 4:
                    P.do("vector", lambda ns: ns.tensor_tensor(
                        out=psA[:, 0:128], in0=psA[:, 0:128],
                        in1=masks[:, 1, :], op=ALU.add),
                        reads=[bmsk, bA[0]], writes=[bA[0]])
                P.do("vector", lambda ns, width=width: ns.tensor_reduce(
                    out=scol[:, 0:1], in_=psA[:, 0:width], axis=AX.X,
                    op=ALU.max, negate=True), reads=bA[0:2], writes=[bscol])
                P.do("vector", lambda ns: ns.tensor_scalar(
                    out=scol[:, 1:2], in0=scol[:, 0:1], scalar1=8.0,
                    scalar2=None, op0=ALU.mult), reads=[bscol], writes=[bscol])
                P.do("scalar", lambda ns, width=width: ns.activation(
                    out=Ubuf[:, 0:width], in_=psA[:, 0:width], func=AF.Exp,
                    bias=scol[:, 1:2], scale=8.0, accum_out=scol[:, 2:3]),
                    reads=bA[0:2] + [bscol], writes=[bU, bscol])
                nk = width // 128
                def utk(ns, nk=nk):
                    inst = None
                    for j in range(nk):
                        inst = ns.transpose(
                            psBb[:, 2048 + j * 128:2048 + (j + 1) * 128],
                            Ubuf[:, j * 128:(j + 1) * 128], ctx.eye_bf[:])
                    return inst
                P.do("tensor", utk, reads=[bU, ctx.beye_bf],
                     writes=[bB[2], bB[3]])
                P.do("vector", lambda ns, nk=nk: ns.tensor_copy(
                    out=UTb[:, 0:nk, :],
                    in_=psBb[:, 2048:2048 + nk * 128].rearrange(
                        "p (k c) -> p k c", k=nk)),
                    reads=[bB[2], bB[3]], writes=[bUT])
                def pvk(ns, nk=nk, k0=k0):
                    inst = None
                    for j in range(nk):
                        inst = ns.matmul(psB[:, 0:64], UTb[:, j, :],
                                         vtok[:, k0 + j, :],
                                         start=(j == 0), stop=(j == nk - 1))
                    return inst
                P.do("tensor", pvk, reads=[bUT, bvt], writes=[bB[0]])
                P.do("vector", lambda ns: ns.reciprocal(
                    out=scol[:, 3:4], in_=scol[:, 2:3]),
                    reads=[bscol], writes=[bscol])
                P.do("scalar", lambda ns: ns.activation(
                    out=obuf, in_=psB[:, 0:64], func=AF.Identity,
                    scale=scol[:, 3:4]), reads=[bB[0], bscol], writes=[bob])
                base = (h % 2) * 64
                P.do("tensor", lambda ns: ns.transpose(
                    psBb[0:64, 1024:1152], obuf, ctx.eye_bf[:]),
                    reads=[bob, ctx.beye_bf], writes=[bB[1]])
                P.do("vector", lambda ns, h=h, qc=qc, base=base: ns.tensor_copy(
                    out=oTi[base:base + 64, (h % 4) // 2, qc * 128:(qc + 1) * 128],
                    in_=psBb[0:64, 1024:1152]),
                    reads=[bB[1]], writes=[boTi])
        emit_l1_wo_partial(ctx, il, 1, locals())
        regen_xinnerT()
        emit_l1_inner_mlp(ctx, il, locals())
        if il == 0:
            regen_xinnerT()

    bx2 = Buf("x2")
    fork_arena(ctx, "x", bx2)
    P.dma("sync", "l1xrld", lambda ns: ns.dma_start(
        out=ctx.x[:], in_=inp["x_spill"]), writes=[bx2])
    ctx.bx = bx2
    for tt in range(NT):
        P.do("vector", lambda ns, tt=tt: ns.tensor_tensor(
            out=ctx.x[:, tt, :], in0=xi[:, tt, :], in1=ctx.x[:, tt, :],
            op=ALU.add), reads=[bxi, bx2], writes=[bx2])


def emit_l1_wo_partial(ctx, il, grp, env):
    P = ctx.P
    inp = ctx.inp
    psA = ctx.psA
    bA = ctx.psA_b
    xi, bxi = ctx.xi, ctx.bxi
    oTi, boTi = env["oTi"], env["boTi"]
    wost, bwos2 = env["wost"], env["bwos2"]
    for it in range(2):
        P.dma("gpsimd", "l1wo", lambda ns, it=it, grp=grp, il=il: ns.dma_start(
            out=wost[:, it, :],
            in_=inp["L1_ffn_wo"][il][grp * 256 + it * 128:grp * 256 + (it + 1) * 128, :],
            max_dma_last_dim=4096), writes=[bwos2])
    for tt in range(NT):
        off = 0 if tt % 2 == 0 else 1024
        bps = bA[0:2] if tt % 2 == 0 else bA[2:4]
        def dmk2(ns, tt=tt, off=off):
            inst = None
            for it in range(2):
                inst = mm_chunks(ns, lambda o, w: psA[:, off + o:off + o + w],
                                 oTi[:, it, tt * 128:(tt + 1) * 128],
                                 lambda o, w: wost[:, it, o:o + w], D,
                                 start=(it == 0), stop=(it == 1))
            return inst
        P.do("tensor", dmk2, reads=[boTi, bwos2], writes=bps)
        P.do("vector", lambda ns, tt=tt, off=off: ns.tensor_tensor(
            out=xi[:, tt, :], in0=psA[:, off:off + 1024], in1=xi[:, tt, :],
            op=ALU.add), reads=bps + [bxi], writes=[bxi])


def emit_l1_inner_mlp(ctx, il, env):
    nc, P, st = ctx.nc, ctx.P, ctx.st
    inp = ctx.inp
    psA, psB = ctx.psA, ctx.psB
    psAb = psA[:].bitcast(BF16)
    bA, bB = ctx.psA_b, ctx.psB_b
    xi, bxi = ctx.xi, ctx.bxi
    xnT, bxnT = ctx.xnT, ctx.bxnT
    NBLK = HIDP // 128
    shm, bshm = env["shm"], env["bshm"]
    lnab, blnab = env["lnab"], env["blnab"]
    gbb, bgbb = env["gbb"], env["bgbb"]
    b1T, bb1T = env["b1T"], env["bb1T"]
    b1stg, bb1s = env["b1stg"], env["bb1s"]
    grow, bgrow = env["grow"], env["bgrow"]
    b2row, bb2r = env["b2row"], env["bb2r"]
    nrm, bnrm = env["nrm"], env["bnrm"]
    xbf = ctx.x[:].rearrange("p a b -> p (a b)").bitcast(BF16)
    hnT = xbf[:, 0:4096].rearrange("p (dt t) -> p dt t", dt=8)
    hpT = xbf[:, 4096:15360].rearrange("p (bk t) -> p bk t", bk=NBLK)
    w1st = [xbf[:, 15360 + i * 2048:15360 + (i + 1) * 2048].rearrange(
        "p (dt c) -> p dt c", dt=8) for i in range(2)]
    w2st = [xbf[:, 19456 + i * 1024:19456 + (i + 1) * 1024] for i in range(2)]
    hntile = xbf[:, 21504:23552]
    gelus = xbf[:, 23552:24064]
    bhn, bhp, bhnt, bgls = Buf("hnT"), Buf("hpT"), Buf("hnt"), Buf("gls")
    bw1 = [Buf("w1s0"), Buf("w1s1")]
    bw2 = [Buf("w2s0"), Buf("w2s1")]
    fork_arena(ctx, "x", bhn, bhp, bhnt, bgls, bw1[0], bw1[1], bw2[0], bw2[1])

    for gi, gnm in [(0, "L1_ffn_ln_g"), (1, "L1_ffn_ln_b")]:
        P.dma("gpsimd", "l1grow", lambda ns, gi=gi, gnm=gnm: ns.dma_start(
            out=grow[:, gi, :], in_=inp[gnm][il].unsqueeze(0)), writes=[bgrow])
        def gmk(ns, gi=gi):
            return mm_chunks(ns, lambda o, w: psA[:, o:o + w],
                             ctx.ones1[:, 0:128],
                             lambda o, w: grow[:, gi, o:o + w], D,
                             start=True, stop=True)
        P.do("tensor", gmk, reads=[bgrow, ctx.bones1], writes=bA[0:2])
        P.do("vector", lambda ns, gi=gi: ns.tensor_copy(
            out=gbb[:, gi, :], in_=psA[:, 0:1024]),
            reads=bA[0:2], writes=[bgbb])
    P.dma("sync", "l1b1s", lambda ns: ns.dma_start(
        out=b1stg[:], in_=inp["L1b1p"][il].rearrange("(a p) -> a p", p=128)),
        writes=[bb1s])
    P.do("tensor", lambda ns: ns.transpose(
        psA[:, 1536:1580], b1stg[:], ctx.eye_f32[0:44, 0:44]),
        reads=[bb1s, ctx.beye_f32], writes=[bA[3]])
    P.do("vector", lambda ns: ns.tensor_copy(
        out=b1T[:], in_=psA[:, 1536:1580]), reads=[bA[3]], writes=[bb1T])
    P.dma("gpsimd", "l1b2r", lambda ns: ns.dma_start(
        out=b2row[:], in_=inp["L1_ffn_b2"][il].unsqueeze(0)), writes=[bb2r])

    for tt in range(NT):
        P.do("vector", lambda ns, tt=tt: ns.tensor_reduce(
            out=lnab[:, 0, tt:tt + 1], in_=xi[:, tt, 0:512], axis=AX.X,
            op=ALU.add), reads=[bxi], writes=[blnab])
        P.do("vector", lambda ns, tt=tt: ns.tensor_reduce(
            out=lnab[:, 1, tt:tt + 1], in_=xi[:, tt, 512:1024], axis=AX.X,
            op=ALU.add), reads=[bxi], writes=[blnab])
        bank = 2 + (tt % 2)
        P.do("scalar", lambda ns, tt=tt, bank=bank: ns.activation(
            out=psB[:, bank * 512:bank * 512 + 512], in_=xi[:, tt, 0:512],
            func=AF.Square, accum_out=lnab[:, 2, tt:tt + 1]),
            reads=[bxi], writes=[bB[bank], blnab])
        P.do("scalar", lambda ns, tt=tt, bank=bank: ns.activation(
            out=psB[:, bank * 512:bank * 512 + 512], in_=xi[:, tt, 512:1024],
            func=AF.Square, accum_out=lnab[:, 3, tt:tt + 1]),
            reads=[bxi], writes=[bB[bank], blnab])
    for q_i, src_slot in [(0, 1), (1, 3)]:
        for tt in range(NT):
            def shk(ns, tt=tt, src_slot=src_slot, q_i=q_i):
                inst = ns.matmul(psA[:, 1792 + q_i:1793 + q_i],
                                 shm[:, 0, :], lnab[:, src_slot, tt:tt + 1],
                                 start=True, stop=(tt == 0))
                if tt > 0:
                    inst = ns.matmul(psA[:, 1792 + q_i:1793 + q_i],
                                     shm[:, 1, :], lnab[:, src_slot, tt - 1:tt],
                                     start=False, stop=True)
                return inst
            P.do("tensor", shk, reads=[bshm, blnab], writes=[bA[3]])
            P.do("vector", lambda ns, tt=tt, q_i=q_i: ns.tensor_copy(
                out=nrm[:, q_i, tt:tt + 1], in_=psA[:, 1792 + q_i:1793 + q_i]),
                reads=[bA[3]], writes=[bnrm])
    P.do("vector", lambda ns: ns.tensor_tensor(
        out=lnab[:, 0, :], in0=lnab[:, 0, :], in1=nrm[:, 0, :], op=ALU.add),
        reads=[blnab, bnrm], writes=[blnab])
    P.do("vector", lambda ns: ns.tensor_tensor(
        out=lnab[:, 2, :], in0=lnab[:, 2, :], in1=nrm[:, 1, :], op=ALU.add),
        reads=[blnab, bnrm], writes=[blnab])
    P.do("vector", lambda ns: ns.tensor_scalar(
        out=lnab[:, 0, :], in0=lnab[:, 0, :], scalar1=1.0 / D, scalar2=None,
        op0=ALU.mult), reads=[blnab], writes=[blnab])
    P.do("vector", lambda ns: ns.scalar_tensor_tensor(
        out=lnab[:, 1, :], in0=lnab[:, 0, :], scalar=-1.0, in1=lnab[:, 0, :],
        op0=ALU.mult, op1=ALU.mult), reads=[blnab], writes=[blnab])
    P.do("vector", lambda ns: ns.scalar_tensor_tensor(
        out=lnab[:, 1, :], in0=lnab[:, 2, :], scalar=1.0 / D, in1=lnab[:, 1, :],
        op0=ALU.mult, op1=ALU.add), reads=[blnab], writes=[blnab])
    P.do("vector", lambda ns: ns.tensor_scalar(
        out=lnab[:, 1, :], in0=lnab[:, 1, :], scalar1=1.0e-5, scalar2=None,
        op0=ALU.add), reads=[blnab], writes=[blnab])
    P.do("scalar", lambda ns: ns.activation(
        out=lnab[:, 1, :], in_=lnab[:, 1, :], func=AF.Sqrt),
        reads=[blnab], writes=[blnab])
    P.do("vector", lambda ns: ns.reciprocal(
        out=lnab[:, 1, :], in_=lnab[:, 1, :]), reads=[blnab], writes=[blnab])
    P.do("vector", lambda ns: ns.scalar_tensor_tensor(
        out=lnab[:, 0, :], in0=lnab[:, 0, :], scalar=-1.0, in1=lnab[:, 1, :],
        op0=ALU.mult, op1=ALU.mult), reads=[blnab], writes=[blnab])

    def build_hn_tile(tt):
        def shtile(ns, tt=tt):
            inst = None
            inst = ns.matmul(psB[:, 0:512], shm[:, 0, :], xi[:, tt, 512:1024],
                             start=True, stop=(tt == 0))
            if tt > 0:
                inst = ns.matmul(psB[:, 0:512], shm[:, 1, :],
                                 xi[:, tt - 1, 512:1024], start=False, stop=True)
            return inst
        P.do("tensor", shtile, reads=[bshm, bxi], writes=[bB[0]])
        P.do("vector", lambda ns, tt=tt: ns.tensor_scalar(
            out=hntile[:, 0:1024], in0=xi[:, tt, 0:512],
            scalar1=lnab[:, 1, tt:tt + 1], scalar2=lnab[:, 0, tt:tt + 1],
            op0=ALU.mult, op1=ALU.add) if False else ns.tensor_scalar(
            out=hntile[:, 0:512], in0=xi[:, tt, 0:512],
            scalar1=lnab[:, 1, tt:tt + 1], scalar2=lnab[:, 0, tt:tt + 1],
            op0=ALU.mult, op1=ALU.add),
            reads=[bxi, blnab], writes=[bhnt])
        P.do("vector", lambda ns, tt=tt: ns.tensor_scalar(
            out=hntile[:, 512:1024], in0=psB[:, 0:512],
            scalar1=lnab[:, 1, tt:tt + 1], scalar2=lnab[:, 0, tt:tt + 1],
            op0=ALU.mult, op1=ALU.add),
            reads=[bB[0], blnab], writes=[bhnt])
        P.do("vector", lambda ns: ns.tensor_tensor(
            out=hntile[:, 0:1024], in0=hntile[:, 0:1024], in1=gbb[:, 0, :],
            op=ALU.mult), reads=[bhnt, bgbb], writes=[bhnt])
        P.do("vector", lambda ns: ns.tensor_tensor(
            out=hntile[:, 0:1024], in0=hntile[:, 0:1024], in1=gbb[:, 1, :],
            op=ALU.add), reads=[bhnt, bgbb], writes=[bhnt])
        b01 = (tt % 2) * 2
        def hnt_tk(ns, tt=tt, b01=b01):
            inst = None
            for dt in range(8):
                inst = ns.transpose(
                    psAb[:, b01 * 1024 + dt * 128:b01 * 1024 + (dt + 1) * 128],
                    hntile[:, dt * 128:(dt + 1) * 128], ctx.eye_bf[:])
            return inst
        P.do("tensor", hnt_tk, reads=[bhnt, ctx.beye_bf],
             writes=[bA[b01], bA[b01 + 1]])
        P.do("vector", lambda ns, tt=tt, b01=b01: ns.tensor_copy(
            out=hnT[:, :, (tt % 4) * 128:(tt % 4 + 1) * 128],
            in_=psAb[:, b01 * 1024:b01 * 1024 + 1024].rearrange(
                "p (dt c) -> p dt c", dt=8)),
            reads=[bA[b01], bA[b01 + 1]], writes=[bhn])

    for tc in range(4):
        for tt in range(tc * 4, tc * 4 + 4):
            build_hn_tile(tt)
        for blk in range(NBLK):
            wb, bwb = w1st[blk % 2], bw1[blk % 2]
            P.dma("gpsimd", f"l1w1{blk % 2}", lambda ns, blk=blk, wb=wb, il=il: ns.dma_start(
                out=wb, in_=inp["L1w1p"][il][:, blk * 256:(blk + 1) * 256].rearrange(
                    "(dt p) c -> p dt c", p=128), max_dma_last_dim=1024),
                writes=[bwb])
            r = (blk % 2) * 2
            for half in range(2):
                def hk(ns, wb=wb, half=half, tc=tc, r=r):
                    inst = None
                    for dt in range(8):
                        inst = ns.matmul(
                            psA[:, (r + half) * 512:(r + half + 1) * 512],
                            wb[:, dt, half * 128:(half + 1) * 128],
                            hnT[:, dt, 0:512],
                            start=(dt == 0), stop=(dt == 7))
                    return inst
                P.do("tensor", hk, reads=[bwb, bhn], writes=[bA[r + half]])
            P.do("scalar", lambda ns, blk=blk, r=r: ns.activation(
                out=hpT[:, blk, :], in_=psA[:, r * 512:r * 512 + 512],
                func=AF.Identity, bias=b1T[:, 2 * blk:2 * blk + 1]),
                reads=[bA[r], bb1T], writes=[bhp])
            P.do("scalar", lambda ns, blk=blk, r=r: ns.activation(
                out=gelus, in_=psA[:, (r + 1) * 512:(r + 2) * 512],
                func=AF.Gelu, bias=b1T[:, 2 * blk + 1:2 * blk + 2]),
                reads=[bA[r + 1], bb1T], writes=[bgls])
            P.do("vector", lambda ns, blk=blk: ns.tensor_tensor(
                out=hpT[:, blk, :], in0=hpT[:, blk, :], in1=gelus,
                op=ALU.mult), reads=[bhp, bgls], writes=[bhp])
        for blk in range(NBLK):
            wb2, bwb2 = w2st[blk % 2], bw2[blk % 2]
            P.dma("gpsimd", f"l1w2{blk % 2}", lambda ns, blk=blk, wb2=wb2, il=il: ns.dma_start(
                out=wb2, in_=inp["L1w2p"][il][blk * 128:(blk + 1) * 128, :],
                max_dma_last_dim=4096), writes=[bwb2])
            def yk2(ns, blk=blk, wb2=wb2):
                inst = None
                for tt in range(4):
                    inst = ns.matmul(psA[:, tt * 512:(tt + 1) * 512],
                                     hpT[:, blk, tt * 128:(tt + 1) * 128],
                                     wb2[:, 0:512],
                                     start=(blk == 0), stop=False)
                    inst = ns.matmul(psB[:, tt * 512:(tt + 1) * 512],
                                     hpT[:, blk, tt * 128:(tt + 1) * 128],
                                     wb2[:, 512:1024],
                                     start=(blk == 0), stop=False)
                return inst
            P.do("tensor", yk2, reads=[bwb2, bhp], writes=bA + bB)
        def bk2(ns):
            inst = None
            for tt in range(4):
                inst = ns.matmul(psA[:, tt * 512:(tt + 1) * 512],
                                 ctx.ones1[:, 0:128], b2row[:, 0:512],
                                 start=False, stop=True)
                inst = ns.matmul(psB[:, tt * 512:(tt + 1) * 512],
                                 ctx.ones1[:, 0:128], b2row[:, 512:1024],
                                 start=False, stop=True)
            return inst
        P.do("tensor", bk2, reads=[bb2r, ctx.bones1], writes=bA + bB)
        for tt in range(4):
            t = tc * 4 + tt
            for dh, ps, bpp in [(0, psA, bA), (1, psB, bB)]:
                P.do("vector", lambda ns, t=t, tt=tt, dh=dh, ps=ps: ns.tensor_tensor(
                    out=xi[:, t, dh * 512:(dh + 1) * 512],
                    in0=ps[:, tt * 512:(tt + 1) * 512],
                    in1=xi[:, t, dh * 512:(dh + 1) * 512], op=ALU.add),
                    reads=[bpp[tt], bxi], writes=[bxi])




def emit_l2_attn(ctx):
    nc, P, st = ctx.nc, ctx.P, ctx.st
    xnT, bxnT = ctx.xnT, ctx.bxnT
    x, bx = ctx.x, ctx.bx
    inp = ctx.inp
    psA, psB = ctx.psA, ctx.psB
    bA, bB = ctx.psA_b, ctx.psB_b
    psAb = psA[:].bitcast(BF16)

    xif = ctx.xi[:].rearrange("p a b -> p (a b)")
    xib = xif.bitcast(BF16)
    qh = xib[0:64, 0:4096].rearrange("p (a t) -> p a t", a=2)[:, 0, :]
    kh = xib[0:64, 4096:8192].rearrange("p (a t) -> p a t", a=2)[:, 0, :]
    sq = xib[0:64, 8192:12288].rearrange("p (a t) -> p a t", a=2)[:, 0, :]
    Ut = [xib[:, 12288:14336], xib[:, 14336:16384]]
    Osb = xib[0:65, 20480:22528]
    Vext = xib[:, 22528:23568].rearrange("p (t c) -> p t c", t=16)
    rowv = xif[0:1, 11840:15936].rearrange("p (a t) -> p a t", a=2)
    bqh, bkh, bsq, bO, bV, brow = (Buf("qh"), Buf("kh"), Buf("sq"),
                                   Buf("Osb"), Buf("Vx"), Buf("rowv"))
    bUt = [Buf("U0"), Buf("U1")]
    fork_arena(ctx, "xi", bqh, bkh, bsq, bUt[0], bUt[1], bO, bV, brow)

    wst = st.enter_context(nc.sbuf_tensor("l2wst", [64, 1024], BF16))
    bwst = Buf("l2wst")
    wT = st.enter_context(nc.sbuf_tensor("l2wT", [128, 3, 8, 64], BF16))
    bwT = Buf("l2wT")
    gcol = st.enter_context(nc.sbuf_tensor("l2g", [128, 4], F32))
    bg = Buf("l2g")
    gst = st.enter_context(nc.sbuf_tensor("l2gs", [1, 1], F32))
    bgst = Buf("l2gs")

    P.dma("sync", "l2gs", lambda ns: ns.dma_start(
        out=gst[:], in_=inp["L2_attn_g_scale"].unsqueeze(0)),
        writes=[bgst])
    P.do("tensor", lambda ns: ns.matmul(
        psA[:, 0:1], ctx.onesr_f[:], gst[:], start=True, stop=True),
        reads=[bgst, ctx.bonesr_f], writes=[bA[0]])
    P.do("vector", lambda ns: ns.tensor_copy(out=gcol[:, 0:1], in_=psA[:, 0:1]),
        reads=[bA[0]], writes=[bg])
    P.do("scalar", lambda ns: ns.activation(
        out=gcol[:, 1:2], in_=gcol[:, 0:1], func=AF.Abs),
        reads=[bg], writes=[bg])
    P.do("vector", lambda ns: ns.tensor_scalar(
        out=gcol[:, 1:2], in0=gcol[:, 1:2], scalar1=-1.0, scalar2=None,
        op0=ALU.mult), reads=[bg], writes=[bg])
    P.do("gpsimd", lambda ns: ns.memset(Vext[:, :, 64:65], 1.0), writes=[bV])

    for h in range(16):
        for wi, wname in enumerate(["L2_attn_Wq", "L2_attn_Wk", "L2_attn_Wv"]):
            P.dma("gpsimd", "l2wst", lambda ns, wname=wname, h=h: ns.dma_start(
                out=wst[:], in_=inp[wname][h * 64:(h + 1) * 64, :],
                max_dma_last_dim=4096), writes=[bwst])
            def trk(ns):
                inst = None
                for et in range(8):
                    inst = ns.transpose(
                        psAb[:, 1024 + et * 64:1024 + (et + 1) * 64],
                        wst[:, et * 128:(et + 1) * 128], ctx.eye_bf[0:64, 0:64])
                return inst
            P.do("tensor", trk, reads=[bwst, ctx.beye_bf], writes=[bA[0], bA[1]])
            P.do("vector", lambda ns, wi=wi: ns.tensor_copy(
                out=wT[:, wi, :, :], in_=psAb[:, 1024:1536].rearrange(
                    "p (et d) -> p et d", et=8)),
                reads=[bA[0], bA[1]], writes=[bwT])
        for wi, dst, bdst in [(0, qh, bqh), (1, kh, bkh)]:
            def pmk(ns, wi=wi):
                inst = None
                for et in range(8):
                    inst = mm_chunks(ns, lambda o, w: psB[0:64, o:o + w],
                                     wT[:, wi, et, :],
                                     lambda o, w: xnT[:, et, o:o + w], T_TOK,
                                     start=(et == 0), stop=(et == 7))
                return inst
            P.do("tensor", pmk, reads=[bwT, bxnT], writes=bB)
            P.do("scalar", lambda ns, dst=dst: ns.activation(
                out=dst, in_=psB[0:64, 0:T_TOK], func=AF.Identity),
                reads=bB, writes=[bdst])
            P.do("scalar", lambda ns, dst=dst: ns.activation(
                out=sq, in_=dst, func=AF.Square),
                reads=[bdst], writes=[bsq])
            def ssk(ns):
                return mm_chunks(ns, lambda o, w: psA[0:1, o:o + w],
                                 ctx.onesc_b[0:64, :],
                                 lambda o, w: sq[:, o:o + w], T_TOK,
                                 start=True, stop=True)
            P.do("tensor", ssk, reads=[bsq, ctx.bonesc_b], writes=bA)
            P.do("scalar", lambda ns, wi=wi: ns.activation(
                out=rowv[:, wi, :], in_=psA[0:1, 0:T_TOK], func=AF.Sqrt),
                reads=bA, writes=[brow])
            P.do("vector", lambda ns, wi=wi: ns.tensor_scalar(
                out=rowv[:, wi, :], in0=rowv[:, wi, :], scalar1=1.0e-12,
                scalar2=None, op0=ALU.max), reads=[brow], writes=[brow])
            P.do("vector", lambda ns, wi=wi: ns.reciprocal(
                out=rowv[:, wi, :], in_=rowv[:, wi, :]), reads=[brow], writes=[brow])
            def bck(ns, wi=wi):
                return mm_chunks(ns, lambda o, w: psA[0:64, o:o + w],
                                 ctx.onesr_f[:, 0:64],
                                 lambda o, w: rowv[:, wi, o:o + w], T_TOK,
                                 start=True, stop=True)
            P.do("tensor", bck, reads=[brow, ctx.bonesr_f], writes=bA)
            P.do("vector", lambda ns, dst=dst: ns.tensor_tensor(
                out=dst, in0=dst, in1=psA[0:64, 0:T_TOK], op=ALU.mult),
                reads=bA + [bdst], writes=[bdst])
        def vmk(ns):
            inst = None
            for et in range(8):
                inst = mm_chunks(ns, lambda o, w: psB[0:64, o:o + w],
                                 wT[:, 2, et, :],
                                 lambda o, w: xnT[:, et, o:o + w], T_TOK,
                                 start=(et == 0), stop=(et == 7))
            return inst
        P.do("tensor", vmk, reads=[bwT, bxnT], writes=bB)
        P.do("scalar", lambda ns: ns.activation(
            out=sq, in_=psB[0:64, 0:T_TOK], func=AF.Identity),
            reads=bB, writes=[bsq])
        def vtk(ns):
            inst = None
            for tt in range(16):
                inst = ns.transpose(psAb[:, tt * 64:(tt + 1) * 64],
                                    sq[:, tt * 128:(tt + 1) * 128],
                                    ctx.eye_bf[0:64, 0:64])
            return inst
        P.do("tensor", vtk, reads=[bsq, ctx.beye_bf], writes=[bA[0], bA[1]])
        P.do("vector", lambda ns: ns.tensor_copy(
            out=Vext[:, :, 0:64], in_=psAb[:, 0:1024].rearrange(
                "p (tt d) -> p tt d", tt=16)),
            reads=[bA[0], bA[1]], writes=[bV])
        for kt in range(16):
            ub, bub = Ut[kt % 2], bUt[kt % 2]
            def smk(ns, kt=kt):
                return mm_chunks(ns, lambda o, w: psA[:, o:o + w],
                                 kh[:, kt * 128:(kt + 1) * 128],
                                 lambda o, w: qh[:, o:o + w], T_TOK,
                                 start=True, stop=True)
            P.do("tensor", smk, reads=[bkh, bqh], writes=bA)
            P.do("scalar", lambda ns, ub=ub: ns.activation(
                out=ub, in_=psA[:, 0:T_TOK], func=AF.Exp,
                bias=gcol[:, 1:2], scale=gcol[:, 0:1]),
                reads=bA + [bg], writes=[bub])
            def pvk(ns, kt=kt, ub=ub):
                return mm_chunks(ns, lambda o, w: psB[0:65, o:o + w],
                                 Vext[:, kt, :],
                                 lambda o, w: ub[:, o:o + w], T_TOK,
                                 start=(kt == 0), stop=(kt == 15))
            P.do("tensor", pvk, reads=[bV, bub], writes=bB)
        P.do("scalar", lambda ns: ns.activation(
            out=Osb, in_=psB[0:65, 0:T_TOK], func=AF.Identity),
            reads=bB, writes=[bO])
        def otk(ns):
            inst = None
            for qt in range(16):
                inst = ns.transpose(psAb[:, qt * 128:qt * 128 + 65],
                                    Osb[:, qt * 128:(qt + 1) * 128],
                                    ctx.eye_bf[0:65, 0:65])
            return inst
        P.do("tensor", otk, reads=[bO, ctx.beye_bf], writes=bA)
        for qt in range(16):
            P.do("vector", lambda ns, qt=qt: ns.reciprocal(
                out=gcol[:, 2:3], in_=psAb[:, qt * 128 + 64:qt * 128 + 65]),
                reads=bA, writes=[bg])
            P.do("vector", lambda ns, qt=qt, h=h: ns.scalar_tensor_tensor(
                out=x[:, qt, h * 64:(h + 1) * 64],
                in0=psAb[:, qt * 128:qt * 128 + 64],
                scalar=gcol[:, 2:3], in1=x[:, qt, h * 64:(h + 1) * 64],
                op0=ALU.mult, op1=ALU.add),
                reads=bA + [bg, bx], writes=[bx])


def emit_l2_ffn(ctx):
    nc, P, st = ctx.nc, ctx.P, ctx.st
    xnT, bxnT = ctx.xnT, ctx.bxnT
    x, bx = ctx.x, ctx.bx
    inp = ctx.inp
    psA, psB = ctx.psA, ctx.psB
    bA, bB = ctx.psA_b, ctx.psB_b
    n_exp = 16

    xif = ctx.xi[:].rearrange("p a b -> p (a b)")
    xib = xif.bitcast(BF16)
    hT2 = xib[:, 0:8192].rearrange("p (f t) -> p f t", f=4)
    wup = xib[:, 8192:12288].rearrange("p (dt f) -> p dt f", dt=8)
    wga = xib[:, 12288:16384].rearrange("p (dt f) -> p dt f", dt=8)
    wdn = xib[:, 16384:20480].rearrange("p (f d) -> p f d", f=4)
    rwTv = xib[:, 20480:20608].rearrange("p (dt e) -> p dt e", dt=8)
    rbias = xib[0:1, 20608:20624]
    rstage = xib[0:16, 20640:21664]
    wgt = xif[:, 10944:10944 + NT * 16].rearrange("p (t e) -> p t e", t=NT)
    gsc = xif[:, 11200:11200 + 6 * 16].rearrange("p (a e) -> p a e", a=6)
    bh, brw, brb, brs, bwgt, bgsc = (Buf("h2"), Buf("rw"), Buf("rb"),
                                     Buf("rs"), Buf("wgt2"), Buf("gsc2"))
    bup, bga2, bdn = Buf("up"), Buf("ga"), Buf("dn")
    fork_arena(ctx, "xi", bh, brw, brb, brs, bwgt, bgsc, bup, bga2, bdn)

    P.dma("gpsimd", "l2rs", lambda ns: ns.dma_start(
        out=rstage, in_=inp["L2_ffn_router_w"], max_dma_last_dim=4096),
        writes=[brs])
    P.dma("gpsimd", "l2rb", lambda ns: ns.dma_start(
        out=rbias, in_=inp["L2_ffn_router_bias"].unsqueeze(0)), writes=[brb])
    psAb = psA[:].bitcast(BF16)
    def rtk(ns):
        inst = None
        for et in range(8):
            inst = ns.transpose(psAb[:, et * 16:(et + 1) * 16],
                                rstage[:, et * 128:(et + 1) * 128],
                                ctx.eye_bf[0:16, 0:16])
        return inst
    P.do("tensor", rtk, reads=[brs, ctx.beye_bf], writes=[bA[0]])
    P.do("vector", lambda ns: ns.tensor_copy(
        out=rwTv[:], in_=psAb[:, 0:128].rearrange("p (dt e) -> p dt e", dt=8)),
        reads=[bA[0]], writes=[brw])

    for t in range(NT):
        def gmk(ns, t=t):
            inst = None
            for dt in range(ND):
                inst = ns.matmul(psA[:, 0:n_exp], xnT[:, dt, t * 128:(t + 1) * 128],
                                 rwTv[:, dt, :], start=(dt == 0), stop=False)
            inst = ns.matmul(psA[:, 0:n_exp], ctx.ones1[:, 0:128], rbias,
                             start=False, stop=True)
            return inst
        P.do("tensor", gmk, reads=[bxnT, brw, brb, ctx.bones1], writes=[bA[0]])
        P.do("vector", lambda ns, t=t: ns.tensor_scalar(
            out=gsc[:, 0, :], in0=psA[:, 0:n_exp], scalar1=1.0, scalar2=None,
            op0=ALU.mult), reads=[bA[0]], writes=[bgsc])
        P.do("vector", lambda ns: ns.tensor_reduce(
            out=gsc[:, 1, 0:1], in_=gsc[:, 0, :], axis=AX.X, op=ALU.max,
            negate=True), reads=[bgsc], writes=[bgsc])
        P.do("vector", lambda ns: ns.tensor_scalar(
            out=gsc[:, 2, :], in0=gsc[:, 0, :], scalar1=gsc[:, 1, 0:1],
            scalar2=None, op0=ALU.add), reads=[bgsc], writes=[bgsc])
        P.do("vector", lambda ns: ns.tensor_scalar(
            out=gsc[:, 3, :], in0=gsc[:, 2, :], scalar1=0.0, scalar2=-1e30,
            op0=ALU.is_equal, op1=ALU.mult), reads=[bgsc], writes=[bgsc])
        P.do("vector", lambda ns: ns.tensor_tensor(
            out=gsc[:, 3, :], in0=gsc[:, 2, :], in1=gsc[:, 3, :], op=ALU.add),
            reads=[bgsc], writes=[bgsc])
        P.do("vector", lambda ns: ns.tensor_reduce(
            out=gsc[:, 1, 1:2], in_=gsc[:, 3, :], axis=AX.X, op=ALU.max),
            reads=[bgsc], writes=[bgsc])
        P.do("scalar", lambda ns: ns.activation(
            out=gsc[:, 4, :], in_=gsc[:, 2, :], func=AF.Exp),
            reads=[bgsc], writes=[bgsc])
        P.do("vector", lambda ns: ns.tensor_scalar(
            out=gsc[:, 5, :], in0=gsc[:, 2, :], scalar1=gsc[:, 1, 1:2],
            scalar2=None, op0=ALU.is_ge), reads=[bgsc], writes=[bgsc])
        P.do("vector", lambda ns: ns.tensor_tensor(
            out=gsc[:, 4, :], in0=gsc[:, 4, :], in1=gsc[:, 5, :], op=ALU.mult),
            reads=[bgsc], writes=[bgsc])
        P.do("vector", lambda ns: ns.tensor_reduce(
            out=gsc[:, 1, 2:3], in_=gsc[:, 4, :], axis=AX.X, op=ALU.add),
            reads=[bgsc], writes=[bgsc])
        P.do("vector", lambda ns: ns.reciprocal(
            out=gsc[:, 1, 3:4], in_=gsc[:, 1, 2:3]), reads=[bgsc], writes=[bgsc])
        P.do("vector", lambda ns, t=t: ns.tensor_scalar(
            out=wgt[:, t, :], in0=gsc[:, 4, :], scalar1=gsc[:, 1, 3:4],
            scalar2=None, op0=ALU.mult), reads=[bgsc], writes=[bwgt])

    for e in range(n_exp):
        for wnm, dst, bdst, ld in [("L2_ffn_up_proj", wup, bup, 2048),
                                   ("L2_ffn_gate_proj", wga, bga2, 2048),
                                   ("L2_ffn_down_proj", wdn, bdn, 4096)]:
            P.dma("gpsimd", f"l2w_{wnm}", lambda ns, wnm=wnm, dst=dst, e=e, ld=ld: ns.dma_start(
                out=dst, in_=inp[wnm][e].rearrange("(a p) b -> p a b", p=128),
                max_dma_last_dim=ld), writes=[bdst])
        for f in range(4):
            for th in range(2):
                tof = th * 1024
                for wsrc, bsrc, pdst, bpdst in [(wup, bup, psA, bA),
                                                (wga, bga2, psB, bB)]:
                    def hk(ns, wsrc=wsrc, f=f, pdst=pdst, tof=tof):
                        inst = None
                        for dt in range(8):
                            inst = mm_chunks(
                                ns, lambda o, w, tof=tof: pdst[:, tof + o:tof + o + w],
                                wsrc[:, dt, f * 128:(f + 1) * 128],
                                lambda o, w, tof=tof: xnT[:, dt, tof + o:tof + o + w],
                                1024, start=(dt == 0), stop=(dt == 7))
                        return inst
                    P.do("tensor", hk, reads=[bsrc, bxnT],
                         writes=bpdst[th * 2:th * 2 + 2])
                P.do("scalar", lambda ns, f=f, tof=tof: ns.activation(
                    out=hT2[:, f, tof:tof + 1024], in_=psB[:, tof:tof + 1024],
                    func=AF.Silu), reads=bB[th * 2:th * 2 + 2], writes=[bh])
                P.do("vector", lambda ns, f=f, tof=tof: ns.tensor_tensor(
                    out=hT2[:, f, tof:tof + 1024], in0=hT2[:, f, tof:tof + 1024],
                    in1=psA[:, tof:tof + 1024],
                    op=ALU.mult), reads=[bh] + bA[th * 2:th * 2 + 2], writes=[bh])
        for tq in range(4):
            def yk(ns, tq=tq):
                inst = None
                for f in range(4):
                    for tt in range(4):
                        hs = hT2[:, f, tq * 512 + tt * 128:tq * 512 + (tt + 1) * 128]
                        inst = ns.matmul(psA[:, tt * 512:(tt + 1) * 512], hs,
                                         wdn[:, f, 0:512],
                                         start=(f == 0), stop=(f == 3))
                        inst = ns.matmul(psB[:, tt * 512:(tt + 1) * 512], hs,
                                         wdn[:, f, 512:1024],
                                         start=(f == 0), stop=(f == 3))
                return inst
            P.do("tensor", yk, reads=[bdn, bh], writes=bA + bB)
            for tt in range(4):
                t = tq * 4 + tt
                for dh, ps, bpp in [(0, psA, bA), (1, psB, bB)]:
                    P.do("vector", lambda ns, t=t, tt=tt, dh=dh, e=e, ps=ps: ns.scalar_tensor_tensor(
                        out=x[:, t, dh * 512:(dh + 1) * 512],
                        in0=ps[:, tt * 512:(tt + 1) * 512],
                        scalar=wgt[:, t, e:e + 1],
                        in1=x[:, t, dh * 512:(dh + 1) * 512],
                        op0=ALU.mult, op1=ALU.add),
                        reads=[bpp[tt], bwgt, bx], writes=[bx])

    hsh2 = xib[:, 0:8192].rearrange("p (f t) -> p f t", f=4)
    shgT = xib[:, 8192:16384].rearrange("p (dt d) -> p dt d", dt=8)
    shuT = xib[:, 16384:24576].rearrange("p (dt d) -> p dt d", dt=8)
    hsh = xib[:, 24576:32768].rearrange("p (f t) -> p f t", f=4)
    bh2, bshg, bshu, bhsh = Buf("h2b"), Buf("shg"), Buf("shu"), Buf("hsh")
    fork_arena(ctx, "xi", bh2, bshg, bshu, bhsh)

    def load_and_transpose(wname, stage_ap, bstage, dstT, bdstT):
        P.dma("gpsimd", "l2shA", lambda ns, wname=wname, stage_ap=stage_ap: ns.dma_start(
            out=stage_ap, in_=inp[wname].rearrange("(st p) d -> p st d", p=128),
            max_dma_last_dim=4096), writes=[bstage])
        for s8 in range(8):
            def trk2(ns, s8=s8, stage_ap=stage_ap):
                inst = None
                for dtc in range(8):
                    inst = ns.transpose(
                        psAb[:, dtc * 128:(dtc + 1) * 128],
                        stage_ap[:, s8, dtc * 128:(dtc + 1) * 128],
                        ctx.eye_bf[:])
                return inst
            P.do("tensor", trk2, reads=[bstage, ctx.beye_bf], writes=[bA[0], bA[1]])
            P.do("vector", lambda ns, s8=s8, dstT=dstT: ns.tensor_copy(
                out=dstT[:, :, s8 * 128:(s8 + 1) * 128],
                in_=psAb[:, 0:1024].rearrange("p (dtc d) -> p dtc d", dtc=8)),
                reads=[bA[0], bA[1]], writes=[bdstT])

    load_and_transpose("L2_ffn_sh_gate",
                       xib[:, 24576:32768].rearrange("p (st d) -> p st d", st=8),
                       bhsh, shgT, bshg)
    load_and_transpose("L2_ffn_sh_up",
                       xib[:, 0:8192].rearrange("p (st d) -> p st d", st=8),
                       bh2, shuT, bshu)
    for f in range(8):
        hdst, bhd = (hsh[:, f, :], bhsh) if f < 4 else (hsh2[:, f - 4, :], bh2)
        for th in range(2):
            tof = th * 1024
            for wsrc, bsrc, pdst, bpdst in [(shuT, bshu, psA, bA),
                                            (shgT, bshg, psB, bB)]:
                def hk(ns, wsrc=wsrc, f=f, pdst=pdst, tof=tof):
                    inst = None
                    for dt in range(8):
                        inst = mm_chunks(
                            ns, lambda o, w, tof=tof: pdst[:, tof + o:tof + o + w],
                            wsrc[:, dt, f * 128:(f + 1) * 128],
                            lambda o, w, tof=tof: xnT[:, dt, tof + o:tof + o + w],
                            1024, start=(dt == 0), stop=(dt == 7))
                    return inst
                P.do("tensor", hk, reads=[bsrc, bxnT],
                     writes=bpdst[th * 2:th * 2 + 2])
            P.do("scalar", lambda ns, hdst=hdst, tof=tof: ns.activation(
                out=hdst[:, tof:tof + 1024], in_=psB[:, tof:tof + 1024],
                func=AF.Silu), reads=bB[th * 2:th * 2 + 2], writes=[bhd])
            P.do("vector", lambda ns, hdst=hdst, tof=tof: ns.tensor_tensor(
                out=hdst[:, tof:tof + 1024], in0=hdst[:, tof:tof + 1024],
                in1=psA[:, tof:tof + 1024], op=ALU.mult),
                reads=[bhd] + bA[th * 2:th * 2 + 2], writes=[bhd])
    load_and_transpose("L2_ffn_sh_down",
                       xib[:, 8192:16384].rearrange("p (st d) -> p st d", st=8),
                       bshg, shuT, bshu)
    shdT = shuT
    for tq in range(4):
        def yk(ns, tq=tq):
            inst = None
            for fr in range(8):
                hsrc = hsh if fr < 4 else hsh2
                f = fr % 4
                for tt in range(4):
                    hs = hsrc[:, f, tq * 512 + tt * 128:tq * 512 + (tt + 1) * 128]
                    inst = ns.matmul(psA[:, tt * 512:(tt + 1) * 512], hs,
                                     shdT[:, fr, 0:512],
                                     start=(fr == 0), stop=(fr == 7))
                    inst = ns.matmul(psB[:, tt * 512:(tt + 1) * 512], hs,
                                     shdT[:, fr, 512:1024],
                                     start=(fr == 0), stop=(fr == 7))
            return inst
        P.do("tensor", yk, reads=[bshu, bhsh, bh2], writes=bA + bB)
        for tt in range(4):
            t = tq * 4 + tt
            for dh, ps, bpp in [(0, psA, bA), (1, psB, bB)]:
                P.do("vector", lambda ns, t=t, tt=tt, dh=dh, ps=ps: ns.tensor_tensor(
                    out=x[:, t, dh * 512:(dh + 1) * 512],
                    in0=ps[:, tt * 512:(tt + 1) * 512],
                    in1=x[:, t, dh * 512:(dh + 1) * 512], op=ALU.add),
                    reads=[bpp[tt], bx], writes=[bx])


def emit_l3_attn(ctx):
    nc, P, st = ctx.nc, ctx.P, ctx.st
    xnT, bxnT = ctx.xnT, ctx.bxnT
    x, bx = ctx.x, ctx.bx
    inp = ctx.inp
    psA, psB = ctx.psA, ctx.psB
    bA, bB = ctx.psA_b, ctx.psB_b
    psBb = psB[:].bitcast(BF16)

    xib = ctx.xi[:].rearrange("p a b -> p (a b)").bitcast(BF16)
    bxi = ctx.bxi
    KmT = xib[:, 0:16384].rearrange("p (dt t) -> p dt t", dt=8)
    Vtm = xib[:, 16384:32768].rearrange("p (t d) -> p t d", t=16)
    bK = Buf("l3K")
    bV = Buf("l3V")
    fork_arena(ctx, "xi", bK, bV)

    Dacc = st.enter_context(nc.sbuf_tensor("l3D", [128, NT, D], BF16))
    bD = Buf("l3D")
    Qbuf = st.enter_context(nc.sbuf_tensor("l3Q", [128, 8, 128], BF16))
    bQ = Buf("l3Q")
    Ubuf = st.enter_context(nc.sbuf_tensor("l3U", [128, T_TOK], BF16))
    bU = Buf("l3U")
    UT = st.enter_context(nc.sbuf_tensor("l3UT", [128, 16, 128], BF16))
    bUT = Buf("l3UT")
    wstage = st.enter_context(nc.sbuf_tensor("l3ws", [128, 8, 128], BF16))
    bws = Buf("l3ws")
    lamst = st.enter_context(nc.sbuf_tensor("l3lam", [16, 4, 64], F32))
    blam = Buf("l3lam")
    cols = st.enter_context(nc.sbuf_tensor("l3col", [128, 8], F32))
    bcol = Buf("l3col")
    onesc, bonec = ctx.onesc_f, ctx.bonesc_f
    onesr, boner = ctx.onesr_f, ctx.bonesr_f

    SCL = float(D) ** -0.25

    for i, (na, nb) in enumerate([("L3_attn_lambda_q1", "L3_attn_lambda_k1"),
                                  ("L3_attn_lambda_q2", "L3_attn_lambda_k2")]):
        P.dma("sync", "l3lam", lambda ns, i=i, na=na: ns.dma_start(
            out=lamst[:, 0, :] if i == 0 else lamst[:, 1, :],
            in_=inp[na].rearrange("(a p) -> a p", p=64)), writes=[blam])
        P.dma("sync", "l3lam", lambda ns, i=i, nb=nb: ns.dma_start(
            out=lamst[:, 2, :], in_=inp[nb].rearrange("(a p) -> a p", p=64)),
            writes=[blam])
        P.do("vector", lambda ns, i=i: ns.tensor_tensor(
            out=lamst[:, 0 if i == 0 else 1, :],
            in0=lamst[:, 0 if i == 0 else 1, :], in1=lamst[:, 2, :],
            op=ALU.mult), reads=[blam], writes=[blam])
        P.do("vector", lambda ns, i=i: ns.tensor_reduce(
            out=lamst[:, 3, i:i + 1], in_=lamst[:, 0 if i == 0 else 1, :],
            axis=AX.X, op=ALU.add), reads=[blam], writes=[blam])
    P.do("tensor", lambda ns: ns.matmul(
        psA[0:1, 0:2], onesc[0:16, :], lamst[:, 3, 0:2],
        start=True, stop=True),
        reads=[blam, bonec], writes=[bA[0]])
    P.do("scalar", lambda ns: ns.activation(
        out=cols[0:1, 0:2], in_=psA[0:1, 0:2], func=AF.Exp),
        reads=[bA[0]], writes=[bcol])
    P.do("vector", lambda ns: ns.tensor_tensor(
        out=cols[0:1, 2:3], in0=cols[0:1, 1:2], in1=cols[0:1, 0:1],
        op=ALU.subtract), reads=[bcol], writes=[bcol])
    P.do("vector", lambda ns: ns.tensor_scalar(
        out=cols[0:1, 2:3], in0=cols[0:1, 2:3], scalar1=0.8, scalar2=None,
        op0=ALU.subtract), reads=[bcol], writes=[bcol])
    P.do("tensor", lambda ns: ns.matmul(
        psA[:, 4:5], onesr[:], cols[0:1, 2:3],
        start=True, stop=True), reads=[bcol, boner], writes=[bA[0]])
    P.do("vector", lambda ns: ns.tensor_copy(out=cols[:, 3:4], in_=psA[:, 4:5]),
        reads=[bA[0]], writes=[bcol])

    wv = Dacc[:].rearrange("p a b -> p (a b)")[:, 0:8192].rearrange(
        "p (dt d) -> p dt d", dt=8)
    bwv = bD
    P.dma("gpsimd", "l3wv", lambda ns: ns.dma_start(
        out=wv[:], in_=inp["L3_attn_w_v"].rearrange("(dt p) d -> p dt d", p=128),
        max_dma_last_dim=8192), writes=[bwv])
    for t in range(NT):
        bps = bA[0:2] if t % 2 == 0 else bA[2:4]
        off = 0 if t % 2 == 0 else 1024
        def vmk(ns, t=t, off=off):
            inst = None
            for dt in range(8):
                inst = mm_chunks(ns, lambda o, w: psA[:, off + o:off + o + w],
                                 xnT[:, dt, t * 128:(t + 1) * 128],
                                 lambda o, w: wv[:, dt, o:o + w], D,
                                 start=(dt == 0), stop=(dt == 7))
            return inst
        P.do("tensor", vmk, reads=[bwv, bxnT], writes=bps)
        P.do("scalar", lambda ns, t=t, off=off: ns.activation(
            out=Vtm[:, t, :], in_=psA[:, off:off + 1024], func=AF.Identity),
            reads=bps, writes=[bV])

    for m in range(2):
        for quarter in range(8):
            P.dma("gpsimd", "l3ws", lambda ns, m=m, quarter=quarter: ns.dma_start(
                out=wstage[:],
                in_=inp["L3_attn_w_k12"][:, m * 1024 + quarter * 128:
                                         m * 1024 + (quarter + 1) * 128].rearrange(
                    "(dt p) e -> p dt e", p=128), max_dma_last_dim=1024),
                writes=[bws])
            et = quarter
            ps, bps = (psA, bA) if et % 2 == 0 else (psB, bB)
            def kmk(ns, ps=ps):
                inst = None
                for dt in range(8):
                    inst = mm_chunks(ns, lambda o, w: ps[:, o:o + w],
                                     wstage[:, dt, :],
                                     lambda o, w: xnT[:, dt, o:o + w], T_TOK,
                                     start=(dt == 0), stop=(dt == 7))
                return inst
            P.do("tensor", kmk, reads=[bws, bxnT], writes=bps)
            P.do("scalar", lambda ns, et=et, ps=ps: ns.activation(
                out=KmT[:, et, :], in_=ps[:], func=AF.Identity),
                reads=bps, writes=[bK])
        dbg_now(ctx, f"l3_xk{m}", x[:], [bx], [128, NT, D])
        for qc in range(16):
            for quarter in range(8):
                P.dma("gpsimd", "l3ws", lambda ns, m=m, quarter=quarter: ns.dma_start(
                    out=wstage[:],
                    in_=inp["L3_attn_w_q12"][:, m * 1024 + quarter * 128:
                                             m * 1024 + (quarter + 1) * 128].rearrange(
                        "(dt p) e -> p dt e", p=128), max_dma_last_dim=1024),
                    writes=[bws])
                et = quarter
                off = 0 if et % 2 == 0 else 512
                bps = bA[0:1] if et % 2 == 0 else bA[1:2]
                def qmk(ns, qc=qc, off=off):
                    inst = None
                    for dt in range(8):
                        inst = ns.matmul(psA[:, off:off + 128],
                                         wstage[:, dt, :],
                                         xnT[:, dt, qc * 128:(qc + 1) * 128],
                                         start=(dt == 0), stop=(dt == 7))
                    return inst
                P.do("tensor", qmk, reads=[bws, bxnT], writes=bps)
                P.do("scalar", lambda ns, et=et, off=off: ns.activation(
                    out=Qbuf[:, et, :], in_=psA[:, off:off + 128],
                    func=AF.Identity), reads=bps, writes=[bQ])
            qt = qc
            def smk(ns):
                inst = None
                for dt in range(8):
                    inst = mm_chunks(ns, lambda o, w: psA[:, o:o + w],
                                     Qbuf[:, dt, :],
                                     lambda o, w: KmT[:, dt, o:o + w], T_TOK,
                                     start=(dt == 0), stop=(dt == 7))
                return inst
            P.do("tensor", smk, reads=[bQ, bK], writes=bA)
            P.do("vector", lambda ns: ns.tensor_reduce(
                out=cols[:, 4:5], in_=psA[:, 0:T_TOK], axis=AX.X, op=ALU.max,
                negate=True), reads=bA, writes=[bcol])
            P.do("vector", lambda ns: ns.tensor_scalar(
                out=cols[:, 5:6], in0=cols[:, 4:5], scalar1=SCL, scalar2=None,
                op0=ALU.mult), reads=[bcol], writes=[bcol])
            P.do("scalar", lambda ns: ns.activation(
                out=Ubuf[:], in_=psA[:, 0:T_TOK], func=AF.Exp,
                bias=cols[:, 5:6], scale=SCL, accum_out=cols[:, 6:7]),
                reads=bA + [bcol], writes=[bU, bcol])
            def tmk(ns):
                inst = None
                for kt in range(16):
                    inst = ns.transpose(psBb[:, kt * 128:(kt + 1) * 128],
                                        Ubuf[:, kt * 128:(kt + 1) * 128],
                                        ctx.eye_bf[:])
                return inst
            P.do("tensor", tmk, reads=[bU, ctx.beye_bf], writes=[bB[0], bB[1]])
            P.do("vector", lambda ns: ns.tensor_copy(
                out=UT[:], in_=psBb[:, 0:2048].rearrange(
                    "p (kt c) -> p kt c", kt=16)),
                reads=[bB[0], bB[1]], writes=[bUT])
            def pvk(ns):
                inst = None
                for kt in range(16):
                    for o in (0, 512):
                        inst = ns.matmul(psA[:, o:o + 512], UT[:, kt, :],
                                         Vtm[:, kt, o:o + 512],
                                         start=(kt == 0), stop=(kt == 15))
                return inst
            P.do("tensor", pvk, reads=[bUT, bV], writes=bA[0:2])
            P.do("vector", lambda ns: ns.reciprocal(
                out=cols[:, 7:8], in_=cols[:, 6:7]), reads=[bcol], writes=[bcol])
            if m == 0:
                P.do("scalar", lambda ns, qt=qt: ns.activation(
                    out=Dacc[:, qt, :], in_=psA[:, 0:1024], func=AF.Identity,
                    scale=cols[:, 7:8]),
                    reads=bA[0:2] + [bcol], writes=[bD])
            else:
                P.do("vector", lambda ns: ns.tensor_tensor(
                    out=cols[:, 7:8], in0=cols[:, 7:8], in1=cols[:, 3:4],
                    op=ALU.mult), reads=[bcol], writes=[bcol])
                P.do("vector", lambda ns, qt=qt: ns.scalar_tensor_tensor(
                    out=Dacc[:, qt, :], in0=psA[:, 0:1024],
                    scalar=cols[:, 7:8], in1=Dacc[:, qt, :],
                    op0=ALU.mult, op1=ALU.add),
                    reads=bA[0:2] + [bcol, bD], writes=[bD])

    ctx.dbg["l3_Vtm"] = (Vtm, [bV], [128, NT, D])
    ctx.dbg["l3_K"] = (KmT, [bK], [128, 8, T_TOK])
    ctx.dbg["l3_Dacc"] = (Dacc[:], [bD], [128, NT, D])
    ctx.dbg["l3_lam"] = (cols[:, 3:4], [bcol], [128, 1])

    dbg_now(ctx, "l3_xpre", x[:], [bx], [128, NT, D])
    msarr = st.enter_context(nc.sbuf_tensor("l3ms", [128, 2, 16], F32))
    bms = Buf("l3ms")
    for t in range(NT):
        P.do("scalar", lambda ns, t=t: ns.activation(
            out=psB[:, (t % 2) * 1024:(t % 2) * 1024 + 1024],
            in_=Dacc[:, t, :], func=AF.Square, accum_out=msarr[:, 0, t:t + 1]),
            reads=[bD], writes=[bB[(t % 2) * 2], bB[(t % 2) * 2 + 1], bms])
    P.do("vector", lambda ns: ns.tensor_scalar(
        out=msarr[:, 1, :], in0=msarr[:, 0, :], scalar1=1.0 / D, scalar2=1.0e-5,
        op0=ALU.mult, op1=ALU.add), reads=[bms], writes=[bms])
    P.do("scalar", lambda ns: ns.activation(
        out=msarr[:, 1, :], in_=msarr[:, 1, :], func=AF.Sqrt),
        reads=[bms], writes=[bms])
    P.do("vector", lambda ns: ns.reciprocal(
        out=msarr[:, 1, :], in_=msarr[:, 1, :]), reads=[bms], writes=[bms])
    P.do("vector", lambda ns: ns.tensor_scalar(
        out=msarr[:, 1, :], in0=msarr[:, 1, :], scalar1=0.2, scalar2=None,
        op0=ALU.mult), reads=[bms], writes=[bms])
    ctx.dbg["l3_ms"] = (msarr[:], [bms], [128, 2, 16])
    for t in range(NT):
        P.do("vector", lambda ns, t=t: ns.scalar_tensor_tensor(
            out=x[:, t, :], in0=Dacc[:, t, :], scalar=msarr[:, 1, t:t + 1],
            in1=x[:, t, :], op0=ALU.mult, op1=ALU.add),
            reads=[bD, bms, bx], writes=[bx])
    ctx.moe_arena = Dacc[:].rearrange("p a b -> p (a b)")
    ctx.moe_arena_buf = bD


def emit_layer(ctx, l):
    emit_outer_ln(ctx)
    if l == 0:
        emit_l0_attn(ctx)
        emit_moe(ctx, {"gate_w": "L0_ffn_gate_w", "w1": "L0_ffn_w1",
                       "b1": "L0_ffn_b1", "w2": "L0_ffn_w2", "b2": "L0_ffn_b2"},
                 AF.Gelu, n_exp=8, has_br=False)
    elif l == 1:
        emit_l1_attn(ctx)
        emit_l1_ffn(ctx)
    elif l == 2:
        emit_l2_attn(ctx)
        emit_l2_ffn(ctx)
    elif l == 3:
        emit_l3_attn(ctx)
        emit_moe(ctx, {"gate_w": "L3_ffn_Wr", "w1": "L3_ffn_W1", "b1": "L3_ffn_b1",
                       "w2": "L3_ffn_W2", "b2": "L3_ffn_b2", "br": "L3_ffn_br"},
                 AF.Relu, n_exp=8, has_br=True)


LAYER_INPUTS = {
    0: ["L0_attn_Wq", "L0_attn_bq", "L0_attn_Wk", "L0_attn_bk", "L0_attn_Wv",
        "L0_attn_bv", "L0_ffn_gate_w", "L0_ffn_w1", "L0_ffn_b1", "L0_ffn_w2",
        "L0_ffn_b2"],
    1: ["L1_attn_Wq", "L1_attn_bq", "L1_attn_Wk", "L1_attn_bk", "L1_attn_Wv",
        "L1_attn_bv", "L1_attn_Wo", "L1_attn_bo", "L1_ffn_wqkv", "L1_ffn_q_scale",
        "L1_ffn_k_scale", "L1_ffn_wo", "L1_ffn_ln_g", "L1_ffn_ln_b", "L1_ffn_w1",
        "L1_ffn_b1", "L1_ffn_w2", "L1_ffn_b2"],
    2: ["L2_attn_Wq", "L2_attn_Wk", "L2_attn_Wv", "L2_attn_g_scale",
        "L2_ffn_router_w", "L2_ffn_router_bias", "L2_ffn_up_proj",
        "L2_ffn_gate_proj", "L2_ffn_down_proj", "L2_ffn_sh_gate", "L2_ffn_sh_up",
        "L2_ffn_sh_down"],
    3: ["L3_attn_w_q12", "L3_attn_w_k12", "L3_attn_w_v", "L3_attn_lambda_q1",
        "L3_attn_lambda_k1", "L3_attn_lambda_q2", "L3_attn_lambda_k2",
        "L3_ffn_Wr", "L3_ffn_br", "L3_ffn_W1", "L3_ffn_b1", "L3_ffn_W2",
        "L3_ffn_b2"],
}
LAYER_CONSTS = {
    0: ["c_dftcos", "c_dftsin", "c_idftA", "c_idftB", "c_eye_f32"],
    1: ["c_eye_f32", "c_eye_bf", "c_rotA", "c_rotB", "c_rotC", "c_rotD",
        "c_tri", "c_triu", "c_shift", "c_bnd"],
    2: ["c_eye_f32", "c_eye_bf"],
    3: ["c_eye_f32", "c_eye_bf"],
}


def _build_and_run(layers, in_maps, shapes):
    consts = host_constants()
    nc = bass.Bass("TRN2", target_bir_lowering=False, dynamic_dma_scratch_size=16384)
    inp = {}
    inp["x_in"] = nc.dram_tensor("x_in", [T_TOK, D], F32, kind="ExternalInput").ap()
    need_w = sorted({n for l in layers for n in LAYER_INPUTS[l]})
    need_c = sorted({n for l in layers for n in LAYER_CONSTS[l]})
    for n in need_w:
        shp = list(shapes[n]) if len(shapes[n]) else [1]
        inp[n] = nc.dram_tensor(n, shp, F32, kind="ExternalInput").ap()
    if 1 in layers:
        inp["L1w1p"] = nc.dram_tensor("L1w1p", [2, 1024, 2 * HIDP], F32,
                                      kind="ExternalInput").ap()
        inp["L1b1p"] = nc.dram_tensor("L1b1p", [2, 2 * HIDP], F32,
                                      kind="ExternalInput").ap()
        inp["L1w2p"] = nc.dram_tensor("L1w2p", [2, HIDP, 1024], F32,
                                      kind="ExternalInput").ap()
        inp["x_spill"] = nc.dram_tensor("x_spill", [128, NT, D], F32,
                                        kind="Internal").ap()
    for n in need_c:
        inp[n] = nc.dram_tensor(n, list(consts[n].shape), F32, kind="ExternalInput").ap()
    x_out = nc.dram_tensor("x_out", [T_TOK, D], F32, kind="ExternalOutput").ap()

    with contextlib.ExitStack() as st:
        P = Prog(nc)
        ctx = Ctx()
        ctx.nc, ctx.P, ctx.st, ctx.inp = nc, P, st, inp
        ctx.x = st.enter_context(nc.sbuf_tensor("x", [128, NT, D], F32))
        ctx.xi = st.enter_context(nc.sbuf_tensor("xi", [128, NT, D], F32))
        ctx.xnT = st.enter_context(nc.sbuf_tensor("xnT", [128, ND, T_TOK], BF16))
        ctx.bx, ctx.bxi, ctx.bxnT = Buf("x"), Buf("xi"), Buf("xnT")
        ctx.groups = {"xi": [ctx.bxi]}
        ctx.ones1 = st.enter_context(nc.sbuf_tensor("ones1", [1, 512], BF16))
        ctx.bones1 = Buf("ones1")
        P.do("gpsimd", lambda ns: ns.memset(ctx.ones1[:], 1.0), writes=[ctx.bones1])
        ctx.onesc_f = st.enter_context(nc.sbuf_tensor("onescf", [128, 1], F32))
        ctx.bonesc_f = Buf("onescf")
        P.do("gpsimd", lambda ns: ns.memset(ctx.onesc_f[:], 1.0), writes=[ctx.bonesc_f])
        ctx.onesr_f = st.enter_context(nc.sbuf_tensor("onesrf", [1, 128], F32))
        ctx.bonesr_f = Buf("onesrf")
        P.do("gpsimd", lambda ns: ns.memset(ctx.onesr_f[:], 1.0), writes=[ctx.bonesr_f])
        ctx.onesc_b = st.enter_context(nc.sbuf_tensor("onescb", [128, 1], BF16))
        ctx.bonesc_b = Buf("onescb")
        P.do("gpsimd", lambda ns: ns.memset(ctx.onesc_b[:], 1.0), writes=[ctx.bonesc_b])

        if "c_eye_f32" in need_c:
            ctx.eye_f32 = st.enter_context(nc.sbuf_tensor("eyef", [128, 128], F32))
            ctx.beye_f32 = Buf("eyef")
            P.dma("sync", "eyef", lambda ns: ns.dma_start(
                out=ctx.eye_f32[:], in_=inp["c_eye_f32"]), writes=[ctx.beye_f32])
        if "c_eye_bf" in need_c:
            ctx.eye_bf = st.enter_context(nc.sbuf_tensor("eyeb", [128, 128], BF16))
            ctx.beye_bf = Buf("eyeb")
            P.dma("gpsimd", "eyeb", lambda ns: ns.dma_start(
                out=ctx.eye_bf[:], in_=inp["c_eye_bf"]), writes=[ctx.beye_bf])
        ctx.psA = st.enter_context(nc.psum_tensor("psA", [128, 2048], F32))
        ctx.psB = st.enter_context(nc.psum_tensor("psB", [128, 2048], F32))
        ctx.psA_b = [Buf(f"psA{i}") for i in range(4)]
        ctx.psB_b = [Buf(f"psB{i}") for i in range(4)]

        P.dma("sync", "xin", lambda ns: ns.dma_start(
            out=ctx.x[:], in_=inp["x_in"].rearrange("(t p) d -> p t d", p=128)),
            writes=[ctx.bx])

        dbg_outs = {}
        for li, l in enumerate(layers):
            if li > 0:
                P.set_fence()
            ctx.lid = l
            ctx.dbg = {}
            with contextlib.ExitStack() as lst:
                ctx.st = lst
                emit_layer(ctx, l)
                if DEBUG_TAPS:
                    for nm, (ap, bufs, shape) in list(ctx.dbg.items()):
                        if nm not in DEBUG_TAPS:
                            continue
                        od = nc.dram_tensor(f"dbg_{nm}", shape, F32,
                                            kind="ExternalOutput").ap()
                        dbg_outs[nm] = od
                        rec = P.dma("gpsimd", f"dbg_{nm}",
                                    lambda ns, od=od, ap=ap: ns.dma_start(
                                        out=od, in_=ap), reads=list(bufs))
                        P.finish([rec])
            ctx.st = st

        rec = P.dma("sync", "xout", lambda ns: ns.dma_start(
            out=x_out.rearrange("(t p) d -> p t d", p=128), in_=ctx.x[:]),
            reads=[ctx.bx])
        P.finish([rec])
        P.finalize(st)

    res = run_bass_kernel_spmd(nc, in_maps, core_ids=list(range(len(in_maps))))
    global _LAST_RES
    _LAST_RES = res.results
    return [r["x_out"] for r in res.results]


HIDP = 2816


def prep_l1_mlp(inputs):
    w1 = np.asarray(inputs["L1_ffn_w1"], np.float32)
    b1 = np.asarray(inputs["L1_ffn_b1"], np.float32)
    w2 = np.asarray(inputs["L1_ffn_w2"], np.float32)
    HID = 2730
    w1p = np.zeros((2, 1024, 2 * HIDP), np.float32)
    b1p = np.zeros((2, 2 * HIDP), np.float32)
    w2p = np.zeros((2, HIDP, 1024), np.float32)
    for blk in range(HIDP // 128):
        lo = blk * 128
        hi = min(lo + 128, HID)
        n = hi - lo
        if n <= 0:
            continue
        w1p[:, :, blk * 256:blk * 256 + n] = w1[:, :, lo:hi]
        w1p[:, :, blk * 256 + 128:blk * 256 + 128 + n] = w1[:, :, HID + lo:HID + hi]
        b1p[:, blk * 256:blk * 256 + n] = b1[:, lo:hi]
        b1p[:, blk * 256 + 128:blk * 256 + 128 + n] = b1[:, HID + lo:HID + hi]
        w2p[:, blk * 128:blk * 128 + n, :] = w2[:, lo:hi, :]
    return {"L1w1p": np.ascontiguousarray(w1p), "L1b1p": np.ascontiguousarray(b1p),
            "L1w2p": np.ascontiguousarray(w2p)}


def kernel(**inputs):
    consts = host_constants()
    x_full = np.ascontiguousarray(inputs["L0_attn_x"], dtype=np.float32)
    n_cores = 8
    shapes = {k: np.asarray(v).shape for k, v in inputs.items()}
    x_cur = [x_full[c] for c in range(n_cores)]

    for layers in LAUNCH_PLAN:
        need_w = sorted({n for l in layers for n in LAYER_INPUTS[l]})
        need_c = sorted({n for l in layers for n in LAYER_CONSTS[l]})
        extra = prep_l1_mlp(inputs) if 1 in layers else {}
        in_maps = []
        for c in range(n_cores):
            m = {"x_in": np.ascontiguousarray(x_cur[c], dtype=np.float32)}
            for n in need_w:
                v = np.asarray(inputs[n], dtype=np.float32)
                if v.ndim == 0:
                    v = v.reshape(1)
                m[n] = np.ascontiguousarray(v)
            m.update(extra)
            for n in need_c:
                m[n] = consts[n]
            in_maps.append(m)
        x_cur = _build_and_run(layers, in_maps, shapes)

    return np.stack(x_cur, axis=0).astype(np.float32)
